# version 01
# baseline (speedup 1.0000x reference)
import contextlib

import numpy as np
import concourse.bass as bass
import concourse.mybir as mybir
from concourse.bass_utils import run_bass_kernel_spmd

F32 = mybir.dt.float32
BF16 = mybir.dt.bfloat16
I32 = mybir.dt.int32
AF = mybir.ActivationFunctionType
ALU = mybir.AluOpType
AX = mybir.AxisListType

H = 1024
P = 256
NH = 16
DH = 64
INNER = 4096
CT = 128
L_TOKENS = 8192
RND_C = 1.5 * 2.0 ** 23
MAGIC_RSQRT = 0x5F3759DF

ENGINES = ("sync", "gpsimd", "vector", "scalar", "tensor")


class Sem:
    def __init__(self, handle):
        self.h = handle
        self.count = 0


class Prog:

    def __init__(self, esems=None):
        self.streams = {e: [] for e in ENGINES}
        self.esems = esems or {}

    def op(self, eng, fname, *args, sem=None, step=1, **kw):
        ser = self.esems.get(eng)
        if ser is not None and fname != "dma_start":
            if sem is None:
                sem = ser
            if sem is ser:
                self.wait(eng, ser, ser.count)
        tok = None
        if sem is not None:
            sem.count += step
            tok = sem.count

        def fn(E, fname=fname, args=args, kw=kw, sem=sem, step=step):
            rargs = [a() if callable(a) else a for a in args]
            rkw = {k: (v() if callable(v) else v) for k, v in kw.items()}
            ins = getattr(E, fname)(*rargs, **rkw)
            if sem is not None:
                ins.then_inc(sem.h, step)

        self.streams[eng].append(fn)
        if fname == "dma_start" and sem is not None:
            self.wait(eng, sem, tok)
        return tok

    def opw(self, eng, sem, fname, *args, **kw):
        tok = self.op(eng, fname, *args, sem=sem, **kw)
        self.wait(eng, sem, tok)
        return tok

    def wait(self, eng, sem, tok):
        if tok is None or tok <= 0:
            return
        self.streams[eng].append(lambda E, s=sem, t=tok: E.wait_ge(s.h, t))

    def raw(self, eng, fn):
        self.streams[eng].append(fn)

    def replay(self, block):
        for name in ENGINES:
            fns = self.streams[name]
            if not fns:
                continue

            def make(fl):
                def body(E):
                    for f in fl:
                        f(E)
                return body

            getattr(block, name)(make(fns))


def emit_ln_cols(p, sV, cols, i_sum, i_sq, i_out_is, i_out_negmis, n, eps,
                 sG=None, cmagic=None):
    c = cols
    p.opw("vector", sV, "tensor_scalar_mul", c[:, 24:25], c[:, i_sum:i_sum + 1],
          1.0 / n)
    p.opw("vector", sV, "tensor_scalar_mul", c[:, 25:26], c[:, i_sq:i_sq + 1], 1.0 / n)
    p.opw("vector", sV, "tensor_mul", c[:, 26:27], c[:, 24:25], c[:, 24:25])
    p.opw("vector", sV, "tensor_sub", c[:, 25:26], c[:, 25:26], c[:, 26:27])
    p.opw("vector", sV, "tensor_scalar_add", c[:, 25:26], c[:, 25:26], eps)
    tv = p.opw("vector", sV, "tensor_single_scalar", c[:, 26:27].bitcast(I32),
               c[:, 25:26].bitcast(I32), 1, op=ALU.logical_shift_right)
    p.wait("gpsimd", sV, tv)
    tg = p.op("gpsimd", "tensor_tensor", c[:, 26:27].bitcast(I32),
              cmagic[:], c[:, 26:27].bitcast(I32), op=ALU.subtract, sem=sG)
    p.wait("vector", sG, tg)
    p.opw("vector", sV, "tensor_scalar_mul", c[:, 27:28], c[:, 25:26], -0.5)
    for _ in range(3):
        p.opw("vector", sV, "tensor_mul", c[:, 28:29], c[:, 26:27], c[:, 26:27])
        p.opw("vector", sV, "tensor_mul", c[:, 28:29], c[:, 28:29], c[:, 27:28])
        p.opw("vector", sV, "tensor_scalar_add", c[:, 28:29], c[:, 28:29], 1.5)
        p.opw("vector", sV, "tensor_mul", c[:, 26:27], c[:, 26:27], c[:, 28:29])
    p.opw("vector", sV, "tensor_copy", c[:, i_out_is:i_out_is + 1], c[:, 26:27])
    p.opw("vector", sV, "tensor_mul", c[:, 29:30], c[:, 24:25], c[:, 26:27])
    return p.opw("vector", sV, "tensor_scalar_mul",
                 c[:, i_out_negmis:i_out_negmis + 1], c[:, 29:30], -1.0)


def build_program(L=None, z=0, run_phases=("A", "B", "C", "D"), dbg=False):
    if L is None:
        L = L_TOKENS
    NCH = L // CT
    KT = H // 128
    NOWN = (L // 2) // CT
    OWN0 = NOWN * z

    nc = bass.Bass("TRN2", target_bir_lowering=False, debug=False)

    inp = {}

    def I(name, shape):
        inp[name] = nc.dram_tensor(name, list(shape), F32, kind="ExternalInput").ap()
        return inp[name]

    x0 = I("x0", (L, H))
    ssm_par = I("ssm_par", (128, 6))
    Bp_in = I("Bp_in", (H, 2 * P))
    Cp_in = I("Cp_in", (2 * P, H))
    Db = I("Db", (128, H))
    rhs_in = I("rhs_in", (H, H))
    rhsT_in = I("rhsT_in", (H, H))
    Wq_in = I("Wq_in", (H, H))
    Wk_in = I("Wk_in", (H, H))
    Wv_in = I("Wv_in", (H, H))
    Wg_in = I("Wg_in", (H, H))
    Wa_in = I("Wa_in", (H, NH))
    bab = I("bab", (128, NH))
    Wo_in = I("Wo_in", (H, H))
    gn_row = I("gn_row", (1, 2 * H))
    zmcols = I("zmcols", (128, 2))
    wgT_in = I("wgT_in", (H, INNER))
    wuT_in = I("wuT_in", (H, INNER))
    wdT_in = I("wdT_in", (INNER, H))

    out = nc.dram_tensor("out", [L // 2, H], F32, kind="ExternalOutput").ap()

    dbg_t = {}
    if dbg:
        for nm, shape, dt in (
            ("d_xn0", (L, H), BF16), ("d_g0", (L, H), F32), ("d_s0", (L, H), F32),
            ("d_x1", (L, H), F32), ("d_za", (128, NCH * NH), F32),
            ("d_cum", (128, NCH * NH), F32), ("d_ln1", (128, NCH * 4), F32),
            ("d_setup", (128, 4864), F32), ("d_ffn0", (128, 4 * H), F32),
            ("d_gst", (128, 2 * NH), F32), ("d_o", (L, H), BF16),
        ):
            dbg_t[nm] = nc.dram_tensor(nm, list(shape), dt, kind="ExternalOutput").ap()

    xn1t_hbm = nc.dram_tensor("xn1t_hbm", [NCH, KT * 128, 128], BF16, kind="Internal").ap()
    x1_hbm = nc.dram_tensor("x1_hbm", [NCH, 128, H], F32, kind="Internal").ap()
    o_hbm = nc.dram_tensor("o_hbm", [NCH, 128, H], BF16, kind="Internal").ap()
    wgq_hbm = nc.dram_tensor("wgq_hbm", [KT, 128, INNER], BF16, kind="Internal").ap()
    wuq_hbm = nc.dram_tensor("wuq_hbm", [KT, 128, INNER], BF16, kind="Internal").ap()
    wdq_hbm = nc.dram_tensor("wdq_hbm", [INNER // 128, 128, H], BF16, kind="Internal").ap()
    rowscr = nc.dram_tensor("rowscr", [NCH, NH], F32, kind="Internal").ap()
    cumt_hbm = nc.dram_tensor("cumt_hbm", [NCH, NH, 128], F32, kind="Internal").ap()
    gu_hbm = nc.dram_tensor("gu_hbm", [L // 256, 128, INNER], BF16, kind="Internal").ap()
    xqt_hbm = nc.dram_tensor("xqt_hbm", [L // 256, KT * 128, 128], BF16, kind="Internal").ap()
    xn1tp_hbm = nc.dram_tensor("xn1tp_hbm", [L // 256, KT * 128, 128], BF16, kind="Internal").ap()
    x1p_hbm = nc.dram_tensor("x1p_hbm", [L // 256, 128, H], F32, kind="Internal").ap()

    consts = nc.dram_tensor("consts_in", [128, 768], F32,
                            kind="ExternalInput").ap()

    es = contextlib.ExitStack()
    es_s0 = contextlib.ExitStack()
    es_ab = contextlib.ExitStack()

    def _sb(stack, name, shape, dt=F32):
        return stack.enter_context(nc.sbuf_tensor(name, list(shape), dt))

    def sb(name, shape, dt=F32):
        return _sb(es, name, shape, dt)

    def sb0(name, shape, dt=F32):
        return es_s0.enter_context(nc.sbuf_tensor(name, list(shape), dt, side="right"))

    def sbab(name, shape, dt=F32):
        return _sb(es_ab, name, shape, dt)

    def psb(name, shape, dt=F32):
        return es.enter_context(nc.psum_tensor(name, list(shape), dt))

    def mksem(name):
        return Sem(es.enter_context(nc.semaphore(name)))

    tri = sb("tri", (128, 128))
    ident = sb("identt", (128, 128))
    identb = sb("identb", (128, 128), BF16)
    pm1 = sb("pm1", (128, 1))
    triuS = sb("triuS", (128, 128))
    trilx8 = sb("trilx8", (128, 8 * 128))
    bab_t = sb("bab_t", (128, NH))
    ln1sv = sb("ln1sv", (128, NCH * 4))
    cumall = sb("cumall", (128, NCH * NH))
    gstat = sb("gstat", (128, 2 * NH))
    cC = sb("cC", (128, 1))
    cmagic = sb("cmagic", (128, 1), I32)
    c05 = sb("c05", (128, 1))
    onescol = sb("onescol", (1, 128))
    ones128 = sb("ones128", (128, 1))
    cols = sb("colsA", (128, 32))
    zmc = sb("zmc", (128, 2))
    scolD = sb("scolD", (128, 6))

    q0e = sbab("q0e", (128, 512))
    q1e = sbab("q1e", (128, 512))
    r21e = sbab("r21e", (128, 512))
    r22e = sbab("r22e", (128, 512))
    trow = [sbab(f"trow{i}", (1, 512)) for i in range(4)]
    rscr = [sbab(f"rscr{i}", (1, 512)) for i in range(2)]
    grow = [sbab(f"grow{i}", (1, 512)) for i in range(2)]
    dexp = sbab("dexp", (128, H), BF16)
    rsexp = sbab("rsexp", (128, H))
    bpw = sbab("bpw", (128, KT * 512), BF16)
    cpw = sbab("cpw", (128, 4 * H), BF16)
    rhq = sbab("rhq", (128, KT * H), BF16)
    wab = sbab("wab", (128, KT * NH), BF16)
    zaall = sbab("zaall", (128, NCH * NH))
    sqdump = sbab("sqdump", (128, H))
    x0t = [sbab(f"x0t{i}", (128, H)) for i in range(2)]
    xn0 = [sbab(f"xn0_{i}", (128, H), BF16) for i in range(2)]
    xn0T = [sbab(f"xn0T{i}", (128, H), BF16) for i in range(2)]
    qtl = [sbab(f"qtl{i}", (128, H), BF16) for i in range(2)]
    qtlT = [sbab(f"qtlT{i}", (128, H), BF16) for i in range(2)]
    g0t = [sbab(f"g0t{i}", (128, H)) for i in range(2)]
    sc0 = [sbab(f"sc0_{i}", (128, H)) for i in range(2)]
    x1t = [sbab(f"x1t{i}", (128, H)) for i in range(2)]
    xn1 = [sbab(f"xn1_{i}", (128, H), BF16) for i in range(2)]
    xn1T = [sbab(f"xn1T{i}", (128, H), BF16) for i in range(2)]
    v0t = [sbab(f"v0t{i}", (128, 512)) for i in range(2)]
    v1t = [sbab(f"v1t{i}", (128, 512)) for i in range(2)]
    yst = [sbab(f"yst{i}", (128, 512), BF16) for i in range(2)]
    ysT = [sbab(f"ysT{i}", (128, 512), BF16) for i in range(2)]
    tsc = [sbab(f"tsc{i}", (128, H)) for i in range(2)]
    tsc2 = [sbab(f"tsc2_{i}", (128, H)) for i in range(2)]

    pbig = psb("pbig", (128, 1024))
    pu = psb("pu", (128, 512))
    ph0 = psb("ph0", (128, 512))
    ph1 = psb("ph1", (128, 512))
    ptiny = psb("ptiny", (128, 512))
    ptr = es_ab.enter_context(nc.psum_tensor("ptr", [128, 1024], BF16))
    ptr2 = es_ab.enter_context(nc.psum_tensor("ptr2", [128, 1024], BF16))

    sLD = mksem("sLD")
    sWR = mksem("sWR")
    sT = mksem("sT")
    sV = mksem("sV")
    sS = mksem("sS")
    sG = mksem("sG")

    WSTG_C = 2048
    wstg = sb0("wstg", (128, WSTG_C))
    wbf_stage = sb0("wbf_stage", (128, WSTG_C), BF16)

    with nc.Block() as blk:
        p = Prog(esems={"vector": sV, "scalar": sS, "gpsimd": sG})

        def ld(dst, src):
            return p.op("sync", "dma_start", sem=sLD, step=16, out=dst, in_=src)

        t_consts = ld(tri[:], consts[:, 0:128])
        t_ident = ld(ident[:], consts[:, 256:384])
        p.op("sync", "dma_start", sem=sLD, step=16, out=pm1[:],
             in_=consts[:, 384:385], allow_slow_non_contiguous=True)
        parms = sb0("parms", (128, 6))
        t_par = ld(parms[:], ssm_par)
        t_db = ld(sqdump[:, 0:H], Db)
        ld(bab_t[:], bab)
        ld(triuS[:], consts[:, 512:640])
        for i8 in range(8):
            ld(trilx8[:, i8 * 128:(i8 + 1) * 128], consts[:, 0:128])

        p.wait("vector", sLD, t_consts)
        p.op("vector", "memset", cC[:], RND_C)
        p.op("vector", "memset", c05[:], 0.5)
        p.op("vector", "memset", onescol[:], 1.0)
        p.op("vector", "memset", ones128[:], 1.0)
        p.op("gpsimd", "memset", cmagic[:], MAGIC_RSQRT)
        p.op("vector", "memset", grow[0][:], 0.0)
        p.op("vector", "memset", grow[1][:], 0.0)
        p.op("vector", "memset", gstat[:], 0.0)
        p.op("vector", "memset", ln1sv[:], 0.0)
        p.wait("vector", sLD, t_ident)
        p.op("vector", "tensor_copy", identb[:], ident[:])
        p.wait("vector", sLD, t_db)
        p.op("vector", "tensor_copy", dexp[:], sqdump[:, 0:H])

        d = {nm: sb0(f"dv_{nm}", (128, 2)) for nm in
             ("steps", "S", "alpha", "c1", "c2", "m00", "m01", "m10", "m11",
              "i00", "i01", "i10", "i11", "t1", "rS")}
        stepsraw = parms[:, 0:2]
        graw = parms[:, 2:4]
        araw = parms[:, 4:6]
        p.wait("scalar", sLD, t_par)
        tok = p.op("scalar", "activation", d["t1"][:], stepsraw, AF.Tanh, scale=0.5,
                   sem=sS)
        p.wait("vector", sS, tok)
        tok = p.opw("vector", sV, "tensor_scalar", d["steps"][:], d["t1"][:], 0.5, 0.5,
                   op0=ALU.mult, op1=ALU.add)
        p.wait("vector", sV, tok)
        p.opw("vector", sV, "tensor_scalar_max", d["t1"][:], graw, 0.0)
        p.opw("vector", sV, "tensor_mul", d["S"][:], d["t1"][:], d["steps"][:])
        p.opw("vector", sV, "tensor_scalar_add", d["S"][:], d["S"][:], 1.0)
        p.opw("vector", sV, "reciprocal", d["rS"][:], d["S"][:])
        p.opw("vector", sV, "tensor_scalar_max", d["t1"][:], araw, 0.0)
        p.opw("vector", sV, "tensor_mul", d["t1"][:], d["t1"][:], d["steps"][:])
        p.opw("vector", sV, "tensor_mul", d["t1"][:], d["t1"][:], d["steps"][:])
        tok = p.opw("vector", sV, "tensor_mul", d["t1"][:], d["t1"][:], d["rS"][:])
        p.wait("scalar", sV, tok)
        tok = p.op("scalar", "activation", d["alpha"][:], d["t1"][:], AF.Tanh,
                   scale=1.0 / 1.99, sem=sS)
        p.wait("vector", sS, tok)
        p.opw("vector", sV, "tensor_scalar_mul", d["alpha"][:], d["alpha"][:], 1.99)
        p.opw("vector", sV, "tensor_copy", d["m00"][:], d["rS"][:])
        p.opw("vector", sV, "reciprocal", d["t1"][:], d["steps"][:])
        p.opw("vector", sV, "tensor_mul", d["m01"][:], d["alpha"][:], d["t1"][:])
        p.opw("vector", sV, "tensor_scalar_mul", d["m01"][:], d["m01"][:], -1.0)
        p.opw("vector", sV, "tensor_mul", d["m10"][:], d["steps"][:], d["rS"][:])
        p.opw("vector", sV, "tensor_scalar", d["m11"][:], d["alpha"][:], -1.0, 1.0,
             op0=ALU.mult, op1=ALU.add)
        p.opw("vector", sV, "tensor_copy", d["c1"][:], d["m10"][:])
        p.opw("vector", sV, "tensor_mul", d["c2"][:], d["m10"][:], d["steps"][:])
        p.opw("vector", sV, "tensor_mul", d["i00"][:], d["m11"][:], d["S"][:])
        p.opw("vector", sV, "tensor_mul", d["i01"][:], d["m01"][:], d["S"][:])
        p.opw("vector", sV, "tensor_scalar_mul", d["i01"][:], d["i01"][:], -1.0)
        p.opw("vector", sV, "tensor_mul", d["i10"][:], d["m10"][:], d["S"][:])
        p.opw("vector", sV, "tensor_scalar_mul", d["i10"][:], d["i10"][:], -1.0)
        p.opw("vector", sV, "tensor_mul", d["i11"][:], d["m00"][:], d["S"][:])

        pos = {e: sb0(f"pow{e}", (128, 258)) for e in ("00", "01", "10", "11")}
        neg = {e: sb0(f"neg{e}", (128, 258)) for e in ("00", "01", "10", "11")}
        powtmp = sb0("powtmp", (128, 129))
        for tab in (pos, neg):
            for e in tab:
                p.opw("vector", sV, "memset", tab[e][:], 0.0)
        for tl in range(2):
            b = tl * 129
            for e in ("00", "11"):
                p.opw("vector", sV, "memset", pos[e][:, b:b + 1], 1.0)
                p.opw("vector", sV, "memset", neg[e][:, b:b + 1], 1.0)
            for e, srct in (("00", "m00"), ("01", "m01"), ("10", "m10"), ("11", "m11")):
                p.opw("vector", sV, "tensor_copy", pos[e][:, b + 1:b + 2],
                     d[srct][:, tl:tl + 1])
            for e, srct in (("00", "i00"), ("01", "i01"), ("10", "i10"), ("11", "i11")):
                p.opw("vector", sV, "tensor_copy", neg[e][:, b + 1:b + 2],
                     d[srct][:, tl:tl + 1])

        def extend(tab, n, tl):
            cnt = min(n, 128 - n)
            if cnt <= 0:
                return
            b = tl * 129
            for eo in ("00", "01", "10", "11"):
                r, c = eo[0], eo[1]
                sA = tab[r + "0"][:, b + n:b + n + 1]
                sB = tab[r + "1"][:, b + n:b + n + 1]
                p.opw("vector", sV, "tensor_single_scalar", powtmp[:, 0:cnt],
                     tab["0" + c][:, b + 1:b + 1 + cnt], sA, op=ALU.mult)
                p.opw("vector", sV, "scalar_tensor_tensor",
                     tab[eo][:, b + n + 1:b + n + 1 + cnt],
                     tab["1" + c][:, b + 1:b + 1 + cnt], sB, powtmp[:, 0:cnt],
                     op0=ALU.mult, op1=ALU.add)

        for tl in range(2):
            n = 1
            while n < 128:
                extend(pos, n, tl)
                extend(neg, n, tl)
                n *= 2

        qpk = {0: sb0("qpk0", (128, 258)), 1: sb0("qpk1", (128, 258))}
        for tl in range(2):
            b = tl * 129
            for comp, (ea, eb) in ((0, ("00", "01")), (1, ("10", "11"))):
                p.opw("vector", sV, "tensor_single_scalar", powtmp[:, 0:129],
                     neg[ea][:, b:b + 129], d["c1"][:, tl:tl + 1], op=ALU.mult)
                p.opw("vector", sV, "scalar_tensor_tensor", qpk[comp][:, b:b + 129],
                     neg[eb][:, b:b + 129], d["c2"][:, tl:tl + 1], powtmp[:, 0:129],
                     op0=ALU.mult, op1=ALU.add)
        tokVq = p.opw("vector", sV, "memset", powtmp[0:1, 0:1], 0.0)

        p.wait("tensor", sV, tokVq)
        p.wait("tensor", sLD, t_consts)
        evtok = {0: 0, 1: 0}

        def table_exp(srctab, base, dest, ptile, drow=None):
            nonlocal bk
            bk = 0
            pb = ptiny[:, 0:128]
            p.wait("tensor", sV, evtok[bk])
            t = p.op("tensor", "transpose", pb[0:128, 0:128] if drow is None
                     else pb[0:1, 0:128],
                     srctab[:, base:base + 128] if drow is None
                     else srctab[:, base:base + 1],
                     ident[:], sem=sT)
            p.wait("vector", sT, t)
            if drow is None:
                p.opw("vector", sV, "tensor_copy",
                     dest[:, ptile * 256 + 0:ptile * 256 + 256:2], pb[:, 0:128])
                evtok[bk] = p.opw("vector", sV, "tensor_copy",
                                 dest[:, ptile * 256 + 1:ptile * 256 + 256:2],
                                 pb[:, 0:128])
            else:
                p.opw("vector", sV, "tensor_copy",
                     dest[0:1, ptile * 256 + 0:ptile * 256 + 256:2],
                     pb[0:1, 0:128])
                evtok[bk] = p.opw("vector", sV, "tensor_copy",
                                 dest[0:1, ptile * 256 + 1:ptile * 256 + 256:2],
                                 pb[0:1, 0:128])
            bk ^= 1

        bk = 0
        for ptile in range(2):
            b = ptile * 129
            table_exp(qpk[0], b, q0e, ptile)
            table_exp(qpk[1], b, q1e, ptile)
            table_exp(pos["10"], b, r21e, ptile)
            table_exp(pos["11"], b, r22e, ptile)
        for i, e in enumerate(("00", "01", "10", "11")):
            for tl in range(2):
                table_exp(pos[e], tl * 129 + 128, trow[i], tl, drow=0)

        def cast_seq(src_dram, dst, ncols, scale_ap=None):
            t = ld(wstg[:, 0:ncols], src_dram)
            p.wait("scalar", sLD, t)
            if scale_ap is None:
                tc = p.op("scalar", "copy", dst, wstg[:, 0:ncols], sem=sS)
            else:
                tc = p.op("scalar", "activation", dst, wstg[:, 0:ncols], AF.Copy,
                          scale=scale_ap, sem=sS)
            p.wait("sync", sS, tc)
            return tc

        for kt in range(KT):
            cast_seq(Bp_in[kt * 128:(kt + 1) * 128, :], bpw[:, kt * 512:(kt + 1) * 512],
                     512)
        for pt in range(4):
            cast_seq(Cp_in[pt * 128:(pt + 1) * 128, :], cpw[:, pt * H:(pt + 1) * H],
                     H, scale_ap=pm1[:, 0:1])
        for kt in range(KT):
            cast_seq(Wa_in[kt * 128:(kt + 1) * 128, :], wab[:, kt * NH:(kt + 1) * NH],
                     NH)

        rsc_row = sb0("rsc_row", (1, H))
        lsc_row = sb0("lsc_row", (1, H))
        amx = sb0("amx", (128, 2))
        for ot in range(KT):
            t = ld(wstg[:, 0:H], rhsT_in[ot * 128:(ot + 1) * 128, :])
            p.wait("vector", sLD, t)
            p.opw("vector", sV, "reduce_max", amx[:, 0:1], wstg[:, 0:H], axis=AX.X,
                 apply_absolute_value=True)
            p.opw("vector", sV, "tensor_scalar_max", amx[:, 0:1], amx[:, 0:1], 1e-12)
            tokv = p.opw("vector", sV, "reciprocal", amx[:, 1:2], amx[:, 0:1])
            p.wait("tensor", sV, tokv)
            p.op("tensor", "transpose", ptiny[0:1, 0:128], amx[:, 0:1], ident[:])
            tokt = p.op("tensor", "transpose", ptiny[0:1, 128:256], amx[:, 1:2],
                        ident[:], sem=sT)
            p.wait("vector", sT, tokt)
            p.opw("vector", sV, "tensor_scalar_mul", lsc_row[:, ot * 128:(ot + 1) * 128],
                 ptiny[0:1, 0:128], 1.0 / 127.0)
            tokv = p.opw("vector", sV, "tensor_scalar_mul",
                        rsc_row[:, ot * 128:(ot + 1) * 128],
                        ptiny[0:1, 128:256], 127.0)
            p.wait("sync", sV, tokv)
        rscexp = wbf_stage[:].bitcast(F32)
        p.wait("tensor", sV, sV.count)
        tokv = 0
        for hh in range(2):
            p.op("tensor", "matmul", pbig[:, 0:512], onescol[:],
                 lsc_row[:, hh * 512:(hh + 1) * 512], start=True, stop=True)
            t2 = p.op("tensor", "matmul", pbig[:, 512:1024], onescol[:],
                      rsc_row[:, hh * 512:(hh + 1) * 512], start=True, stop=True,
                      sem=sT)
            p.wait("vector", sT, t2)
            p.opw("vector", sV, "tensor_copy", rsexp[:, hh * 512:(hh + 1) * 512],
                 pbig[:, 0:512])
            tokv = p.opw("vector", sV, "tensor_copy", rscexp[:, hh * 512:(hh + 1) * 512],
                        pbig[:, 512:1024])
            p.wait("tensor", sV, tokv)
        for kt in range(KT):
            t = ld(wstg[:, 0:H], rhs_in[kt * 128:(kt + 1) * 128, :])
            p.wait("vector", sLD, t)
            p.opw("vector", sV, "tensor_mul", wstg[:, 0:H], wstg[:, 0:H], rscexp)
            p.opw("vector", sV, "tensor_scalar", wstg[:, 0:H], wstg[:, 0:H], RND_C, RND_C,
                 op0=ALU.add, op1=ALU.subtract)
            p.opw("vector", sV, "tensor_scalar", wstg[:, 0:H], wstg[:, 0:H], 127.0, -127.0,
                 op0=ALU.min, op1=ALU.max)
            tokv = p.opw("vector", sV, "tensor_copy", rhq[:, kt * H:(kt + 1) * H],
                        wstg[:, 0:H])
            p.wait("sync", sV, tokv)

        def quant_ffn1(w_in, whbm, ntile, ncols, name):
            nsub = (ncols + WSTG_C - 1) // WSTG_C
            cs = min(ncols, WSTG_C)
            racc = sb0(f"racc_{name}", (128, ntile * nsub))
            for it in range(ntile):
                for hh in range(nsub):
                    t = ld(wstg[:, 0:cs],
                           w_in[it * 128:(it + 1) * 128, hh * cs:(hh + 1) * cs])
                    p.wait("vector", sLD, t)
                    tokv = p.opw("vector", sV, "reduce_sum",
                                racc[:, it * nsub + hh:it * nsub + hh + 1],
                                wstg[:, 0:cs], axis=AX.X, apply_absolute_value=True,
                                )
                    p.wait("sync", sV, tokv)
            p.wait("tensor", sV, sV.count)
            t1 = p.op("tensor", "matmul", ptiny[0:1, 0:ntile * nsub], ones128[:],
                      racc[:], start=True, stop=True, sem=sT)
            p.wait("vector", sT, t1)
            p.opw("vector", sV, "reduce_sum", amx[0:1, 0:1], ptiny[0:1, 0:ntile * nsub],
                 axis=AX.X)
            p.opw("vector", sV, "tensor_scalar_mul", amx[0:1, 0:1], amx[0:1, 0:1],
                 1.0 / (float(ntile) * 128.0 * ncols))
            p.opw("vector", sV, "tensor_scalar_max", amx[0:1, 0:1], amx[0:1, 0:1], 1e-5)
            tokv = p.opw("vector", sV, "reciprocal", amx[0:1, 1:2], amx[0:1, 0:1])
            p.wait("tensor", sV, tokv)
            t2 = p.op("tensor", "matmul", ptiny[:, 256:258], onescol[:],
                      amx[0:1, 0:2], start=True, stop=True, sem=sT)
            p.wait("vector", sT, t2)
            scol = scolD[:, 2 * _sidx[name]:2 * _sidx[name] + 2]
            tokv = p.opw("vector", sV, "tensor_copy", scol, ptiny[:, 256:258])
            for it in range(ntile):
                for hh in range(nsub):
                    t = ld(wstg[:, 0:cs],
                           w_in[it * 128:(it + 1) * 128, hh * cs:(hh + 1) * cs])
                    p.wait("vector", sLD, t)
                    p.opw("vector", sV, "tensor_single_scalar", wstg[:, 0:cs],
                         wstg[:, 0:cs], scol[:, 1:2], op=ALU.mult)
                    p.opw("vector", sV, "tensor_scalar", wstg[:, 0:cs], wstg[:, 0:cs],
                         RND_C, RND_C, op0=ALU.add, op1=ALU.subtract)
                    p.opw("vector", sV, "tensor_scalar", wstg[:, 0:cs], wstg[:, 0:cs],
                         1.0, -1.0, op0=ALU.min, op1=ALU.max)
                    tokv = p.opw("vector", sV, "tensor_copy", wbf_stage[:, 0:cs],
                                wstg[:, 0:cs])
                    p.wait("gpsimd", sV, tokv)
                    tokw = p.op("gpsimd", "dma_start", sem=sWR, step=16,
                                out=whbm[it][:, hh * cs:(hh + 1) * cs],
                                in_=wbf_stage[:, 0:cs])
                    p.wait("vector", sWR, tokw)
                    p.wait("sync", sV, tokv)
            return scol

        _sidx = {"wg": 0, "wu": 1, "wd": 2}
        scol_g = quant_ffn1(wgT_in, wgq_hbm, KT, INNER, "wg")
        scol_u = quant_ffn1(wuT_in, wuq_hbm, KT, INNER, "wu")
        scol_d = quant_ffn1(wdT_in, wdq_hbm, INNER // 128, H, "wd")

        p.replay(blk)
    es_s0.close()

    if dbg:
        with nc.Block() as blk:
            p = Prog(esems={"vector": sV, "scalar": sS, "gpsimd": sG})
            dsu = dbg_t["d_setup"]
            cvt = sb0("dsu_cvt", (128, 1024))
            specs = [(q0e[:], 0, 512, None), (r21e[:], 512, 512, None),
                     (r22e[:], 1024, 512, None), (rsexp[:], 1536, 1024, None),
                     (rhq[:, 0:1024], 2560, 1024, BF16),
                     (bpw[:, 0:512], 3584, 512, BF16),
                     (cpw[:, 0:512], 4096, 512, BF16),
                     (triuS[:, 0:128], 4608, 128, None),
                     (trilx8[:, 0:128], 4736, 128, None)]
            for ap, off, w, cast in specs:
                if cast is not None:
                    tv = p.op("vector", "tensor_copy", cvt[:, 0:w], ap, sem=sV)
                    p.wait("gpsimd", sV, tv)
                    tw = p.op("gpsimd", "dma_start", sem=sWR, step=16,
                              out=dsu[:, off:off + w], in_=cvt[:, 0:w])
                    p.wait("vector", sWR, tw)
                else:
                    p.op("gpsimd", "dma_start", sem=sWR, step=16,
                         out=dsu[:, off:off + w], in_=ap)
            p.wait("gpsimd", sWR, sWR.count)
            p.replay(blk)

    if "A" in run_phases:
        with nc.Block() as blk:
            p = Prog(esems={"vector": sV, "scalar": sS, "gpsimd": sG})
            tokEvac = {0: 0, 1: 0, "next": 0}
            tokPuFree = 0
            tChain = 0
            tokS0_prev = 0
            tokV0V1_prev = 0
            tokZaEv_prev = 0
            last_x0_use = [0, 0]
            last_v_guard = [0, 0]
            last_xn0_write_guard = [0, 0]
            last_qtl_guard = [0, 0]
            last_xn1_guard = [0, 0]
            last_xn1_dma = [0, 0]
            last_x1_dma = [0, 0]

            def transpose_group(src_bf16, dst_bf16, guard_sem, guard_tok):
                bank = tokEvac["next"]
                tokEvac["next"] ^= 1
                pb = (ptr if bank == 0 else ptr2)[:]
                p.wait("tensor", sV, tokEvac[bank])
                p.wait("tensor", guard_sem, guard_tok)
                last_t = None
                for i in range(8):
                    last_t = p.op("tensor", "transpose",
                                  pb[:, i * 128:(i + 1) * 128],
                                  src_bf16[:, i * 128:(i + 1) * 128], identb[:],
                                  sem=sT if i == 7 else None)
                p.wait("vector", sT, last_t)
                tokEvac[bank] = p.op("vector", "tensor_copy", dst_bf16[:], pb[:],
                                     sem=sV)
                return tokEvac[bank], last_t

            for n in range(NCH):
                pr = n % 2
                own = OWN0 <= n < OWN0 + NOWN

                p.wait("sync", sV, last_x0_use[pr])
                tLD = p.op("sync", "dma_start", sem=sLD, step=16,
                           out=x0t[pr][:], in_=x0[n * 128:(n + 1) * 128, :])

                p.wait("vector", sLD, tLD)
                p.opw("vector", sV, "reduce_sum", cols[:, 0:1], x0t[pr][:], axis=AX.X)
                p.wait("scalar", sLD, tLD)
                tSq = p.op("scalar", "activation", sqdump[:], x0t[pr][:], AF.Square,
                           accum_out=cols[:, 1:2], sem=sS)
                p.wait("vector", sS, tSq)
                tokN = emit_ln_cols(p, sV, cols, 0, 1, 2, 3, H, 1e-5, sG=sG, cmagic=cmagic)
                p.wait("vector", sV, max(tokN, last_xn0_write_guard[pr]))
                tXn0 = p.op("vector", "scalar_tensor_tensor", xn0[pr][:], x0t[pr][:],
                            cols[:, 2:3], cols[:, 3:4].broadcast_to((128, H)),
                            op0=ALU.mult, op1=ALU.add, sem=sV)

                p.opw("vector", sV, "reduce_max", cols[:, 8:9], xn0[pr][:],
                      axis=AX.X, apply_absolute_value=True)
                p.opw("vector", sV, "tensor_scalar_max", cols[:, 8:9], cols[:, 8:9],
                      1e-12)
                p.opw("vector", sV, "reciprocal", cols[:, 9:10], cols[:, 8:9])
                p.opw("vector", sV, "tensor_scalar_mul", cols[:, 10:11],
                      cols[:, 9:10], 127.0)
                tQs = p.opw("vector", sV, "tensor_scalar_mul", cols[:, 11:12],
                            cols[:, 8:9], 0.5 / 127.0)
                p.wait("vector", sV, tQs)
                tQ1 = p.op("vector", "scalar_tensor_tensor", tsc2[pr][:], xn0[pr][:],
                           cols[:, 10:11], cC[:, 0:1].broadcast_to((128, H)),
                           op0=ALU.mult, op1=ALU.add, sem=sV)
                p.wait("vector", sT, last_qtl_guard[pr])
                p.op("vector", "tensor_scalar", tsc2[pr][:], tsc2[pr][:], RND_C, 127.0,
                     op0=ALU.subtract, op1=ALU.min)
                tQ2 = p.op("vector", "tensor_scalar_max", qtl[pr][:], tsc2[pr][:],
                           -127.0, sem=sV)

                tEx0, tTx0 = transpose_group(xn0[pr], xn0T[pr], sV, tXn0)
                last_xn0_write_guard[pr] = 0
                tEq, tTq = transpose_group(qtl[pr], qtlT[pr], sV, tQ2)

                p.wait("tensor", sV, max(tokS0_prev, tEq))
                for kt in range(KT):
                    p.op("tensor", "matmul", pbig[:, 0:512],
                         qtlT[pr][:, kt * 128:(kt + 1) * 128],
                         rhq[:, kt * H:kt * H + 512],
                         start=(kt == 0), stop=(kt == KT - 1))
                tFfn = None
                for kt in range(KT):
                    last = kt == KT - 1
                    tFfn = p.op("tensor", "matmul", pbig[:, 512:1024],
                                qtlT[pr][:, kt * 128:(kt + 1) * 128],
                                rhq[:, kt * H + 512:(kt + 1) * H],
                                start=(kt == 0), stop=last, sem=sT if last else None)
                p.wait("vector", sT, tFfn)
                tDeq = p.op("vector", "tensor_mul", tsc[pr][:], pbig[:], rsexp[:],
                            sem=sV)
                p.wait("scalar", sV, tDeq)
                tTh = p.op("scalar", "activation", tsc2[pr][:], tsc[pr][:], AF.Tanh,
                           scale=cols[:, 11:12], sem=sS)
                p.wait("vector", sS, tTh)
                p.op("vector", "scalar_tensor_tensor", tsc[pr][:], tsc2[pr][:], 1.0,
                     tsc[pr][:], op0=ALU.add, op1=ALU.mult)
                tG0 = p.op("vector", "tensor_single_scalar", g0t[pr][:], tsc[pr][:],
                           cols[:, 11:12], op=ALU.mult, sem=sV)

                p.wait("tensor", sV, max(tokPuFree, tEx0))
                tBu = None
                for kt in range(KT):
                    last = kt == KT - 1
                    tBu = p.op("tensor", "matmul", pu[:],
                               xn0T[pr][:, kt * 128:(kt + 1) * 128],
                               bpw[:, kt * 512:(kt + 1) * 512],
                               start=(kt == 0), stop=last, sem=sT if last else None)
                p.wait("vector", sT, tBu)
                p.wait("vector", sG, last_v_guard[pr])
                p.op("vector", "tensor_mul", v0t[pr][:], pu[:], q0e[:])
                tokV0V1_prev = p.op("vector", "tensor_mul", v1t[pr][:], pu[:], q1e[:],
                                    sem=sV)
                p.op("vector", "tensor_add", v0t[pr][0:1, :], v0t[pr][0:1, :],
                     grow[0][:])
                tVc = p.op("vector", "tensor_add", v1t[pr][0:1, :], v1t[pr][0:1, :],
                           grow[1][:], sem=sV)
                p.wait("tensor", sV, tVc)
                p.op("tensor", "matmul", ph0[:], tri[:], v0t[pr][:], start=True,
                     stop=True)
                p.op("tensor", "matmul", ph1[:], tri[:], v1t[pr][:], start=True,
                     stop=True)
                p.wait("tensor", sV, tokZaEv_prev)
                p.op("tensor", "matmul", pu[0:1, 0:512], ones128[:], v0t[pr][:],
                     start=True, stop=True)
                tCum = p.op("tensor", "matmul", ptiny[0:1, 0:512], ones128[:],
                            v1t[pr][:], start=True, stop=True, sem=sT)
                p.wait("vector", sT, tCum)
                p.op("vector", "tensor_mul", v0t[pr][:], ph0[:], r21e[:])
                tYab = p.op("vector", "tensor_mul", v1t[pr][:], ph1[:],
                            r22e[:], sem=sV)
                p.wait("gpsimd", sV, tYab)
                tYs = p.op("gpsimd", "tensor_add", yst[pr][:], v0t[pr][:],
                           v1t[pr][:], sem=sG)
                p.op("vector", "tensor_mul", rscr[0][:], pu[0:1, 0:512], trow[0][:])
                tcv = p.op("vector", "tensor_mul", rscr[1][:], ptiny[0:1, 0:512],
                           trow[1][:], sem=sV)
                p.wait("vector", sV, tcv)
                p.op("vector", "tensor_add", grow[0][:], rscr[0][:], rscr[1][:])
                p.op("vector", "tensor_mul", rscr[0][:], pu[0:1, 0:512], trow[2][:])
                tcv = p.op("vector", "tensor_mul", rscr[1][:], ptiny[0:1, 0:512],
                           trow[3][:], sem=sV)
                p.wait("vector", sV, tcv)
                tChain = p.op("vector", "tensor_add", grow[1][:], rscr[0][:],
                              rscr[1][:], sem=sV)

                p.wait("tensor", sV, tokEvac[0])
                p.wait("tensor", sG, tYs)
                tT = None
                pbb = ptr[:, 0:512]
                for j in range(4):
                    tT = p.op("tensor", "transpose", pbb[:, j * 128:(j + 1) * 128],
                              yst[pr][:, j * 128:(j + 1) * 128], identb[:],
                              sem=sT if j == 3 else None)
                p.wait("vector", sT, tT)
                tokEvac[0] = p.op("vector", "tensor_copy", ysT[pr][:], pbb[:], sem=sV)
                tEys = tokEvac[0]
                p.wait("tensor", sV, max(tEys, tDeq))
                for pt in range(4):
                    p.op("tensor", "matmul", pbig[:, 0:512],
                         ysT[pr][:, pt * 128:(pt + 1) * 128],
                         cpw[:, pt * H:pt * H + 512],
                         start=(pt == 0), stop=(pt == 3))
                tCy = None
                for pt in range(4):
                    last = pt == 3
                    tCy = p.op("tensor", "matmul", pbig[:, 512:1024],
                               ysT[pr][:, pt * 128:(pt + 1) * 128],
                               cpw[:, pt * H + 512:(pt + 1) * H],
                               start=(pt == 0), stop=last, sem=sT if last else None)
                p.wait("vector", sT, tCy)
                p.op("vector", "tensor_mul", tsc2[pr][:], xn0[pr][:], dexp[:])
                tS0 = p.op("vector", "tensor_add", sc0[pr][:], tsc2[pr][:], pbig[:],
                           sem=sV)
                tokS0_prev = tS0

                p.wait("vector", sWR, last_x1_dma[pr])
                p.op("vector", "tensor_mul", tsc[pr][:], sc0[pr][:], g0t[pr][:])
                tX1 = p.op("vector", "tensor_add", x1t[pr][:], tsc[pr][:], x0t[pr][:],
                           sem=sV)
                last_x0_use[pr] = tX1
                p.wait("gpsimd", sV, tX1)
                last_x1_dma[pr] = p.op("gpsimd", "dma_start", sem=sWR, step=16,
                                       out=x1_hbm[n], in_=x1t[pr][:])

                p.opw("vector", sV, "reduce_sum", cols[:, 16:17], x1t[pr][:],
                      axis=AX.X)
                p.wait("scalar", sV, tX1)
                tSq1 = p.op("scalar", "activation", sqdump[:], x1t[pr][:], AF.Square,
                            accum_out=cols[:, 17:18], sem=sS)
                p.wait("vector", sS, tSq1)
                tokN1 = emit_ln_cols(p, sV, cols, 16, 17, 18, 19, H, 1e-5, sG=sG, cmagic=cmagic)
                p.op("vector", "tensor_copy", ln1sv[:, n * 4 + 0:n * 4 + 1],
                     cols[:, 18:19])
                p.op("vector", "tensor_copy", ln1sv[:, n * 4 + 1:n * 4 + 2],
                     cols[:, 19:20])
                p.wait("vector", sT, last_xn1_guard[pr])
                p.wait("vector", sV, tokN1)
                tXn1 = p.op("vector", "scalar_tensor_tensor", xn1[pr][:], x1t[pr][:],
                            cols[:, 18:19], cols[:, 19:20].broadcast_to((128, H)),
                            op0=ALU.mult, op1=ALU.add, sem=sV)
                p.op("vector", "reduce_max", ln1sv[:, n * 4 + 2:n * 4 + 3],
                     xn1[pr][:], axis=AX.X, apply_absolute_value=True)

                p.wait("vector", sWR, last_xn1_dma[pr])
                tExn1, tTxn1 = transpose_group(xn1[pr], xn1T[pr], sV, tXn1)
                p.wait("gpsimd", sV, tExn1)
                last_xn1_dma[pr] = p.op(
                    "gpsimd", "dma_start", sem=sWR, step=16,
                    out=xn1t_hbm[n].rearrange("(k q) t -> q k t", q=128),
                    in_=xn1T[pr][:])

                p.wait("tensor", sV, max(tokZaEv_prev, tExn1, tChain))
                tZa = None
                for kt in range(KT):
                    last = kt == KT - 1
                    tZa = p.op("tensor", "matmul", ptiny[:, 0:NH],
                               xn1T[pr][:, kt * 128:(kt + 1) * 128],
                               wab[:, kt * NH:(kt + 1) * NH],
                               start=(kt == 0), stop=last, sem=sT if last else None)
                p.wait("vector", sT, tZa)
                tokZaEv_prev = p.op("vector", "tensor_copy",
                                    zaall[:, n * NH:(n + 1) * NH], ptiny[:, 0:NH],
                                    sem=sV)

                if dbg and n == 0:
                    p.wait("gpsimd", sV, tDeq)
                    p.op("gpsimd", "dma_start", sem=sWR, step=16,
                         out=dbg_t["d_ffn0"][:, 0:H], in_=tsc[pr][:])
                    p.wait("gpsimd", sS, tTh)
                    p.op("gpsimd", "dma_start", sem=sWR, step=16,
                         out=dbg_t["d_ffn0"][:, H:2 * H], in_=tsc2[pr][:])
                    cvt0 = tsc2[pr]
                    p.wait("gpsimd", sV, tQ2)
                    p.op("gpsimd", "dma_start", sem=sWR, step=16,
                         out=dbg_t["d_ffn0"][:, 2 * H:2 * H + 512].bitcast(BF16),
                         in_=qtl[pr][:])
                    p.wait("gpsimd", sV, tEq)
                    p.op("gpsimd", "dma_start", sem=sWR, step=16,
                         out=dbg_t["d_ffn0"][:, 2 * H + 512:3 * H].bitcast(BF16),
                         in_=qtlT[pr][:])
                if dbg:
                    p.wait("gpsimd", sV, tXn0)
                    p.op("gpsimd", "dma_start", sem=sWR, step=16,
                         out=dbg_t["d_xn0"][n * 128:(n + 1) * 128, :], in_=xn0[pr][:])
                    p.wait("gpsimd", sV, tG0)
                    p.op("gpsimd", "dma_start", sem=sWR, step=16,
                         out=dbg_t["d_g0"][n * 128:(n + 1) * 128, :], in_=g0t[pr][:])
                    p.wait("gpsimd", sV, tS0)
                    p.op("gpsimd", "dma_start", sem=sWR, step=16,
                         out=dbg_t["d_s0"][n * 128:(n + 1) * 128, :], in_=sc0[pr][:])
                    p.wait("gpsimd", sV, tX1)
                    p.op("gpsimd", "dma_start", sem=sWR, step=16,
                         out=dbg_t["d_x1"][n * 128:(n + 1) * 128, :], in_=x1t[pr][:])

                if dbg:
                    for eng in ("vector", "scalar", "tensor", "sync", "gpsimd"):
                        p.wait(eng, sWR, sWR.count)
                tokPuFree = tChain
                last_v_guard[pr] = tYs
                last_xn0_write_guard[pr] = tS0
                last_qtl_guard[pr] = tTq
                last_xn1_guard[pr] = tTxn1

            if dbg:
                p.wait("gpsimd", sV, tokZaEv_prev)
                p.op("gpsimd", "dma_start", sem=sWR, step=16,
                     out=dbg_t["d_za"], in_=zaall[:])
                p.op("gpsimd", "dma_start", sem=sWR, step=16,
                     out=dbg_t["d_ln1"], in_=ln1sv[:])
            p.wait("gpsimd", sWR, sWR.count)
            p.replay(blk)

    if "B" in run_phases:
        with nc.Block() as blk:
            p = Prog(esems={"vector": sV, "scalar": sS, "gpsimd": sG})
            la = zaall
            for n in range(NCH):
                p.op("vector", "tensor_add", la[:, n * NH:(n + 1) * NH],
                     la[:, n * NH:(n + 1) * NH], bab_t[:])
            tv = p.op("vector", "memset", cols[0:1, 31:32], 0.0, sem=sV)
            p.wait("scalar", sV, tv)
            W = NCH * NH
            ts1 = p.op("scalar", "activation", sqdump[:, 0:W], la[:, 0:W], AF.Abs,
                       sem=sS)
            p.wait("scalar", sS, ts1)
            ts1 = p.op("scalar", "activation", sqdump[:, 0:W], sqdump[:, 0:W], AF.Exp,
                       scale=-1.0, sem=sS)
            p.wait("vector", sS, ts1)
            tv = p.opw("vector", sV, "tensor_scalar_add", sqdump[:, 0:W],
                       sqdump[:, 0:W], 1.0)
            p.wait("scalar", sV, tv)
            ts1 = p.op("scalar", "activation", sqdump[:, 0:W], sqdump[:, 0:W], AF.Ln,
                       sem=sS)
            p.wait("vector", sS, ts1)
            p.opw("vector", sV, "tensor_scalar_max", la[:, 0:W], la[:, 0:W], 0.0)
            p.opw("vector", sV, "tensor_add", la[:, 0:W], la[:, 0:W], sqdump[:, 0:W])
            tv = p.opw("vector", sV, "tensor_scalar_mul", la[:, 0:W], la[:, 0:W],
                       -1.0)
            p.wait("tensor", sV, tv)
            for half in range((W + 511) // 512):
                w0, w1 = half * 512, min(W, (half + 1) * 512)
                p.op("tensor", "matmul", pbig[:, w0:w1], tri[:], la[:, w0:w1],
                     start=True, stop=True)
            tv = 0
            for half in range((W + 511) // 512):
                w0, w1 = half * 512, min(W, (half + 1) * 512)
                p.wait("tensor", sV, tv)
                tt = p.op("tensor", "matmul", ptiny[0:1, 0:w1 - w0], ones128[:],
                          la[:, w0:w1], start=True, stop=True, sem=sT)
                p.wait("vector", sT, tt)
                tv = p.op("vector", "tensor_copy", sqdump[0:1, w0:w1],
                          ptiny[0:1, 0:w1 - w0], sem=sV)
            totm = sbab("totm", (128, NH))
            offm = sbab("offm", (128, NH))
            p.op("vector", "memset", totm[:], 0.0)
            p.op("vector", "memset", offm[:], 0.0)
            p.wait("gpsimd", sV, sV.count)
            tw = p.op("gpsimd", "dma_start", sem=sWR, step=16,
                      out=rowscr.flatten(), in_=sqdump[0:1, 0:W])
            p.wait("gpsimd", sWR, tw)
            tw = p.op("gpsimd", "dma_start", sem=sWR, step=16,
                      out=totm[0:NCH, :], in_=rowscr)
            p.wait("tensor", sWR, tw)
            tt = p.op("tensor", "matmul", ph0[0:NCH, 0:NH], triuS[0:NCH, 0:NCH],
                      totm[0:NCH, :], start=True, stop=True, sem=sT)
            p.wait("vector", sT, tt)
            tv = p.op("vector", "tensor_copy", offm[0:NCH, :], ph0[0:NCH, 0:NH],
                      sem=sV)
            p.wait("gpsimd", sV, tv)
            tw = p.op("gpsimd", "dma_start", sem=sWR, step=16,
                      out=rowscr, in_=offm[0:NCH, :])
            p.wait("gpsimd", sWR, tw)
            tw = p.op("gpsimd", "dma_start", sem=sWR, step=16,
                      out=sqdump[0:1, 0:W], in_=rowscr.flatten())
            p.wait("tensor", sWR, tw)
            tv = 0
            for half in range((W + 511) // 512):
                w0, w1 = half * 512, min(W, (half + 1) * 512)
                p.wait("tensor", sV, tv)
                tt = p.op("tensor", "matmul", ph1[:, 0:w1 - w0], onescol[:],
                          sqdump[0:1, w0:w1], start=True, stop=True, sem=sT)
                p.wait("vector", sT, tt)
                tv = p.op("vector", "tensor_copy", cumall[:, w0:w1],
                          pbig[:, w0:w1], sem=sV)
                tv = p.op("vector", "tensor_add", cumall[:, w0:w1],
                          cumall[:, w0:w1], ph1[:, 0:w1 - w0], sem=sV)
            cumTst = sbab("cumTst", (NH, 128))
            evt = 0
            evw = 0
            for n in range(NCH):
                p.wait("tensor", sV, max(tv, evt))
                tt = p.op("tensor", "transpose", ph0[0:NH, 0:128],
                          cumall[:, n * NH:(n + 1) * NH], ident[:], sem=sT)
                p.wait("vector", sT, tt)
                p.wait("vector", sWR, evw)
                evt = p.op("vector", "tensor_copy", cumTst[:], ph0[0:NH, 0:128],
                           sem=sV)
                p.wait("gpsimd", sV, evt)
                evw = p.op("gpsimd", "dma_start", sem=sWR, step=16,
                           out=cumt_hbm[n], in_=cumTst[:])
            if dbg:
                p.wait("gpsimd", sV, evt)
                p.op("gpsimd", "dma_start", sem=sWR, step=16, out=dbg_t["d_cum"],
                     in_=cumall[:])
                p.op("gpsimd", "dma_start", sem=sWR, step=16,
                     out=dbg_t["d_gst"][:, 0:NH], in_=totm[:])
                p.op("gpsimd", "dma_start", sem=sWR, step=16,
                     out=dbg_t["d_gst"][:, NH:2 * NH], in_=offm[:])

            p.wait("gpsimd", sWR, sWR.count)
            p.replay(blk)

    es_ab.close()

    es_cd = contextlib.ExitStack()

    def sbc(name, shape, dt=F32):
        return es_cd.enter_context(nc.sbuf_tensor(name, list(shape), dt))

    pdsC = es.enter_context(nc.psum_tensor("pdsC", [128, 512], F32))
    ptrC = es.enter_context(nc.psum_tensor("ptrC", [128, 1024], BF16))

    if "C" in run_phases:
        wq_w = sbc("wq_w", (128, KT * H), BF16)
        wk_w = sbc("wk_w", (128, KT * H), BF16)
        wv_w = sbc("wv_w", (128, KT * H), BF16)
        wstgC = sbc("wstgC", (128, H))
        xs = [sbc(f"xsC{i}", (128, H), BF16) for i in range(2)]
        crow = [sbc(f"crow{i}", (1, NH * 128)) for i in range(2)]
        qs = sbc("qs", (128, H), BF16)
        qsT = sbc("qsT", (128, H), BF16)
        ks = sbc("ks", (128, H), BF16)
        kT = sbc("kT", (128, H), BF16)
        kw = sbc("kw", (128, H), BF16)
        vt = sbc("vt", (128, H), BF16)
        expin = sbc("expin", (128, H))
        etile = sbc("etile", (128, H))
        e2til = sbc("e2til", (128, H))
        sct = sbc("sct", (128, H), BF16)
        obuf = [sbc(f"obuf{i}", (128, H), BF16) for i in range(2)]
        tscC = sbc("tscC", (128, H))
        tscC2 = sbc("tscC2", (128, H))
        sqC = sbc("sqC", (128, H))
        oasm = sbc("oasm", (128, H))
        sst = sbc("sst", (128, 512))
        sstb = sbc("sstb", (128, 512), BF16)
        a16 = sbc("a16", (128, NH))
        w16 = sbc("w16", (128, NH))
        peb = sbc("peb", (128, NH))
        grow16 = sbc("grow16", (1, NH))
        gsum = sbc("gsum", (128, NH))
        gss = sbc("gss", (128, NH))
        colsC = sbc("colsC", (128, 16))

        with nc.Block() as blk:
            p = Prog(esems={"vector": sV, "scalar": sS, "gpsimd": sG})

            def wcast(dst, src_dram):
                for kt in range(KT):
                    t = p.op("sync", "dma_start", sem=sLD, step=16,
                             out=wstgC[:], in_=src_dram[kt * 128:(kt + 1) * 128, :])
                    p.wait("scalar", sLD, t)
                    tc = p.op("scalar", "copy", dst[:, kt * H:(kt + 1) * H],
                              wstgC[:], sem=sS)
                    p.wait("sync", sS, tc)

            wcast(wq_w, Wq_in)
            wcast(wk_w, Wk_in)
            wcast(wv_w, Wv_in)
            p.op("vector", "memset", sst[:], 0.0)
            p.op("vector", "memset", sstb[:], 0.0)
            p.op("vector", "memset", crow[0][:], 0.0)
            p.op("vector", "memset", crow[1][:], 0.0)
            p.op("vector", "memset", gsum[:], 0.0)
            for t_ in (oasm, expin, etile, e2til, tscC, tscC2, sqC, a16, w16,
                       peb, colsC, grow16, xs[0], xs[1], qs, qsT, ks, kT, kw, vt,
                       sct, obuf[0], obuf[1]):
                p.op("vector", "memset", t_[:], 0.0)
            tv0 = p.op("vector", "memset", gss[:], 0.0, sem=sV)
            tokD = {"oin": 0, "oit": 0, "qk": 0, "ds": 0, "gb": 0, "big": tv0,
                    "k2": 0, "tr": 0, "sst": 0, "obuf": [0, 0], "xs": [0, 0],
                    "crow": [tv0, tv0], "ds": 0, "tE": 0, "tE2": 0, "sq": 0}
            tSS = 0
            p.wait("sync", sV, tv0)
            p.wait("tensor", sV, tv0)

            for n in range(NCH):
                pr = n % 2
                p.wait("sync", sT, tokD["xs"][pr])
                tX = p.op("sync", "dma_start", sem=sLD, step=16, out=xs[pr][:],
                          in_=xn1t_hbm[n].rearrange("(k q) t -> q k t", q=128))
                p.wait("sync", sV, tokD["crow"][pr])
                tC = p.op("sync", "dma_start", sem=sLD, step=16,
                          out=crow[pr][:],
                          in_=cumt_hbm[n].flatten())
                p.wait("tensor", sLD, tX)
                p.wait("tensor", sV, tokD["big"])
                for kt in range(KT):
                    p.op("tensor", "matmul", pbig[:, 0:512],
                         xs[pr][:, kt * 128:(kt + 1) * 128],
                         wq_w[:, kt * H:kt * H + 512],
                         start=(kt == 0), stop=(kt == KT - 1))
                tQm = None
                for kt in range(KT):
                    tQm = p.op("tensor", "matmul", pbig[:, 512:1024],
                               xs[pr][:, kt * 128:(kt + 1) * 128],
                               wq_w[:, kt * H + 512:(kt + 1) * H],
                               start=(kt == 0), stop=(kt == KT - 1),
                               sem=sT if kt == KT - 1 else None)
                p.wait("tensor", sV, tokD["oit"])
                for kt in range(KT):
                    p.op("tensor", "matmul", ph0[:],
                         xs[pr][:, kt * 128:(kt + 1) * 128],
                         wk_w[:, kt * H:kt * H + 512],
                         start=(kt == 0), stop=(kt == KT - 1))
                tKm = None
                for kt in range(KT):
                    tKm = p.op("tensor", "matmul", ph1[:],
                               xs[pr][:, kt * 128:(kt + 1) * 128],
                               wk_w[:, kt * H + 512:(kt + 1) * H],
                               start=(kt == 0), stop=(kt == KT - 1),
                               sem=sT if kt == KT - 1 else None)
                p.wait("scalar", sT, tQm)
                p.wait("scalar", sV, tokD["oin"])
                tS1 = p.op("scalar", "activation", tscC[:], pbig[:], AF.Tanh,
                           scale=0.5, sem=sS)
                p.wait("vector", sS, tS1)
                p.wait("vector", sT, tokD["ds"])
                p.op("vector", "scalar_tensor_tensor", tscC2[:], tscC[:], 1.0,
                     pbig[:], op0=ALU.add, op1=ALU.mult)
                tQs = p.op("vector", "tensor_scalar_mul", qs[:], tscC2[:], 0.5,
                           sem=sV)
                p.wait("scalar", sT, tKm)
                p.wait("scalar", sV, tQs)
                tS2 = p.op("scalar", "activation", tscC[:, 0:512], ph0[:], AF.Tanh,
                           scale=0.5, sem=sS)
                p.wait("scalar", sS, tS2)
                tS2 = p.op("scalar", "activation", tscC[:, 512:1024], ph1[:],
                           AF.Tanh, scale=0.5, sem=sS)
                p.wait("vector", sS, tS2)
                p.op("vector", "scalar_tensor_tensor", tscC2[:, 0:512],
                     tscC[:, 0:512], 1.0, ph0[:], op0=ALU.add, op1=ALU.mult)
                p.op("vector", "scalar_tensor_tensor", tscC2[:, 512:1024],
                     tscC[:, 512:1024], 1.0, ph1[:], op0=ALU.add, op1=ALU.mult)
                tKs = p.op("vector", "tensor_scalar_mul", ks[:], tscC2[:], 0.5,
                           sem=sV)
                p.wait("tensor", sV, tQs)
                for kt in range(KT):
                    p.op("tensor", "matmul", pbig[:, 0:512],
                         xs[pr][:, kt * 128:(kt + 1) * 128],
                         wv_w[:, kt * H:kt * H + 512],
                         start=(kt == 0), stop=(kt == KT - 1))
                tVm = None
                for kt in range(KT):
                    tVm = p.op("tensor", "matmul", pbig[:, 512:1024],
                               xs[pr][:, kt * 128:(kt + 1) * 128],
                               wv_w[:, kt * H + 512:(kt + 1) * H],
                               start=(kt == 0), stop=(kt == KT - 1),
                               sem=sT if kt == KT - 1 else None)
                p.wait("vector", sT, tVm)
                tVs = p.op("vector", "tensor_copy", vt[:], pbig[:], sem=sV)

                p.wait("sync", sLD, tC)
                p.wait("tensor", sLD, tC)
                p.wait("tensor", sV, tSS)
                prv = crow[1 - pr]
                tt = p.op("tensor", "matmul", pdsC[:, 0:NH], onescol[:],
                          prv[0:1, 127::128], start=True, stop=True)
                tt = p.op("tensor", "matmul", pdsC[:, NH:2 * NH], onescol[:],
                          crow[pr][0:1, 127::128], start=True, stop=True, sem=sT)
                p.wait("vector", sT, tt)
                p.op("vector", "tensor_sub", peb[:], cumall[:, n * NH:(n + 1) * NH],
                     pdsC[:, 0:NH])
                tv = p.op("vector", "tensor_sub", w16[:], pdsC[:, NH:2 * NH],
                          cumall[:, n * NH:(n + 1) * NH], sem=sV)
                p.wait("scalar", sV, tv)
                p.op("scalar", "activation", a16[:], peb[:], AF.Exp)
                tA = p.op("scalar", "activation", w16[:], w16[:], AF.Exp, sem=sS)
                p.wait("vector", sS, tA)
                p.op("vector", "tensor_sub", grow16[:], crow[pr][0:1, 127::128],
                     prv[0:1, 127::128])
                tv = p.op("vector", "memset", colsC[0:1, 15:16], 0.0, sem=sV)
                p.wait("scalar", sV, tv)
                tGr = p.op("scalar", "activation", grow16[:], grow16[:], AF.Exp,
                           sem=sS)
                p.wait("vector", sS, tA)
                tKw = p.op("vector", "tensor_tensor", kw[:], ks[:],
                           w16[:].rearrange("p (h o) -> p h o", o=1)
                           .broadcast_to((128, NH, DH)), op=ALU.mult, sem=sV)

                p.wait("tensor", sV, max(tokD["tr"], tQs))
                tt = None
                for i in range(8):
                    tt = p.op("tensor", "transpose", ptrC[:, i * 128:(i + 1) * 128],
                              qs[:, i * 128:(i + 1) * 128], identb[:],
                              sem=sT if i == 7 else None)
                p.wait("vector", sT, tt)
                tTq = p.op("vector", "tensor_copy", qsT[:], ptrC[:], sem=sV)
                p.wait("tensor", sV, max(tTq, tKs))
                for i in range(8):
                    tt = p.op("tensor", "transpose", ptrC[:, i * 128:(i + 1) * 128],
                              ks[:, i * 128:(i + 1) * 128], identb[:],
                              sem=sT if i == 7 else None)
                p.wait("vector", sT, tt)
                tTk = p.op("vector", "tensor_copy", kT[:], ptrC[:], sem=sV)
                tokD["tr"] = tTk

                tOe = [None, None]
                for g in range(2):
                    p.wait("tensor", sV, max(tVs, tokD["qk"]))
                    tt = p.op("tensor", "matmul", pbig[:, 0:512], onescol[:],
                              crow[pr][0:1, g * 1024:g * 1024 + 512],
                              start=True, stop=True)
                    tt = p.op("tensor", "matmul", pbig[:, 512:1024], onescol[:],
                              crow[pr][0:1, g * 1024 + 512:(g + 1) * 1024],
                              start=True, stop=True, sem=sT)
                    p.wait("vector", sT, tt)
                    p.wait("vector", sS, tokD["tE"])
                    h0 = g * 8
                    p.op("vector", "tensor_sub", expin[:], pbig[:],
                         cumall[:, n * NH + h0:n * NH + h0 + 8]
                         .rearrange("p (h o) -> p h o", o=1).broadcast_to((128, 8, 128)))
                    tv = p.op("vector", "tensor_scalar_min", expin[:], expin[:],
                              0.0, sem=sV)
                    p.wait("scalar", sV, tv)
                    p.wait("scalar", sG, tokD["tE2"])
                    tE = p.op("scalar", "activation", etile[:], expin[:], AF.Exp,
                              sem=sS)
                    tokD["tE"] = tE
                    p.wait("gpsimd", sS, tE)
                    p.wait("gpsimd", sV, tokD["qk"])
                    tE2 = p.op("gpsimd", "tensor_tensor", e2til[:], etile[:],
                               trilx8[:], op=ALU.mult, sem=sG)
                    tokD["tE2"] = tE2
                    p.wait("tensor", sV, tv)
                    tt = None
                    for hh in range(8):
                        h = h0 + hh
                        base = (h % 2) * 64
                        c0 = (hh % 2) * 512 + (hh // 2) * 128
                        tt = p.op("tensor", "matmul",
                                  pbig[:, c0:c0 + 128],
                                  kT[base:base + 64,
                                     (h // 2) * 128:(h // 2) * 128 + 128],
                                  qsT[base:base + 64,
                                      (h // 2) * 128:(h // 2) * 128 + 128],
                                  start=True, stop=True,
                                  sem=sT if hh == 7 else None)
                    p.wait("vector", sT, tt)
                    p.wait("vector", sG, tE2)
                    tSc = p.op("vector", "tensor_mul",
                               sct[:].rearrange("p (h2 par i) -> p h2 par i",
                                                par=2, i=128),
                               e2til[:].rearrange("p (h2 par i) -> p h2 par i",
                                                  par=2, i=128),
                               pbig[:].rearrange("p (par h2 i) -> p h2 par i",
                                                 par=2, i=128),
                               sem=sV)
                    p.wait("tensor", sV, tSc)
                    if g == 0:
                        p.wait("tensor", sV, tokD["oin"])
                    for hh in range(8):
                        h = h0 + hh
                        phb = ph0 if h % 2 == 0 else ph1
                        c0 = (h // 2) * 64
                        p.op("tensor", "matmul", phb[:, c0:c0 + 64],
                             sct[:, hh * 128:(hh + 1) * 128],
                             vt[:, h * 64:(h + 1) * 64],
                             start=True, stop=True)
                    for hh in range(8):
                        h = h0 + hh
                        pib = pu if h % 2 == 0 else ptiny
                        c0 = (h // 2) * 64
                        base = (h % 2) * 64
                        tOe[g] = p.op("tensor", "matmul", pib[:, c0:c0 + 64],
                                      qsT[base:base + 64,
                                          (h // 2) * 128:(h // 2) * 128 + 128],
                                      sstb[base:base + 64,
                                           (h // 2) * 64:(h // 2) * 64 + 64],
                                      start=True, stop=True,
                                      sem=sT if hh == 7 else None)
                    p.wait("tensor", sV, max(tKw, tokD["sst"]))
                    for hh in range(8):
                        h = h0 + hh
                        base = (h % 2) * 64
                        tDs = p.op("tensor", "matmul",
                                   pdsC[base:base + 64,
                                        (h // 2) * 64:(h // 2) * 64 + 64],
                                   kw[:, h * 64:(h + 1) * 64],
                                   vt[:, h * 64:(h + 1) * 64],
                                   start=True, stop=True,
                                   sem=sT if hh == 7 else None)
                    tokD["qk"] = tSc
                tokD["ds"] = tDs
                p.wait("vector", sT, tOe[1])
                p.wait("vector", sS, tokD["sq"])
                p.wait("vector", sWR, tokD["obuf"][pr])
                p.op("vector", "tensor_mul", tscC[:, 0:512], pu[:],
                     a16[:, 0::2].rearrange("p (h o) -> p h o", o=1)
                     .broadcast_to((128, 8, DH)))
                tv = p.op("vector", "tensor_mul", tscC[:, 512:1024], ptiny[:],
                          a16[:, 1::2].rearrange("p (h o) -> p h o", o=1)
                          .broadcast_to((128, 8, DH)), sem=sV)
                oaV = oasm[:].rearrange("p (h2 x) -> p h2 x", x=128)
                p.op("vector", "tensor_add", oaV[:, :, 0:DH],
                     tscC[:, 0:512].rearrange("p (h2 d) -> p h2 d", d=DH),
                     ph0[:].rearrange("p (h2 d) -> p h2 d", d=DH))
                tOo = p.op("vector", "tensor_add", oaV[:, :, DH:128],
                           tscC[:, 512:1024].rearrange("p (h2 d) -> p h2 d",
                                                       d=DH),
                           ph1[:].rearrange("p (h2 d) -> p h2 d", d=DH),
                           sem=sV)
                tokD["oit"] = tOo
                tokD["oin"] = tOo
                tOb = p.op("vector", "tensor_copy", obuf[pr][:], oasm[:], sem=sV)
                p.wait("gpsimd", sV, tOb)
                tokD["obuf"][pr] = p.op("gpsimd", "dma_start", sem=sWR, step=16,
                                        out=o_hbm[n], in_=obuf[pr][:])
                if dbg:
                    tokD["obuf"][pr] = p.op("gpsimd", "dma_start", sem=sWR, step=16,
                                            out=dbg_t["d_o"][n * 128:(n + 1) * 128, :],
                                            in_=obuf[pr][:])
                if dbg and n == 0:
                    for src_t, off, w, bc in ((qs, 0, 512, True), (ks, 512, 512, True),
                                              (vt, 1024, 512, True),
                                              (sct, 1536, 512, True),
                                              (a16, 2048, 16, False),
                                              (w16, 2064, 16, False),
                                              (peb, 2080, 16, False),
                                              (etile, 2560, 1024, False)):
                        p.wait("gpsimd", sV, sV.count)
                        if bc:
                            p.op("gpsimd", "dma_start", sem=sWR, step=16,
                                 out=dbg_t["d_ffn0"][:, off:off + w].bitcast(BF16),
                                 in_=src_t[:])
                        else:
                            p.op("gpsimd", "dma_start", sem=sWR, step=16,
                                 out=dbg_t["d_ffn0"][:, off:off + w],
                                 in_=src_t[:, 0:w])
                    for e in ("vector", "scalar", "tensor", "sync"):
                        p.wait(e, sWR, sWR.count)
                p.op("vector", "tensor_reduce", colsC[:, 0:NH],
                     oasm[:].rearrange("p (h d) -> p h d", d=DH), axis=AX.X,
                     op=ALU.add)
                p.opw("vector", sV, "tensor_add", gsum[:], gsum[:], colsC[:, 0:NH])
                p.wait("scalar", sV, tOo)
                tSq = p.op("scalar", "activation", sqC[:], oasm[:], AF.Square,
                           sem=sS)
                tokD["sq"] = tSq
                p.wait("vector", sS, tSq)
                p.op("vector", "tensor_reduce", colsC[:, 0:NH],
                     sqC[:].rearrange("p (h d) -> p h d", d=DH), axis=AX.X,
                     op=ALU.add)
                tv = p.opw("vector", sV, "tensor_add", gss[:], gss[:],
                           colsC[:, 0:NH])
                p.wait("tensor", sV, max(tv, tokD["qk"]))
                p.wait("tensor", sS, tGr)
                tt = p.op("tensor", "matmul", pbig[0:64, 0:512], onescol[0:1, 0:64],
                          grow16[0:1, 0::2].rearrange("q (h o) -> q h o", o=1)
                          .broadcast_to((1, 8, 64)), start=True, stop=True)
                tt = p.op("tensor", "matmul", pbig[64:128, 0:512],
                          onescol[0:1, 0:64],
                          grow16[0:1, 1::2].rearrange("q (h o) -> q h o", o=1)
                          .broadcast_to((1, 8, 64)), start=True, stop=True, sem=sT)
                p.wait("vector", sT, max(tt, tDs))
                p.op("vector", "tensor_mul", tscC[:, 0:512], sst[:], pbig[:, 0:512])
                p.op("vector", "tensor_add", sst[:], tscC[:, 0:512], pdsC[:])
                tSS = p.op("vector", "tensor_copy", sstb[:], sst[:], sem=sV)
                tokD["sst"] = tSS
                tokD["big"] = tSS
                tokD["crow"][0] = tSS
                tokD["crow"][1] = tSS
                tokD["xs"][pr] = tVm

            p.wait("vector", sV, tSS)
            p.op("vector", "tensor_copy", gstat[:, 0:NH], gsum[:])
            tv = p.op("vector", "tensor_copy", gstat[:, NH:2 * NH], gss[:], sem=sV)
            if dbg:
                p.wait("gpsimd", sV, tv)
                p.op("gpsimd", "dma_start", sem=sWR, step=16,
                     out=dbg_t["d_gst"], in_=gstat[:])
            p.wait("gpsimd", sWR, sWR.count)
            p.wait("vector", sV, sV.count)
            p.replay(blk)

    es_cd.close()

    if "D" in run_phases:
        es_d = contextlib.ExitStack()
        es_d1 = contextlib.ExitStack()
        es_d2 = contextlib.ExitStack()

        def sbd(name, shape, dt=F32):
            return es_d.enter_context(nc.sbuf_tensor(name, list(shape), dt))

        def sbd1(name, shape, dt=F32):
            return es_d1.enter_context(nc.sbuf_tensor(name, list(shape), dt))

        SCLb = sbd("SCLb", (128, H))
        BIAb = sbd("BIAb", (128, H))
        colD = sbd("colD", (1, 160))
        dqcol = sbd("dqcol", (128, 3 * NOWN))
        xa = sbd("xaD", (128, H))
        xsel = sbd("xselD", (128, H))
        xnb = sbd("xnbD", (128, H), BF16)
        tscD = sbd("tscD", (128, H))
        tscD2 = sbd("tscD2", (128, H))
        xnT = sbd("xnTD", (128, H), BF16)
        gub = sbd("gubD", (128, H), BF16)
        gu_l = [sbd(f"gu_l{i}", (128, INNER), BF16) for i in range(2)]
        guqT = sbd("guqT", (128, INNER), BF16)
        g1t = sbd("g1tD", (128, H))
        ogb = sbd("ogbD", (128, H), BF16)
        ogT = sbd("ogTD", (128, H), BF16)
        outt = [sbd(f"outtD{i}", (128, H)) for i in range(2)]
        rowD = sbd1("rowD", (1, 2 * H))
        gnr = sbd1("gnrD", (1, 2 * H))
        qb = sbd1("qbD", (128, H), BF16)
        qbT = sbd1("qbTD", (128, H), BF16)
        wg_sl = sbd1("wg_sl", (128, KT * H), BF16)
        wu_sl = sbd1("wu_sl", (128, KT * H), BF16)
        xq_l = [sbd1(f"xq_l{i}", (128, H), BF16) for i in range(2)]

        DQG, DQU, AMG = 0, NOWN, 2 * NOWN

        with nc.Block() as blk:
            p = Prog(esems={"vector": sV, "scalar": sS, "gpsimd": sG})
            tZM = p.op("sync", "dma_start", sem=sLD, step=16, out=zmc[:],
                       in_=zmcols)
            p.op("sync", "dma_start", sem=sLD, step=16, out=gnr[:],
                 in_=gn_row)
            p.wait("vector", sLD, sLD.count)
            p.wait("tensor", sV, sV.count)
            tt = p.op("tensor", "matmul", pdsC[0:1, 0:2 * NH], ones128[:],
                      gstat[:], start=True, stop=True, sem=sT)
            p.wait("vector", sT, tt)
            CNT = 1.0 / (float(L) * DH)
            p.opw("vector", sV, "tensor_scalar_mul", colD[0:1, 0:16],
                  pdsC[0:1, 0:NH], CNT)
            p.opw("vector", sV, "tensor_scalar_mul", colD[0:1, 32:48],
                  pdsC[0:1, NH:2 * NH], CNT)
            p.opw("vector", sV, "tensor_mul", colD[0:1, 48:64], colD[0:1, 0:16],
                  colD[0:1, 0:16])
            p.opw("vector", sV, "tensor_sub", colD[0:1, 32:48], colD[0:1, 32:48],
                  colD[0:1, 48:64])
            tv = p.opw("vector", sV, "tensor_scalar_add", colD[0:1, 32:48],
                       colD[0:1, 32:48], 1e-5)
            p.opw("vector", sV, "tensor_single_scalar",
                  colD[0:1, 64:80].bitcast(I32), colD[0:1, 32:48].bitcast(I32), 1,
                  op=ALU.logical_shift_right)
            p.wait("gpsimd", sV, sV.count)
            tg = p.op("gpsimd", "tensor_tensor", colD[0:1, 80:96].bitcast(I32),
                      cmagic[0:1, 0:1].broadcast_to((1, NH)).bitcast(I32),
                      colD[0:1, 64:80].bitcast(I32), op=ALU.subtract, sem=sG)
            p.wait("vector", sG, tg)
            p.opw("vector", sV, "tensor_scalar_mul", colD[0:1, 96:112],
                  colD[0:1, 32:48], -0.5)
            for _ in range(3):
                p.opw("vector", sV, "tensor_mul", colD[0:1, 112:128],
                      colD[0:1, 80:96], colD[0:1, 80:96])
                p.opw("vector", sV, "tensor_mul", colD[0:1, 112:128],
                      colD[0:1, 112:128], colD[0:1, 96:112])
                p.opw("vector", sV, "tensor_scalar_add", colD[0:1, 112:128],
                      colD[0:1, 112:128], 1.5)
                p.opw("vector", sV, "tensor_mul", colD[0:1, 80:96],
                      colD[0:1, 80:96], colD[0:1, 112:128])
            p.opw("vector", sV, "tensor_mul",
                  rowD[0:1, 0:H].rearrange("q (h d) -> q h d", d=DH),
                  gnr[0:1, 0:H].rearrange("q (h d) -> q h d", d=DH),
                  colD[0:1, 80:96].rearrange("q (h o) -> q h o", o=1)
                  .broadcast_to((1, NH, DH)))
            p.opw("vector", sV, "tensor_mul", colD[0:1, 128:144], colD[0:1, 0:16],
                  colD[0:1, 80:96])
            p.opw("vector", sV, "tensor_scalar_mul", colD[0:1, 128:144],
                  colD[0:1, 128:144], -1.0)
            p.opw("vector", sV, "tensor_mul",
                  rowD[0:1, H:2 * H].rearrange("q (h d) -> q h d", d=DH),
                  gnr[0:1, 0:H].rearrange("q (h d) -> q h d", d=DH),
                  colD[0:1, 128:144].rearrange("q (h o) -> q h o", o=1)
                  .broadcast_to((1, NH, DH)))
            tv = p.opw("vector", sV, "tensor_add", rowD[0:1, H:2 * H],
                       rowD[0:1, H:2 * H], gnr[0:1, H:2 * H])
            p.wait("tensor", sV, tv)
            for hh in range(2):
                tt = p.op("tensor", "matmul", pbig[:, 0:512], onescol[:],
                          rowD[0:1, hh * 512:(hh + 1) * 512], start=True,
                          stop=True)
                tt = p.op("tensor", "matmul", pbig[:, 512:1024], onescol[:],
                          rowD[0:1, H + hh * 512:H + (hh + 1) * 512],
                          start=True, stop=True, sem=sT)
                p.wait("vector", sT, tt)
                p.op("vector", "tensor_copy", SCLb[:, hh * 512:(hh + 1) * 512],
                     pbig[:, 0:512])
                tv = p.op("vector", "tensor_copy",
                          BIAb[:, hh * 512:(hh + 1) * 512], pbig[:, 512:1024],
                          sem=sV)
                p.wait("tensor", sV, tv)
            p.op("vector", "memset", dqcol[:], 0.0)
            p.wait("vector", sLD, tZM)

            tokTRd = 0
            tokXselW = 0
            tokXaRd = 0
            tokQT = {0: 0, 1: 0}
            for j in range(NOWN):
                p.wait("sync", sV, tokXaRd)
                t = p.op("sync", "dma_start", sem=sLD, step=16, out=xa[:],
                         in_=x1_hbm[j])
                p.wait("vector", sLD, t)
                p.wait("vector", sWR, tokXselW)
                tok1 = p.op("vector", "tensor_single_scalar", xsel[:], xa[:],
                            zmc[:, 0:1], op=ALU.mult, sem=sV)
                p.wait("sync", sV, tok1)
                t = p.op("sync", "dma_start", sem=sLD, step=16, out=xa[:],
                         in_=x1_hbm[j + NOWN])
                p.wait("vector", sLD, t)
                tXs = p.op("vector", "scalar_tensor_tensor", xsel[:], xa[:],
                           zmc[:, 1:2], xsel[:], op0=ALU.mult, op1=ALU.add,
                           sem=sV)
                tokXaRd = tXs
                p.wait("gpsimd", sV, tXs)
                tokXselW = p.op("gpsimd", "dma_start", sem=sWR, step=16,
                                out=x1p_hbm[j], in_=xsel[:])
                p.opw("vector", sV, "reduce_sum", cols[:, 0:1], xsel[:],
                      axis=AX.X)
                p.wait("scalar", sV, tXs)
                tSq = p.op("scalar", "activation", tscD[:], xsel[:], AF.Square,
                           accum_out=cols[:, 1:2], sem=sS)
                p.wait("vector", sS, tSq)
                emit_ln_cols(p, sV, cols, 0, 1, 2, 3, H, 1e-5, sG=sG,
                             cmagic=cmagic)
                tXn = p.op("vector", "scalar_tensor_tensor", xnb[:], xsel[:],
                           cols[:, 2:3], cols[:, 3:4].broadcast_to((128, H)),
                           op0=ALU.mult, op1=ALU.add, sem=sV)
                p.opw("vector", sV, "reduce_max", cols[:, 8:9], xnb[:],
                      axis=AX.X, apply_absolute_value=True)
                p.opw("vector", sV, "tensor_scalar_max", cols[:, 8:9],
                      cols[:, 8:9], 1e-5)
                p.opw("vector", sV, "reciprocal", cols[:, 9:10], cols[:, 8:9])
                p.opw("vector", sV, "tensor_scalar_mul", cols[:, 10:11],
                      cols[:, 9:10], 127.0)
                p.opw("vector", sV, "tensor_single_scalar",
                      dqcol[:, DQG + j:DQG + j + 1], cols[:, 8:9],
                      scolD[:, 0:1], op=ALU.mult)
                p.opw("vector", sV, "tensor_scalar_mul",
                      dqcol[:, DQG + j:DQG + j + 1],
                      dqcol[:, DQG + j:DQG + j + 1], 1.0 / 254.0)
                p.opw("vector", sV, "tensor_single_scalar",
                      dqcol[:, DQU + j:DQU + j + 1], cols[:, 8:9],
                      scolD[:, 2:3], op=ALU.mult)
                p.opw("vector", sV, "tensor_scalar_mul",
                      dqcol[:, DQU + j:DQU + j + 1],
                      dqcol[:, DQU + j:DQU + j + 1], 1.0 / 127.0)
                p.op("vector", "scalar_tensor_tensor", tscD2[:], xnb[:],
                     cols[:, 10:11], cC[:, 0:1].broadcast_to((128, H)),
                     op0=ALU.mult, op1=ALU.add)
                p.op("vector", "tensor_scalar", tscD2[:], tscD2[:], RND_C,
                     127.0, op0=ALU.subtract, op1=ALU.min)
                tQ = p.op("vector", "tensor_scalar_max", qb[:], tscD2[:],
                          -128.0, sem=sV)
                for (srcb, dstb, tok) in ((xnb, xnT, tXn), (qb, qbT, tQ)):
                    p.wait("tensor", sV, max(tokTRd, tok))
                    lt = None
                    for i in range(8):
                        lt = p.op("tensor", "transpose",
                                  ptrC[:, i * 128:(i + 1) * 128],
                                  srcb[:, i * 128:(i + 1) * 128], identb[:],
                                  sem=sT if i == 7 else None)
                    p.wait("vector", sT, lt)
                    tokTRd = p.op("vector", "tensor_copy", dstb[:], ptrC[:],
                                  sem=sV)
                p.wait("gpsimd", sV, tokTRd)
                tokQT[0] = p.op(
                    "gpsimd", "dma_start", sem=sWR, step=16,
                    out=xn1tp_hbm[j].rearrange("(k q) t -> q k t", q=128),
                    in_=xnT[:])
                tokQT[0] = p.op(
                    "gpsimd", "dma_start", sem=sWR, step=16,
                    out=xqt_hbm[j].rearrange("(k q) t -> q k t", q=128),
                    in_=qbT[:])
                p.wait("vector", sWR, tokQT[0])

            p.wait("sync", sWR, sWR.count)
            tokGuW = {0: 0, 1: 0}
            tokGuRd = 0
            tokXq = {0: 0, 1: 0}
            tokMM = 0
            for s in range(4):
                tWS = 0
                p.wait("sync", sT, tokMM)
                for kt in range(KT):
                    tWS = p.op("sync", "dma_start", sem=sLD, step=16,
                               out=wg_sl[:, kt * H:(kt + 1) * H],
                               in_=wgq_hbm[kt][:, s * H:(s + 1) * H])
                    tWS = p.op("sync", "dma_start", sem=sLD, step=16,
                               out=wu_sl[:, kt * H:(kt + 1) * H],
                               in_=wuq_hbm[kt][:, s * H:(s + 1) * H])
                for j in range(NOWN):
                    pr = j % 2
                    p.wait("sync", sT, tokXq[pr])
                    tX = p.op("sync", "dma_start", sem=sLD, step=16,
                              out=xq_l[pr][:],
                              in_=xqt_hbm[j].rearrange("(k q) t -> q k t",
                                                       q=128))
                    p.wait("tensor", sLD, max(tX, tWS))
                    p.wait("tensor", sV, tokGuRd)
                    for kt in range(KT):
                        p.op("tensor", "matmul", pbig[:, 0:512],
                             xq_l[pr][:, kt * 128:(kt + 1) * 128],
                             wg_sl[:, kt * H:kt * H + 512],
                             start=(kt == 0), stop=(kt == KT - 1))
                    for kt in range(KT):
                        p.op("tensor", "matmul", pbig[:, 512:1024],
                             xq_l[pr][:, kt * 128:(kt + 1) * 128],
                             wg_sl[:, kt * H + 512:(kt + 1) * H],
                             start=(kt == 0), stop=(kt == KT - 1))
                    for kt in range(KT):
                        p.op("tensor", "matmul", pu[:],
                             xq_l[pr][:, kt * 128:(kt + 1) * 128],
                             wu_sl[:, kt * H:kt * H + 512],
                             start=(kt == 0), stop=(kt == KT - 1))
                    for kt in range(KT):
                        tokMM = p.op("tensor", "matmul", ptiny[:],
                                     xq_l[pr][:, kt * 128:(kt + 1) * 128],
                                     wu_sl[:, kt * H + 512:(kt + 1) * H],
                                     start=(kt == 0), stop=(kt == KT - 1),
                                     sem=sT if kt == KT - 1 else None)
                    tokXq[pr] = tokMM
                    p.wait("scalar", sT, tokMM)
                    tS1 = p.op("scalar", "activation", tscD[:], pbig[:],
                               AF.Tanh, scale=dqcol[:, DQG + j:DQG + j + 1],
                               sem=sS)
                    p.wait("vector", sS, tS1)
                    p.op("vector", "tensor_scalar_add", tscD[:], tscD[:], 1.0)
                    p.op("vector", "scalar_tensor_tensor", tscD2[:], pbig[:],
                         dqcol[:, DQG + j:DQG + j + 1], tscD[:], op0=ALU.mult,
                         op1=ALU.mult)
                    p.op("vector", "scalar_tensor_tensor",
                         gub[:, 0:512], pu[:], dqcol[:, DQU + j:DQU + j + 1],
                         tscD2[:, 0:512], op0=ALU.mult, op1=ALU.mult)
                    tGu = p.op("vector", "scalar_tensor_tensor",
                               gub[:, 512:1024], ptiny[:],
                               dqcol[:, DQU + j:DQU + j + 1],
                               tscD2[:, 512:1024], op0=ALU.mult, op1=ALU.mult,
                               sem=sV)
                    tokGuRd = tGu
                    p.opw("vector", sV, "reduce_max", cols[:, 12:13], gub[:],
                          axis=AX.X, apply_absolute_value=True)
                    p.opw("vector", sV, "tensor_max",
                          dqcol[:, AMG + j:AMG + j + 1],
                          dqcol[:, AMG + j:AMG + j + 1], cols[:, 12:13])
                    p.wait("gpsimd", sV, tGu)
                    p.wait("gpsimd", sWR, tokGuW[pr])
                    tokGuW[pr] = p.op("gpsimd", "dma_start", sem=sWR, step=16,
                                      out=gu_hbm[j][:, s * H:(s + 1) * H],
                                      in_=gub[:])
                    p.wait("vector", sWR, tokGuW[pr])

            for j in range(NOWN):
                p.opw("vector", sV, "tensor_scalar_max",
                      dqcol[:, AMG + j:AMG + j + 1],
                      dqcol[:, AMG + j:AMG + j + 1], 1e-5)
                p.opw("vector", sV, "tensor_single_scalar",
                      dqcol[:, AMG + j:AMG + j + 1],
                      dqcol[:, AMG + j:AMG + j + 1], scolD[:, 4:5],
                      op=ALU.mult)
                p.opw("vector", sV, "tensor_scalar_mul",
                      dqcol[:, AMG + j:AMG + j + 1],
                      dqcol[:, AMG + j:AMG + j + 1], 1.0 / 254.0)

            es_d1.close()
            wd_w = es_d2.enter_context(
                nc.sbuf_tensor("wd_w", [128, 32 * H], BF16))
            wg_w = es_d2.enter_context(
                nc.sbuf_tensor("wg_wD", [128, KT * H], BF16))
            wo_w = es_d2.enter_context(
                nc.sbuf_tensor("wo_wD", [128, KT * H], BF16))
            p.wait("sync", sV, sV.count)
            p.wait("sync", sWR, sWR.count)
            tWD = 0
            for kt in range(32):
                tWD = p.op("sync", "dma_start", sem=sLD, step=16,
                           out=wd_w[:, kt * H:(kt + 1) * H], in_=wdq_hbm[kt])
            for kt in range(KT):
                t = p.op("sync", "dma_start", sem=sLD, step=16,
                         out=tscD[:], in_=Wg_in[kt * 128:(kt + 1) * 128, :])
                p.wait("vector", sLD, t)
                tv = p.op("vector", "tensor_copy", wg_w[:, kt * H:(kt + 1) * H],
                          tscD[:], sem=sV)
                p.wait("sync", sV, tv)
                t = p.op("sync", "dma_start", sem=sLD, step=16,
                         out=tscD[:], in_=Wo_in[kt * 128:(kt + 1) * 128, :])
                p.wait("vector", sLD, t)
                tv = p.op("vector", "tensor_copy", wo_w[:, kt * H:(kt + 1) * H],
                          tscD[:], sem=sV)
                p.wait("sync", sV, tv)

            tokOut = {0: 0, 1: 0}
            tokGuL = {0: 0, 1: 0}
            tokTRd = 0
            tokQrd = 0
            tokG1 = 0
            tokOg = 0
            tokGp = 0
            tokSg = 0
            tokOselRd = 0
            p.wait("tensor", sLD, sLD.count)
            for j in range(NOWN):
                pr = j % 2
                p.wait("sync", sV, tokGuL[pr])
                tGu = p.op("sync", "dma_start", sem=sLD, step=16,
                           out=gu_l[pr][:], in_=gu_hbm[j])
                p.opw("vector", sV, "tensor_single_scalar", cols[:, 15:16],
                      dqcol[:, AMG + j:AMG + j + 1], scolD[:, 5:6],
                      op=ALU.mult)
                p.opw("vector", sV, "reciprocal", cols[:, 15:16],
                      cols[:, 15:16])
                p.opw("vector", sV, "tensor_scalar_mul", cols[:, 15:16],
                      cols[:, 15:16], 0.5)
                p.wait("vector", sLD, tGu)
                for q4 in range(4):
                    sl = slice(q4 * H, (q4 + 1) * H)
                    p.wait("vector", sT, tokQrd if q4 == 0 else 0)
                    p.op("vector", "scalar_tensor_tensor", tscD2[:],
                         gu_l[pr][:, sl], cols[:, 15:16],
                         cC[:, 0:1].broadcast_to((128, H)), op0=ALU.mult,
                         op1=ALU.add)
                    p.op("vector", "tensor_scalar", tscD2[:], tscD2[:], RND_C,
                         127.0, op0=ALU.subtract, op1=ALU.min)
                    tQ = p.op("vector", "tensor_scalar_max", gub[:], tscD2[:],
                              -128.0, sem=sV)
                    p.wait("tensor", sV, max(tokTRd, tQ))
                    lt = None
                    for i in range(8):
                        lt = p.op("tensor", "transpose",
                                  ptrC[:, i * 128:(i + 1) * 128],
                                  gub[:, i * 128:(i + 1) * 128], identb[:],
                                  sem=sT if i == 7 else None)
                    p.wait("vector", sT, lt)
                    tokTRd = p.op("vector", "tensor_copy", guqT[:, sl],
                                  ptrC[:], sem=sV)
                tokGuL[pr] = tokTRd
                p.wait("tensor", sV, max(tokTRd, tokG1))
                for kt in range(32):
                    p.op("tensor", "matmul", pbig[:, 0:512],
                         guqT[:, kt * 128:(kt + 1) * 128],
                         wd_w[:, kt * H:kt * H + 512],
                         start=(kt == 0), stop=(kt == 31))
                tDn = None
                for kt in range(32):
                    tDn = p.op("tensor", "matmul", pbig[:, 512:1024],
                               guqT[:, kt * 128:(kt + 1) * 128],
                               wd_w[:, kt * H + 512:(kt + 1) * H],
                               start=(kt == 0), stop=(kt == 31),
                               sem=sT if kt == 31 else None)
                tokQrd = tDn
                p.wait("scalar", sT, tDn)
                tS1 = p.op("scalar", "activation", tscD[:], pbig[:], AF.Tanh,
                           scale=dqcol[:, AMG + j:AMG + j + 1], sem=sS)
                p.wait("vector", sS, tS1)
                p.op("vector", "tensor_scalar_add", tscD[:], tscD[:], 1.0)
                tokG1 = p.op("vector", "scalar_tensor_tensor", g1t[:], pbig[:],
                             dqcol[:, AMG + j:AMG + j + 1], tscD[:],
                             op0=ALU.mult, op1=ALU.mult, sem=sV)
                p.wait("sync", sV, tokOselRd)
                t = p.op("sync", "dma_start", sem=sLD, step=16, out=xnb[:],
                         in_=o_hbm[j])
                p.wait("vector", sLD, t)
                tok1 = p.op("vector", "tensor_single_scalar", xsel[:], xnb[:],
                            zmc[:, 0:1], op=ALU.mult, sem=sV)
                p.wait("sync", sV, tok1)
                t = p.op("sync", "dma_start", sem=sLD, step=16, out=xnb[:],
                         in_=o_hbm[j + NOWN])
                p.wait("vector", sLD, t)
                tokOselRd = p.op("vector", "scalar_tensor_tensor", xsel[:],
                                 xnb[:], zmc[:, 1:2], xsel[:], op0=ALU.mult,
                                 op1=ALU.add, sem=sV)
                p.op("vector", "tensor_mul", xsel[:], xsel[:], SCLb[:])
                tOn = p.op("vector", "tensor_add", xsel[:], xsel[:], BIAb[:],
                           sem=sV)
                p.wait("sync", sT, tokGp)
                tX = p.op("sync", "dma_start", sem=sLD, step=16, out=xnT[:],
                          in_=xn1tp_hbm[j].rearrange("(k q) t -> q k t",
                                                     q=128))
                p.wait("tensor", sLD, tX)
                p.wait("tensor", sS, tokSg)
                for kt in range(KT):
                    p.op("tensor", "matmul", pu[:],
                         xnT[:, kt * 128:(kt + 1) * 128],
                         wg_w[:, kt * H:kt * H + 512],
                         start=(kt == 0), stop=(kt == KT - 1))
                tGp = None
                for kt in range(KT):
                    tGp = p.op("tensor", "matmul", ptiny[:],
                               xnT[:, kt * 128:(kt + 1) * 128],
                               wg_w[:, kt * H + 512:(kt + 1) * H],
                               start=(kt == 0), stop=(kt == KT - 1),
                               sem=sT if kt == KT - 1 else None)
                tokGp = tGp
                p.wait("scalar", sT, tGp)
                tS2 = p.op("scalar", "activation", tscD[:, 0:512], pu[:],
                           AF.Tanh, scale=0.5, sem=sS)
                tS2 = p.op("scalar", "activation", tscD[:, 512:1024],
                           ptiny[:], AF.Tanh, scale=0.5, sem=sS)
                tokSg = tS2
                p.wait("vector", sS, tS2)
                p.op("vector", "tensor_scalar", tscD[:], tscD[:], 0.5, 0.5,
                     op0=ALU.mult, op1=ALU.add)
                tOgb = p.op("vector", "tensor_mul", ogb[:], xsel[:], tscD[:],
                            sem=sV)
                p.wait("tensor", sV, max(tokTRd, tOgb))
                lt = None
                for i in range(8):
                    lt = p.op("tensor", "transpose",
                              ptrC[:, i * 128:(i + 1) * 128],
                              ogb[:, i * 128:(i + 1) * 128], identb[:],
                              sem=sT if i == 7 else None)
                p.wait("vector", sT, lt)
                tokTRd = p.op("vector", "tensor_copy", ogT[:], ptrC[:], sem=sV)
                tokOg = tokTRd
                p.wait("tensor", sV, max(tokTRd, tokG1))
                for kt in range(KT):
                    p.op("tensor", "matmul", pbig[:, 0:512],
                         ogT[:, kt * 128:(kt + 1) * 128],
                         wo_w[:, kt * H:kt * H + 512],
                         start=(kt == 0), stop=(kt == KT - 1))
                tWo = None
                for kt in range(KT):
                    tWo = p.op("tensor", "matmul", pbig[:, 512:1024],
                               ogT[:, kt * 128:(kt + 1) * 128],
                               wo_w[:, kt * H + 512:(kt + 1) * H],
                               start=(kt == 0), stop=(kt == KT - 1),
                               sem=sT if kt == KT - 1 else None)
                tokQrd = tWo
                p.wait("sync", sV, tokXaRd)
                tX1 = p.op("sync", "dma_start", sem=sLD, step=16, out=xa[:],
                           in_=x1p_hbm[j])
                p.wait("vector", sT, tWo)
                p.wait("vector", sLD, tX1)
                p.wait("vector", sWR, tokOut[pr])
                p.op("vector", "tensor_mul", outt[pr][:], pbig[:], g1t[:])
                tOu = p.op("vector", "tensor_add", outt[pr][:], outt[pr][:],
                           xa[:], sem=sV)
                tokXaRd = tOu
                p.wait("gpsimd", sV, tOu)
                tokOut[pr] = p.op("gpsimd", "dma_start", sem=sWR, step=16,
                                  out=out[j * 128:(j + 1) * 128, :],
                                  in_=outt[pr][:])
            p.wait("gpsimd", sWR, sWR.count)
            p.wait("vector", sV, sV.count)
            p.wait("sync", sWR, sWR.count)
            p.replay(blk)
        es_d2.close()
        es_d.close()

    es.close()
    return nc, dbg_t, inp



def make_in_map(inputs, b, z=0, L=None):
    if L is None:
        L = L_TOKENS
    f32 = np.float32
    x = np.ascontiguousarray(inputs["L0_attn_input_sequence"][b, :L]).astype(f32, copy=False)
    sp = np.stack([inputs["L0_attn_steps_raw"].reshape(2, 128).T,
                   inputs["L0_attn_G_raw"].reshape(2, 128).T,
                   inputs["L0_attn_A_raw"].reshape(2, 128).T], axis=1)
    ssm_par = np.ascontiguousarray(sp.reshape(128, 6).astype(f32))
    Bp = np.ascontiguousarray(
        inputs["L0_attn_B_mat"].transpose(1, 0, 2).reshape(H, 2 * P).astype(f32))
    Cp = np.ascontiguousarray(
        inputs["L0_attn_C_mat"].transpose(1, 2, 0).reshape(2 * P, H).astype(f32))
    Db = np.ascontiguousarray(
        np.broadcast_to(inputs["L0_attn_D_vec"][None, :], (128, H)).astype(f32))
    rhs = np.ascontiguousarray(inputs["L0_ffn_rhs"].astype(f32))
    rhsT = np.ascontiguousarray(rhs.T)
    gn_row = np.ascontiguousarray(
        np.concatenate([inputs["L1_attn_gn_w"], inputs["L1_attn_gn_b"]])[None, :]
        .astype(f32))
    bab = np.ascontiguousarray(
        np.broadcast_to(inputs["L1_attn_ba"][None, :], (128, NH)).astype(f32))
    triu_np = np.triu(np.ones((128, 128), np.float32))
    tril_np = np.tril(np.ones((128, 128), np.float32))
    ident_np = np.eye(128, dtype=np.float32)
    pm1_np = np.ones((128, 128), np.float32)
    pm1_np[1::2, :] = -1.0
    triuS_np = np.triu(np.ones((128, 128), np.float32), 1)
    zpad = np.zeros((128, 128), np.float32)
    consts_in = np.ascontiguousarray(np.concatenate(
        [triu_np, tril_np, ident_np, pm1_np, triuS_np, zpad], axis=1))
    return {
        "zmcols": np.ascontiguousarray(
            np.broadcast_to(np.array([1.0 - z, float(z)], np.float32)[None, :],
                            (128, 2))),
        "consts_in": consts_in,
        "x0": x,
        "ssm_par": ssm_par,
        "Bp_in": Bp,
        "Cp_in": Cp,
        "Db": Db,
        "rhs_in": rhs,
        "rhsT_in": rhsT,
        "Wq_in": np.ascontiguousarray(inputs["L1_attn_Wq"].astype(f32)),
        "Wk_in": np.ascontiguousarray(inputs["L1_attn_Wk"].astype(f32)),
        "Wv_in": np.ascontiguousarray(inputs["L1_attn_Wv"].astype(f32)),
        "Wg_in": np.ascontiguousarray(inputs["L1_attn_Wg"].astype(f32)),
        "Wa_in": np.ascontiguousarray(inputs["L1_attn_Wa"].astype(f32)),
        "bab": bab,
        "Wo_in": np.ascontiguousarray(inputs["L1_attn_Wo"].astype(f32)),
        "gn_row": gn_row,
        "wgT_in": np.ascontiguousarray(inputs["L1_ffn_w_gate"].T.astype(f32)),
        "wuT_in": np.ascontiguousarray(inputs["L1_ffn_w_up"].T.astype(f32)),
        "wdT_in": np.ascontiguousarray(inputs["L1_ffn_w_down"].T.astype(f32)),
    }


def kernel(**inputs):
    nc, _, _ = build_program()
    in_maps = []
    for c in range(8):
        b = c // 2
        m = make_in_map(inputs, b, z=c % 2)
        in_maps.append(m)
    res = run_bass_kernel_spmd(nc, in_maps, core_ids=list(range(8)))
    B, L = 4, L_TOKENS
    outp = np.empty((B, L, H), np.float32)
    for c in range(8):
        b, zz = c // 2, c % 2
        outp[b, zz * (L // 2):(zz + 1) * (L // 2)] = res.results[c]["out"]
    return outp


# version 02
# speedup vs baseline: 1.0218x; 1.0218x over previous
import contextlib

import numpy as np
import concourse.bass as bass
import concourse.mybir as mybir
from concourse.bass_utils import run_bass_kernel_spmd

F32 = mybir.dt.float32
BF16 = mybir.dt.bfloat16
I32 = mybir.dt.int32
AF = mybir.ActivationFunctionType
ALU = mybir.AluOpType
AX = mybir.AxisListType

H = 1024
P = 256
NH = 16
DH = 64
INNER = 4096
CT = 128
L_TOKENS = 8192
RND_C = 1.5 * 2.0 ** 23
MAGIC_RSQRT = 0x5F3759DF

ENGINES = ("sync", "gpsimd", "vector", "scalar", "tensor")


class Sem:
    def __init__(self, handle):
        self.h = handle
        self.count = 0


class Prog:

    def __init__(self, esems=None):
        self.streams = {e: [] for e in ENGINES}
        self.esems = esems or {}

    def op(self, eng, fname, *args, sem=None, step=1, **kw):
        ser = self.esems.get(eng)
        if ser is not None and fname != "dma_start":
            if sem is None:
                sem = ser
            if sem is ser:
                self.wait(eng, ser, ser.count)
        tok = None
        if sem is not None:
            sem.count += step
            tok = sem.count

        def fn(E, fname=fname, args=args, kw=kw, sem=sem, step=step):
            rargs = [a() if callable(a) else a for a in args]
            rkw = {k: (v() if callable(v) else v) for k, v in kw.items()}
            ins = getattr(E, fname)(*rargs, **rkw)
            if sem is not None:
                ins.then_inc(sem.h, step)

        self.streams[eng].append(fn)
        if fname == "dma_start" and sem is not None:
            self.wait(eng, sem, tok)
        return tok

    def opw(self, eng, sem, fname, *args, **kw):
        tok = self.op(eng, fname, *args, sem=sem, **kw)
        self.wait(eng, sem, tok)
        return tok

    def wait(self, eng, sem, tok):
        if tok is None or tok <= 0:
            return
        self.streams[eng].append(lambda E, s=sem, t=tok: E.wait_ge(s.h, t))

    def raw(self, eng, fn):
        self.streams[eng].append(fn)

    def replay(self, block):
        for name in ENGINES:
            fns = self.streams[name]
            if not fns:
                continue

            def make(fl):
                def body(E):
                    for f in fl:
                        f(E)
                return body

            getattr(block, name)(make(fns))


def emit_ln_cols(p, sV, cols, i_sum, i_sq, i_out_is, i_out_negmis, n, eps,
                 sG=None, cmagic=None):
    c = cols
    p.opw("vector", sV, "tensor_scalar_mul", c[:, 24:25], c[:, i_sum:i_sum + 1],
          1.0 / n)
    p.opw("vector", sV, "tensor_scalar_mul", c[:, 25:26], c[:, i_sq:i_sq + 1], 1.0 / n)
    p.opw("vector", sV, "tensor_mul", c[:, 26:27], c[:, 24:25], c[:, 24:25])
    p.opw("vector", sV, "tensor_sub", c[:, 25:26], c[:, 25:26], c[:, 26:27])
    p.opw("vector", sV, "tensor_scalar_add", c[:, 25:26], c[:, 25:26], eps)
    tv = p.opw("vector", sV, "tensor_single_scalar", c[:, 26:27].bitcast(I32),
               c[:, 25:26].bitcast(I32), 1, op=ALU.logical_shift_right)
    p.wait("gpsimd", sV, tv)
    tg = p.op("gpsimd", "tensor_tensor", c[:, 26:27].bitcast(I32),
              cmagic[:], c[:, 26:27].bitcast(I32), op=ALU.subtract, sem=sG)
    p.wait("vector", sG, tg)
    p.opw("vector", sV, "tensor_scalar_mul", c[:, 27:28], c[:, 25:26], -0.5)
    for _ in range(3):
        p.opw("vector", sV, "tensor_mul", c[:, 28:29], c[:, 26:27], c[:, 26:27])
        p.opw("vector", sV, "tensor_mul", c[:, 28:29], c[:, 28:29], c[:, 27:28])
        p.opw("vector", sV, "tensor_scalar_add", c[:, 28:29], c[:, 28:29], 1.5)
        p.opw("vector", sV, "tensor_mul", c[:, 26:27], c[:, 26:27], c[:, 28:29])
    p.opw("vector", sV, "tensor_copy", c[:, i_out_is:i_out_is + 1], c[:, 26:27])
    p.opw("vector", sV, "tensor_mul", c[:, 29:30], c[:, 24:25], c[:, 26:27])
    return p.opw("vector", sV, "tensor_scalar_mul",
                 c[:, i_out_negmis:i_out_negmis + 1], c[:, 29:30], -1.0)


def build_program(L=None, z=0, run_phases=("A", "B", "C", "D"), dbg=False):
    if L is None:
        L = L_TOKENS
    NCH = L // CT
    KT = H // 128
    NOWN = (L // 2) // CT
    OWN0 = NOWN * z

    nc = bass.Bass("TRN2", target_bir_lowering=False, debug=False)

    inp = {}

    def I(name, shape):
        inp[name] = nc.dram_tensor(name, list(shape), F32, kind="ExternalInput").ap()
        return inp[name]

    x0 = I("x0", (L, H))
    ssm_par = I("ssm_par", (128, 6))
    Bp_in = I("Bp_in", (H, 2 * P))
    Cp_in = I("Cp_in", (2 * P, H))
    Db = I("Db", (128, H))
    rhs_in = I("rhs_in", (H, H))
    rhsT_in = I("rhsT_in", (H, H))
    Wq_in = I("Wq_in", (H, H))
    Wk_in = I("Wk_in", (H, H))
    Wv_in = I("Wv_in", (H, H))
    Wg_in = I("Wg_in", (H, H))
    Wa_in = I("Wa_in", (H, NH))
    bab = I("bab", (128, NH))
    Wo_in = I("Wo_in", (H, H))
    gn_row = I("gn_row", (1, 2 * H))
    zmcols = I("zmcols", (128, 2))
    wgT_in = I("wgT_in", (H, INNER))
    wuT_in = I("wuT_in", (H, INNER))
    wdT_in = I("wdT_in", (INNER, H))

    out = nc.dram_tensor("out", [L // 2, H], F32, kind="ExternalOutput").ap()

    dbg_t = {}
    if dbg:
        for nm, shape, dt in (
            ("d_xn0", (L, H), BF16), ("d_g0", (L, H), F32), ("d_s0", (L, H), F32),
            ("d_x1", (L, H), F32), ("d_za", (128, NCH * NH), F32),
            ("d_cum", (128, NCH * NH), F32), ("d_ln1", (128, NCH * 4), F32),
            ("d_setup", (128, 4864), F32), ("d_ffn0", (128, 4 * H), F32),
            ("d_gst", (128, 2 * NH), F32), ("d_o", (L, H), BF16),
        ):
            dbg_t[nm] = nc.dram_tensor(nm, list(shape), dt, kind="ExternalOutput").ap()

    xn1t_hbm = nc.dram_tensor("xn1t_hbm", [NCH, KT * 128, 128], BF16, kind="Internal").ap()
    x1_hbm = nc.dram_tensor("x1_hbm", [NCH, 128, H], F32, kind="Internal").ap()
    o_hbm = nc.dram_tensor("o_hbm", [NCH, 128, H], BF16, kind="Internal").ap()
    wgq_hbm = nc.dram_tensor("wgq_hbm", [KT, 128, INNER], BF16, kind="Internal").ap()
    wuq_hbm = nc.dram_tensor("wuq_hbm", [KT, 128, INNER], BF16, kind="Internal").ap()
    wdq_hbm = nc.dram_tensor("wdq_hbm", [INNER // 128, 128, H], BF16, kind="Internal").ap()
    rowscr = nc.dram_tensor("rowscr", [NCH, NH], F32, kind="Internal").ap()
    cumt_hbm = nc.dram_tensor("cumt_hbm", [NCH, NH, 128], F32, kind="Internal").ap()
    gu_hbm = nc.dram_tensor("gu_hbm", [L // 256, 128, INNER], BF16, kind="Internal").ap()
    xqt_hbm = nc.dram_tensor("xqt_hbm", [L // 256, KT * 128, 128], BF16, kind="Internal").ap()
    xn1tp_hbm = nc.dram_tensor("xn1tp_hbm", [L // 256, KT * 128, 128], BF16, kind="Internal").ap()
    x1p_hbm = nc.dram_tensor("x1p_hbm", [L // 256, 128, H], F32, kind="Internal").ap()

    consts = nc.dram_tensor("consts_in", [128, 768], F32,
                            kind="ExternalInput").ap()

    es = contextlib.ExitStack()
    es_s0 = contextlib.ExitStack()
    es_ab = contextlib.ExitStack()

    def _sb(stack, name, shape, dt=F32):
        return stack.enter_context(nc.sbuf_tensor(name, list(shape), dt))

    def sb(name, shape, dt=F32):
        return _sb(es, name, shape, dt)

    def sb0(name, shape, dt=F32):
        return es_s0.enter_context(nc.sbuf_tensor(name, list(shape), dt, side="right"))

    def sbab(name, shape, dt=F32):
        return _sb(es_ab, name, shape, dt)

    def psb(name, shape, dt=F32):
        return es.enter_context(nc.psum_tensor(name, list(shape), dt))

    def mksem(name):
        return Sem(es.enter_context(nc.semaphore(name)))

    tri = sb("tri", (128, 128))
    ident = sb("identt", (128, 128))
    identb = sb("identb", (128, 128), BF16)
    pm1 = sb("pm1", (128, 1))
    triuS = sb("triuS", (128, 128))
    trilx8 = sb("trilx8", (128, 8 * 128))
    bab_t = sb("bab_t", (128, NH))
    ln1sv = sb("ln1sv", (128, NCH * 4))
    cumall = sb("cumall", (128, NCH * NH))
    gstat = sb("gstat", (128, 2 * NH))
    cC = sb("cC", (128, 1))
    cmagic = sb("cmagic", (128, 1), I32)
    c05 = sb("c05", (128, 1))
    onescol = sb("onescol", (1, 128))
    ones128 = sb("ones128", (128, 1))
    cols = sb("colsA", (128, 32))
    zmc = sb("zmc", (128, 2))
    scolD = sb("scolD", (128, 6))

    q0e = sbab("q0e", (128, 512))
    q1e = sbab("q1e", (128, 512))
    r21e = sbab("r21e", (128, 512))
    r22e = sbab("r22e", (128, 512))
    trow = [sbab(f"trow{i}", (1, 512)) for i in range(4)]
    rscr = [sbab(f"rscr{i}", (1, 512)) for i in range(2)]
    grow = [sbab(f"grow{i}", (1, 512)) for i in range(2)]
    dexp = sbab("dexp", (128, H), BF16)
    rsexp = sbab("rsexp", (128, H))
    bpw = sbab("bpw", (128, KT * 512), BF16)
    cpw = sbab("cpw", (128, 4 * H), BF16)
    rhq = sbab("rhq", (128, KT * H), BF16)
    wab = sbab("wab", (128, KT * NH), BF16)
    zaall = sbab("zaall", (128, NCH * NH))
    sqdump = sbab("sqdump", (128, H))
    x0t = [sbab(f"x0t{i}", (128, H)) for i in range(2)]
    xn0 = [sbab(f"xn0_{i}", (128, H), BF16) for i in range(2)]
    xn0T = [sbab(f"xn0T{i}", (128, H), BF16) for i in range(2)]
    qtl = [sbab(f"qtl{i}", (128, H), BF16) for i in range(2)]
    qtlT = [sbab(f"qtlT{i}", (128, H), BF16) for i in range(2)]
    g0t = [sbab(f"g0t{i}", (128, H)) for i in range(2)]
    sc0 = [sbab(f"sc0_{i}", (128, H)) for i in range(2)]
    x1t = [sbab(f"x1t{i}", (128, H)) for i in range(2)]
    xn1 = [sbab(f"xn1_{i}", (128, H), BF16) for i in range(2)]
    xn1T = [sbab(f"xn1T{i}", (128, H), BF16) for i in range(2)]
    v0t = [sbab(f"v0t{i}", (128, 512)) for i in range(2)]
    v1t = [sbab(f"v1t{i}", (128, 512)) for i in range(2)]
    yst = [sbab(f"yst{i}", (128, 512), BF16) for i in range(2)]
    ysT = [sbab(f"ysT{i}", (128, 512), BF16) for i in range(2)]
    tsc = [sbab(f"tsc{i}", (128, H)) for i in range(2)]
    tsc2 = [sbab(f"tsc2_{i}", (128, H)) for i in range(2)]

    pbig = psb("pbig", (128, 1024))
    pu = psb("pu", (128, 512))
    ph0 = psb("ph0", (128, 512))
    ph1 = psb("ph1", (128, 512))
    ptiny = psb("ptiny", (128, 512))
    ptr = es_ab.enter_context(nc.psum_tensor("ptr", [128, 1024], BF16))
    ptr2 = es_ab.enter_context(nc.psum_tensor("ptr2", [128, 1024], BF16))

    sLD = mksem("sLD")
    sWR = mksem("sWR")
    sT = mksem("sT")
    sV = mksem("sV")
    sS = mksem("sS")
    sG = mksem("sG")

    WSTG_C = 2048
    wstg = sb0("wstg", (128, WSTG_C))
    wbf_stage = sb0("wbf_stage", (128, WSTG_C), BF16)

    with nc.Block() as blk:
        p = Prog(esems={"vector": sV, "scalar": sS, "gpsimd": sG})

        def ld(dst, src):
            return p.op("sync", "dma_start", sem=sLD, step=16, out=dst, in_=src)

        t_consts = ld(tri[:], consts[:, 0:128])
        t_ident = ld(ident[:], consts[:, 256:384])
        p.op("sync", "dma_start", sem=sLD, step=16, out=pm1[:],
             in_=consts[:, 384:385], allow_slow_non_contiguous=True)
        parms = sb0("parms", (128, 6))
        t_par = ld(parms[:], ssm_par)
        t_db = ld(sqdump[:, 0:H], Db)
        ld(bab_t[:], bab)
        ld(triuS[:], consts[:, 512:640])
        for i8 in range(8):
            ld(trilx8[:, i8 * 128:(i8 + 1) * 128], consts[:, 0:128])

        p.wait("vector", sLD, t_consts)
        p.op("vector", "memset", cC[:], RND_C)
        p.op("vector", "memset", c05[:], 0.5)
        p.op("vector", "memset", onescol[:], 1.0)
        p.op("vector", "memset", ones128[:], 1.0)
        p.op("gpsimd", "memset", cmagic[:], MAGIC_RSQRT)
        p.op("vector", "memset", grow[0][:], 0.0)
        p.op("vector", "memset", grow[1][:], 0.0)
        p.op("vector", "memset", gstat[:], 0.0)
        p.op("vector", "memset", ln1sv[:], 0.0)
        p.wait("vector", sLD, t_ident)
        p.op("vector", "tensor_copy", identb[:], ident[:])
        p.wait("vector", sLD, t_db)
        p.op("vector", "tensor_copy", dexp[:], sqdump[:, 0:H])

        d = {nm: sb0(f"dv_{nm}", (128, 2)) for nm in
             ("steps", "S", "alpha", "c1", "c2", "m00", "m01", "m10", "m11",
              "i00", "i01", "i10", "i11", "t1", "rS")}
        stepsraw = parms[:, 0:2]
        graw = parms[:, 2:4]
        araw = parms[:, 4:6]
        p.wait("scalar", sLD, t_par)
        tok = p.op("scalar", "activation", d["t1"][:], stepsraw, AF.Tanh, scale=0.5,
                   sem=sS)
        p.wait("vector", sS, tok)
        tok = p.opw("vector", sV, "tensor_scalar", d["steps"][:], d["t1"][:], 0.5, 0.5,
                   op0=ALU.mult, op1=ALU.add)
        p.wait("vector", sV, tok)
        p.opw("vector", sV, "tensor_scalar_max", d["t1"][:], graw, 0.0)
        p.opw("vector", sV, "tensor_mul", d["S"][:], d["t1"][:], d["steps"][:])
        p.opw("vector", sV, "tensor_scalar_add", d["S"][:], d["S"][:], 1.0)
        p.opw("vector", sV, "reciprocal", d["rS"][:], d["S"][:])
        p.opw("vector", sV, "tensor_scalar_max", d["t1"][:], araw, 0.0)
        p.opw("vector", sV, "tensor_mul", d["t1"][:], d["t1"][:], d["steps"][:])
        p.opw("vector", sV, "tensor_mul", d["t1"][:], d["t1"][:], d["steps"][:])
        tok = p.opw("vector", sV, "tensor_mul", d["t1"][:], d["t1"][:], d["rS"][:])
        p.wait("scalar", sV, tok)
        tok = p.op("scalar", "activation", d["alpha"][:], d["t1"][:], AF.Tanh,
                   scale=1.0 / 1.99, sem=sS)
        p.wait("vector", sS, tok)
        p.opw("vector", sV, "tensor_scalar_mul", d["alpha"][:], d["alpha"][:], 1.99)
        p.opw("vector", sV, "tensor_copy", d["m00"][:], d["rS"][:])
        p.opw("vector", sV, "reciprocal", d["t1"][:], d["steps"][:])
        p.opw("vector", sV, "tensor_mul", d["m01"][:], d["alpha"][:], d["t1"][:])
        p.opw("vector", sV, "tensor_scalar_mul", d["m01"][:], d["m01"][:], -1.0)
        p.opw("vector", sV, "tensor_mul", d["m10"][:], d["steps"][:], d["rS"][:])
        p.opw("vector", sV, "tensor_scalar", d["m11"][:], d["alpha"][:], -1.0, 1.0,
             op0=ALU.mult, op1=ALU.add)
        p.opw("vector", sV, "tensor_copy", d["c1"][:], d["m10"][:])
        p.opw("vector", sV, "tensor_mul", d["c2"][:], d["m10"][:], d["steps"][:])
        p.opw("vector", sV, "tensor_mul", d["i00"][:], d["m11"][:], d["S"][:])
        p.opw("vector", sV, "tensor_mul", d["i01"][:], d["m01"][:], d["S"][:])
        p.opw("vector", sV, "tensor_scalar_mul", d["i01"][:], d["i01"][:], -1.0)
        p.opw("vector", sV, "tensor_mul", d["i10"][:], d["m10"][:], d["S"][:])
        p.opw("vector", sV, "tensor_scalar_mul", d["i10"][:], d["i10"][:], -1.0)
        p.opw("vector", sV, "tensor_mul", d["i11"][:], d["m00"][:], d["S"][:])

        pos = {e: sb0(f"pow{e}", (128, 258)) for e in ("00", "01", "10", "11")}
        neg = {e: sb0(f"neg{e}", (128, 258)) for e in ("00", "01", "10", "11")}
        powtmp = sb0("powtmp", (128, 129))
        for tab in (pos, neg):
            for e in tab:
                p.opw("vector", sV, "memset", tab[e][:], 0.0)
        for tl in range(2):
            b = tl * 129
            for e in ("00", "11"):
                p.opw("vector", sV, "memset", pos[e][:, b:b + 1], 1.0)
                p.opw("vector", sV, "memset", neg[e][:, b:b + 1], 1.0)
            for e, srct in (("00", "m00"), ("01", "m01"), ("10", "m10"), ("11", "m11")):
                p.opw("vector", sV, "tensor_copy", pos[e][:, b + 1:b + 2],
                     d[srct][:, tl:tl + 1])
            for e, srct in (("00", "i00"), ("01", "i01"), ("10", "i10"), ("11", "i11")):
                p.opw("vector", sV, "tensor_copy", neg[e][:, b + 1:b + 2],
                     d[srct][:, tl:tl + 1])

        def extend(tab, n, tl):
            cnt = min(n, 128 - n)
            if cnt <= 0:
                return
            b = tl * 129
            for eo in ("00", "01", "10", "11"):
                r, c = eo[0], eo[1]
                sA = tab[r + "0"][:, b + n:b + n + 1]
                sB = tab[r + "1"][:, b + n:b + n + 1]
                p.opw("vector", sV, "tensor_single_scalar", powtmp[:, 0:cnt],
                     tab["0" + c][:, b + 1:b + 1 + cnt], sA, op=ALU.mult)
                p.opw("vector", sV, "scalar_tensor_tensor",
                     tab[eo][:, b + n + 1:b + n + 1 + cnt],
                     tab["1" + c][:, b + 1:b + 1 + cnt], sB, powtmp[:, 0:cnt],
                     op0=ALU.mult, op1=ALU.add)

        for tl in range(2):
            n = 1
            while n < 128:
                extend(pos, n, tl)
                extend(neg, n, tl)
                n *= 2

        qpk = {0: sb0("qpk0", (128, 258)), 1: sb0("qpk1", (128, 258))}
        for tl in range(2):
            b = tl * 129
            for comp, (ea, eb) in ((0, ("00", "01")), (1, ("10", "11"))):
                p.opw("vector", sV, "tensor_single_scalar", powtmp[:, 0:129],
                     neg[ea][:, b:b + 129], d["c1"][:, tl:tl + 1], op=ALU.mult)
                p.opw("vector", sV, "scalar_tensor_tensor", qpk[comp][:, b:b + 129],
                     neg[eb][:, b:b + 129], d["c2"][:, tl:tl + 1], powtmp[:, 0:129],
                     op0=ALU.mult, op1=ALU.add)
        tokVq = p.opw("vector", sV, "memset", powtmp[0:1, 0:1], 0.0)

        p.wait("tensor", sV, tokVq)
        p.wait("tensor", sLD, t_consts)
        evtok = {0: 0, 1: 0}

        def table_exp(srctab, base, dest, ptile, drow=None):
            nonlocal bk
            bk = 0
            pb = ptiny[:, 0:128]
            p.wait("tensor", sV, evtok[bk])
            t = p.op("tensor", "transpose", pb[0:128, 0:128] if drow is None
                     else pb[0:1, 0:128],
                     srctab[:, base:base + 128] if drow is None
                     else srctab[:, base:base + 1],
                     ident[:], sem=sT)
            p.wait("vector", sT, t)
            if drow is None:
                p.opw("vector", sV, "tensor_copy",
                     dest[:, ptile * 256 + 0:ptile * 256 + 256:2], pb[:, 0:128])
                evtok[bk] = p.opw("vector", sV, "tensor_copy",
                                 dest[:, ptile * 256 + 1:ptile * 256 + 256:2],
                                 pb[:, 0:128])
            else:
                p.opw("vector", sV, "tensor_copy",
                     dest[0:1, ptile * 256 + 0:ptile * 256 + 256:2],
                     pb[0:1, 0:128])
                evtok[bk] = p.opw("vector", sV, "tensor_copy",
                                 dest[0:1, ptile * 256 + 1:ptile * 256 + 256:2],
                                 pb[0:1, 0:128])
            bk ^= 1

        bk = 0
        for ptile in range(2):
            b = ptile * 129
            table_exp(qpk[0], b, q0e, ptile)
            table_exp(qpk[1], b, q1e, ptile)
            table_exp(pos["10"], b, r21e, ptile)
            table_exp(pos["11"], b, r22e, ptile)
        for i, e in enumerate(("00", "01", "10", "11")):
            for tl in range(2):
                table_exp(pos[e], tl * 129 + 128, trow[i], tl, drow=0)

        def cast_seq(src_dram, dst, ncols, scale_ap=None):
            t = ld(wstg[:, 0:ncols], src_dram)
            p.wait("scalar", sLD, t)
            if scale_ap is None:
                tc = p.op("scalar", "copy", dst, wstg[:, 0:ncols], sem=sS)
            else:
                tc = p.op("scalar", "activation", dst, wstg[:, 0:ncols], AF.Copy,
                          scale=scale_ap, sem=sS)
            p.wait("sync", sS, tc)
            return tc

        for kt in range(KT):
            cast_seq(Bp_in[kt * 128:(kt + 1) * 128, :], bpw[:, kt * 512:(kt + 1) * 512],
                     512)
        for pt in range(4):
            cast_seq(Cp_in[pt * 128:(pt + 1) * 128, :], cpw[:, pt * H:(pt + 1) * H],
                     H, scale_ap=pm1[:, 0:1])
        for kt in range(KT):
            cast_seq(Wa_in[kt * 128:(kt + 1) * 128, :], wab[:, kt * NH:(kt + 1) * NH],
                     NH)

        rsc_row = sb0("rsc_row", (1, H))
        lsc_row = sb0("lsc_row", (1, H))
        amx = sb0("amx", (128, 2))
        for ot in range(KT):
            t = ld(wstg[:, 0:H], rhsT_in[ot * 128:(ot + 1) * 128, :])
            p.wait("vector", sLD, t)
            p.opw("vector", sV, "reduce_max", amx[:, 0:1], wstg[:, 0:H], axis=AX.X,
                 apply_absolute_value=True)
            p.opw("vector", sV, "tensor_scalar_max", amx[:, 0:1], amx[:, 0:1], 1e-12)
            tokv = p.opw("vector", sV, "reciprocal", amx[:, 1:2], amx[:, 0:1])
            p.wait("tensor", sV, tokv)
            p.op("tensor", "transpose", ptiny[0:1, 0:128], amx[:, 0:1], ident[:])
            tokt = p.op("tensor", "transpose", ptiny[0:1, 128:256], amx[:, 1:2],
                        ident[:], sem=sT)
            p.wait("vector", sT, tokt)
            p.opw("vector", sV, "tensor_scalar_mul", lsc_row[:, ot * 128:(ot + 1) * 128],
                 ptiny[0:1, 0:128], 1.0 / 127.0)
            tokv = p.opw("vector", sV, "tensor_scalar_mul",
                        rsc_row[:, ot * 128:(ot + 1) * 128],
                        ptiny[0:1, 128:256], 127.0)
            p.wait("sync", sV, tokv)
        rscexp = wbf_stage[:].bitcast(F32)
        p.wait("tensor", sV, sV.count)
        tokv = 0
        for hh in range(2):
            p.op("tensor", "matmul", pbig[:, 0:512], onescol[:],
                 lsc_row[:, hh * 512:(hh + 1) * 512], start=True, stop=True)
            t2 = p.op("tensor", "matmul", pbig[:, 512:1024], onescol[:],
                      rsc_row[:, hh * 512:(hh + 1) * 512], start=True, stop=True,
                      sem=sT)
            p.wait("vector", sT, t2)
            p.opw("vector", sV, "tensor_copy", rsexp[:, hh * 512:(hh + 1) * 512],
                 pbig[:, 0:512])
            tokv = p.opw("vector", sV, "tensor_copy", rscexp[:, hh * 512:(hh + 1) * 512],
                        pbig[:, 512:1024])
            p.wait("tensor", sV, tokv)
        for kt in range(KT):
            t = ld(wstg[:, 0:H], rhs_in[kt * 128:(kt + 1) * 128, :])
            p.wait("vector", sLD, t)
            p.opw("vector", sV, "tensor_mul", wstg[:, 0:H], wstg[:, 0:H], rscexp)
            p.opw("vector", sV, "tensor_scalar", wstg[:, 0:H], wstg[:, 0:H], RND_C, RND_C,
                 op0=ALU.add, op1=ALU.subtract)
            p.opw("vector", sV, "tensor_scalar", wstg[:, 0:H], wstg[:, 0:H], 127.0, -127.0,
                 op0=ALU.min, op1=ALU.max)
            tokv = p.opw("vector", sV, "tensor_copy", rhq[:, kt * H:(kt + 1) * H],
                        wstg[:, 0:H])
            p.wait("sync", sV, tokv)

        def quant_ffn1(w_in, whbm, ntile, ncols, name):
            nsub = (ncols + WSTG_C - 1) // WSTG_C
            cs = min(ncols, WSTG_C)
            racc = sb0(f"racc_{name}", (128, ntile * nsub))
            for it in range(ntile):
                for hh in range(nsub):
                    t = ld(wstg[:, 0:cs],
                           w_in[it * 128:(it + 1) * 128, hh * cs:(hh + 1) * cs])
                    p.wait("vector", sLD, t)
                    tokv = p.opw("vector", sV, "reduce_sum",
                                racc[:, it * nsub + hh:it * nsub + hh + 1],
                                wstg[:, 0:cs], axis=AX.X, apply_absolute_value=True,
                                )
                    p.wait("sync", sV, tokv)
            p.wait("tensor", sV, sV.count)
            t1 = p.op("tensor", "matmul", ptiny[0:1, 0:ntile * nsub], ones128[:],
                      racc[:], start=True, stop=True, sem=sT)
            p.wait("vector", sT, t1)
            p.opw("vector", sV, "reduce_sum", amx[0:1, 0:1], ptiny[0:1, 0:ntile * nsub],
                 axis=AX.X)
            p.opw("vector", sV, "tensor_scalar_mul", amx[0:1, 0:1], amx[0:1, 0:1],
                 1.0 / (float(ntile) * 128.0 * ncols))
            p.opw("vector", sV, "tensor_scalar_max", amx[0:1, 0:1], amx[0:1, 0:1], 1e-5)
            tokv = p.opw("vector", sV, "reciprocal", amx[0:1, 1:2], amx[0:1, 0:1])
            p.wait("tensor", sV, tokv)
            t2 = p.op("tensor", "matmul", ptiny[:, 256:258], onescol[:],
                      amx[0:1, 0:2], start=True, stop=True, sem=sT)
            p.wait("vector", sT, t2)
            scol = scolD[:, 2 * _sidx[name]:2 * _sidx[name] + 2]
            tokv = p.opw("vector", sV, "tensor_copy", scol, ptiny[:, 256:258])
            for it in range(ntile):
                for hh in range(nsub):
                    t = ld(wstg[:, 0:cs],
                           w_in[it * 128:(it + 1) * 128, hh * cs:(hh + 1) * cs])
                    p.wait("vector", sLD, t)
                    p.opw("vector", sV, "tensor_single_scalar", wstg[:, 0:cs],
                         wstg[:, 0:cs], scol[:, 1:2], op=ALU.mult)
                    p.opw("vector", sV, "tensor_scalar", wstg[:, 0:cs], wstg[:, 0:cs],
                         RND_C, RND_C, op0=ALU.add, op1=ALU.subtract)
                    p.opw("vector", sV, "tensor_scalar", wstg[:, 0:cs], wstg[:, 0:cs],
                         1.0, -1.0, op0=ALU.min, op1=ALU.max)
                    tokv = p.opw("vector", sV, "tensor_copy", wbf_stage[:, 0:cs],
                                wstg[:, 0:cs])
                    p.wait("gpsimd", sV, tokv)
                    tokw = p.op("gpsimd", "dma_start", sem=sWR, step=16,
                                out=whbm[it][:, hh * cs:(hh + 1) * cs],
                                in_=wbf_stage[:, 0:cs])
                    p.wait("vector", sWR, tokw)
                    p.wait("sync", sV, tokv)
            return scol

        _sidx = {"wg": 0, "wu": 1, "wd": 2}
        scol_g = quant_ffn1(wgT_in, wgq_hbm, KT, INNER, "wg")
        scol_u = quant_ffn1(wuT_in, wuq_hbm, KT, INNER, "wu")
        scol_d = quant_ffn1(wdT_in, wdq_hbm, INNER // 128, H, "wd")

        p.replay(blk)
    es_s0.close()

    if dbg:
        with nc.Block() as blk:
            p = Prog(esems={"vector": sV, "scalar": sS, "gpsimd": sG})
            dsu = dbg_t["d_setup"]
            cvt = sb0("dsu_cvt", (128, 1024))
            specs = [(q0e[:], 0, 512, None), (r21e[:], 512, 512, None),
                     (r22e[:], 1024, 512, None), (rsexp[:], 1536, 1024, None),
                     (rhq[:, 0:1024], 2560, 1024, BF16),
                     (bpw[:, 0:512], 3584, 512, BF16),
                     (cpw[:, 0:512], 4096, 512, BF16),
                     (triuS[:, 0:128], 4608, 128, None),
                     (trilx8[:, 0:128], 4736, 128, None)]
            for ap, off, w, cast in specs:
                if cast is not None:
                    tv = p.op("vector", "tensor_copy", cvt[:, 0:w], ap, sem=sV)
                    p.wait("gpsimd", sV, tv)
                    tw = p.op("gpsimd", "dma_start", sem=sWR, step=16,
                              out=dsu[:, off:off + w], in_=cvt[:, 0:w])
                    p.wait("vector", sWR, tw)
                else:
                    p.op("gpsimd", "dma_start", sem=sWR, step=16,
                         out=dsu[:, off:off + w], in_=ap)
            p.wait("gpsimd", sWR, sWR.count)
            p.replay(blk)

    if "A" in run_phases:
        with nc.Block() as blk:
            p = Prog(esems={"vector": sV, "scalar": sS, "gpsimd": sG})
            tokEvac = {0: 0, 1: 0, "next": 0}
            tokPuFree = 0
            tChain = 0
            tokS0_prev = 0
            tokV0V1_prev = 0
            tokZaEv_prev = 0
            last_x0_use = [0, 0]
            last_v_guard = [0, 0]
            last_xn0_write_guard = [0, 0]
            last_qtl_guard = [0, 0]
            last_xn1_guard = [0, 0]
            last_xn1_dma = [0, 0]
            last_x1_dma = [0, 0]

            def transpose_group(src_bf16, dst_bf16, guard_sem, guard_tok):
                bank = tokEvac["next"]
                tokEvac["next"] ^= 1
                pb = (ptr if bank == 0 else ptr2)[:]
                p.wait("tensor", sS, tokEvac[bank])
                p.wait("tensor", guard_sem, guard_tok)
                last_t = None
                for i in range(8):
                    last_t = p.op("tensor", "transpose",
                                  pb[:, i * 128:(i + 1) * 128],
                                  src_bf16[:, i * 128:(i + 1) * 128], identb[:],
                                  sem=sT if i == 7 else None)
                p.wait("scalar", sT, last_t)
                tokEvac[bank] = p.op("scalar", "copy", dst_bf16[:], pb[:],
                                     sem=sS)
                return tokEvac[bank], last_t

            for n in range(NCH):
                pr = n % 2
                own = OWN0 <= n < OWN0 + NOWN

                p.wait("sync", sV, last_x0_use[pr])
                tLD = p.op("sync", "dma_start", sem=sLD, step=16,
                           out=x0t[pr][:], in_=x0[n * 128:(n + 1) * 128, :])

                p.wait("vector", sLD, tLD)
                p.opw("vector", sV, "reduce_sum", cols[:, 0:1], x0t[pr][:], axis=AX.X)
                p.wait("scalar", sLD, tLD)
                tSq = p.op("scalar", "activation", sqdump[:], x0t[pr][:], AF.Square,
                           accum_out=cols[:, 1:2], sem=sS)
                p.wait("vector", sS, tSq)
                tokN = emit_ln_cols(p, sV, cols, 0, 1, 2, 3, H, 1e-5, sG=sG, cmagic=cmagic)
                p.wait("vector", sV, max(tokN, last_xn0_write_guard[pr]))
                tXn0 = p.op("vector", "scalar_tensor_tensor", xn0[pr][:], x0t[pr][:],
                            cols[:, 2:3], cols[:, 3:4].broadcast_to((128, H)),
                            op0=ALU.mult, op1=ALU.add, sem=sV)

                p.opw("vector", sV, "reduce_max", cols[:, 8:9], xn0[pr][:],
                      axis=AX.X, apply_absolute_value=True)
                p.opw("vector", sV, "tensor_scalar_max", cols[:, 8:9], cols[:, 8:9],
                      1e-12)
                p.opw("vector", sV, "reciprocal", cols[:, 9:10], cols[:, 8:9])
                p.opw("vector", sV, "tensor_scalar_mul", cols[:, 10:11],
                      cols[:, 9:10], 127.0)
                tQs = p.opw("vector", sV, "tensor_scalar_mul", cols[:, 11:12],
                            cols[:, 8:9], 0.5 / 127.0)
                p.wait("vector", sV, tQs)
                tQ1 = p.op("vector", "scalar_tensor_tensor", tsc2[pr][:], xn0[pr][:],
                           cols[:, 10:11], cC[:, 0:1].broadcast_to((128, H)),
                           op0=ALU.mult, op1=ALU.add, sem=sV)
                p.wait("vector", sT, last_qtl_guard[pr])
                p.op("vector", "tensor_scalar", tsc2[pr][:], tsc2[pr][:], RND_C, 127.0,
                     op0=ALU.subtract, op1=ALU.min)
                tQ2 = p.op("vector", "tensor_scalar_max", qtl[pr][:], tsc2[pr][:],
                           -127.0, sem=sV)

                tEx0, tTx0 = transpose_group(xn0[pr], xn0T[pr], sV, tXn0)
                last_xn0_write_guard[pr] = 0
                tEq, tTq = transpose_group(qtl[pr], qtlT[pr], sV, tQ2)

                p.wait("tensor", sV, tokS0_prev)
                p.wait("tensor", sS, tEq)
                for kt in range(KT):
                    p.op("tensor", "matmul", pbig[:, 0:512],
                         qtlT[pr][:, kt * 128:(kt + 1) * 128],
                         rhq[:, kt * H:kt * H + 512],
                         start=(kt == 0), stop=(kt == KT - 1))
                tFfn = None
                for kt in range(KT):
                    last = kt == KT - 1
                    tFfn = p.op("tensor", "matmul", pbig[:, 512:1024],
                                qtlT[pr][:, kt * 128:(kt + 1) * 128],
                                rhq[:, kt * H + 512:(kt + 1) * H],
                                start=(kt == 0), stop=last, sem=sT if last else None)
                p.wait("vector", sT, tFfn)
                tDeq = p.op("vector", "tensor_mul", tsc[pr][:], pbig[:], rsexp[:],
                            sem=sV)
                p.wait("scalar", sV, tDeq)
                tTh = p.op("scalar", "activation", tsc2[pr][:], tsc[pr][:], AF.Tanh,
                           scale=cols[:, 11:12], sem=sS)
                p.wait("vector", sS, tTh)
                p.op("vector", "scalar_tensor_tensor", tsc[pr][:], tsc2[pr][:], 1.0,
                     tsc[pr][:], op0=ALU.add, op1=ALU.mult)
                tG0 = p.op("vector", "tensor_single_scalar", g0t[pr][:], tsc[pr][:],
                           cols[:, 11:12], op=ALU.mult, sem=sV)

                p.wait("tensor", sV, tokPuFree)
                p.wait("tensor", sS, tEx0)
                tBu = None
                for kt in range(KT):
                    last = kt == KT - 1
                    tBu = p.op("tensor", "matmul", pu[:],
                               xn0T[pr][:, kt * 128:(kt + 1) * 128],
                               bpw[:, kt * 512:(kt + 1) * 512],
                               start=(kt == 0), stop=last, sem=sT if last else None)
                p.wait("vector", sT, tBu)
                p.wait("vector", sG, last_v_guard[pr])
                p.op("vector", "tensor_mul", v0t[pr][:], pu[:], q0e[:])
                tokV0V1_prev = p.op("vector", "tensor_mul", v1t[pr][:], pu[:], q1e[:],
                                    sem=sV)
                p.op("vector", "tensor_add", v0t[pr][0:1, :], v0t[pr][0:1, :],
                     grow[0][:])
                tVc = p.op("vector", "tensor_add", v1t[pr][0:1, :], v1t[pr][0:1, :],
                           grow[1][:], sem=sV)
                p.wait("tensor", sV, tVc)
                p.op("tensor", "matmul", ph0[:], tri[:], v0t[pr][:], start=True,
                     stop=True)
                p.op("tensor", "matmul", ph1[:], tri[:], v1t[pr][:], start=True,
                     stop=True)
                p.wait("tensor", sV, tokZaEv_prev)
                p.op("tensor", "matmul", pu[0:1, 0:512], ones128[:], v0t[pr][:],
                     start=True, stop=True)
                tCum = p.op("tensor", "matmul", ptiny[0:1, 0:512], ones128[:],
                            v1t[pr][:], start=True, stop=True, sem=sT)
                p.wait("vector", sT, tCum)
                p.op("vector", "tensor_mul", v0t[pr][:], ph0[:], r21e[:])
                tYab = p.op("vector", "tensor_mul", v1t[pr][:], ph1[:],
                            r22e[:], sem=sV)
                p.wait("gpsimd", sV, tYab)
                tYs = p.op("gpsimd", "tensor_add", yst[pr][:], v0t[pr][:],
                           v1t[pr][:], sem=sG)
                p.op("vector", "tensor_mul", rscr[0][:], pu[0:1, 0:512], trow[0][:])
                tcv = p.op("vector", "tensor_mul", rscr[1][:], ptiny[0:1, 0:512],
                           trow[1][:], sem=sV)
                p.wait("vector", sV, tcv)
                p.op("vector", "tensor_add", grow[0][:], rscr[0][:], rscr[1][:])
                p.op("vector", "tensor_mul", rscr[0][:], pu[0:1, 0:512], trow[2][:])
                tcv = p.op("vector", "tensor_mul", rscr[1][:], ptiny[0:1, 0:512],
                           trow[3][:], sem=sV)
                p.wait("vector", sV, tcv)
                tChain = p.op("vector", "tensor_add", grow[1][:], rscr[0][:],
                              rscr[1][:], sem=sV)

                p.wait("tensor", sS, tokEvac[0])
                p.wait("tensor", sG, tYs)
                tT = None
                pbb = ptr[:, 0:512]
                for j in range(4):
                    tT = p.op("tensor", "transpose", pbb[:, j * 128:(j + 1) * 128],
                              yst[pr][:, j * 128:(j + 1) * 128], identb[:],
                              sem=sT if j == 3 else None)
                p.wait("scalar", sT, tT)
                tokEvac[0] = p.op("scalar", "copy", ysT[pr][:], pbb[:], sem=sS)
                tEys = tokEvac[0]
                p.wait("tensor", sS, tEys)
                p.wait("tensor", sV, tDeq)
                for pt in range(4):
                    p.op("tensor", "matmul", pbig[:, 0:512],
                         ysT[pr][:, pt * 128:(pt + 1) * 128],
                         cpw[:, pt * H:pt * H + 512],
                         start=(pt == 0), stop=(pt == 3))
                tCy = None
                for pt in range(4):
                    last = pt == 3
                    tCy = p.op("tensor", "matmul", pbig[:, 512:1024],
                               ysT[pr][:, pt * 128:(pt + 1) * 128],
                               cpw[:, pt * H + 512:(pt + 1) * H],
                               start=(pt == 0), stop=last, sem=sT if last else None)
                p.wait("vector", sT, tCy)
                p.op("vector", "tensor_mul", tsc2[pr][:], xn0[pr][:], dexp[:])
                tS0 = p.op("vector", "tensor_add", sc0[pr][:], tsc2[pr][:], pbig[:],
                           sem=sV)
                tokS0_prev = tS0

                p.wait("vector", sWR, last_x1_dma[pr])
                p.op("vector", "tensor_mul", tsc[pr][:], sc0[pr][:], g0t[pr][:])
                tX1 = p.op("vector", "tensor_add", x1t[pr][:], tsc[pr][:], x0t[pr][:],
                           sem=sV)
                last_x0_use[pr] = tX1
                p.wait("gpsimd", sV, tX1)
                last_x1_dma[pr] = p.op("gpsimd", "dma_start", sem=sWR, step=16,
                                       out=x1_hbm[n], in_=x1t[pr][:])

                p.opw("vector", sV, "reduce_sum", cols[:, 16:17], x1t[pr][:],
                      axis=AX.X)
                p.wait("scalar", sV, tX1)
                tSq1 = p.op("scalar", "activation", sqdump[:], x1t[pr][:], AF.Square,
                            accum_out=cols[:, 17:18], sem=sS)
                p.wait("vector", sS, tSq1)
                tokN1 = emit_ln_cols(p, sV, cols, 16, 17, 18, 19, H, 1e-5, sG=sG, cmagic=cmagic)
                p.op("vector", "tensor_copy", ln1sv[:, n * 4 + 0:n * 4 + 1],
                     cols[:, 18:19])
                p.op("vector", "tensor_copy", ln1sv[:, n * 4 + 1:n * 4 + 2],
                     cols[:, 19:20])
                p.wait("vector", sT, last_xn1_guard[pr])
                p.wait("vector", sV, tokN1)
                tXn1 = p.op("vector", "scalar_tensor_tensor", xn1[pr][:], x1t[pr][:],
                            cols[:, 18:19], cols[:, 19:20].broadcast_to((128, H)),
                            op0=ALU.mult, op1=ALU.add, sem=sV)
                p.op("vector", "reduce_max", ln1sv[:, n * 4 + 2:n * 4 + 3],
                     xn1[pr][:], axis=AX.X, apply_absolute_value=True)

                p.wait("vector", sWR, last_xn1_dma[pr])
                tExn1, tTxn1 = transpose_group(xn1[pr], xn1T[pr], sV, tXn1)
                p.wait("gpsimd", sS, tExn1)
                last_xn1_dma[pr] = p.op(
                    "gpsimd", "dma_start", sem=sWR, step=16,
                    out=xn1t_hbm[n].rearrange("(k q) t -> q k t", q=128),
                    in_=xn1T[pr][:])

                p.wait("tensor", sV, max(tokZaEv_prev, tChain))
                p.wait("tensor", sS, tExn1)
                tZa = None
                for kt in range(KT):
                    last = kt == KT - 1
                    tZa = p.op("tensor", "matmul", ptiny[:, 0:NH],
                               xn1T[pr][:, kt * 128:(kt + 1) * 128],
                               wab[:, kt * NH:(kt + 1) * NH],
                               start=(kt == 0), stop=last, sem=sT if last else None)
                p.wait("vector", sT, tZa)
                tokZaEv_prev = p.op("vector", "tensor_copy",
                                    zaall[:, n * NH:(n + 1) * NH], ptiny[:, 0:NH],
                                    sem=sV)

                if dbg and n == 0:
                    p.wait("gpsimd", sV, tDeq)
                    p.op("gpsimd", "dma_start", sem=sWR, step=16,
                         out=dbg_t["d_ffn0"][:, 0:H], in_=tsc[pr][:])
                    p.wait("gpsimd", sS, tTh)
                    p.op("gpsimd", "dma_start", sem=sWR, step=16,
                         out=dbg_t["d_ffn0"][:, H:2 * H], in_=tsc2[pr][:])
                    cvt0 = tsc2[pr]
                    p.wait("gpsimd", sV, tQ2)
                    p.op("gpsimd", "dma_start", sem=sWR, step=16,
                         out=dbg_t["d_ffn0"][:, 2 * H:2 * H + 512].bitcast(BF16),
                         in_=qtl[pr][:])
                    p.wait("gpsimd", sV, tEq)
                    p.op("gpsimd", "dma_start", sem=sWR, step=16,
                         out=dbg_t["d_ffn0"][:, 2 * H + 512:3 * H].bitcast(BF16),
                         in_=qtlT[pr][:])
                if dbg:
                    p.wait("gpsimd", sV, tXn0)
                    p.op("gpsimd", "dma_start", sem=sWR, step=16,
                         out=dbg_t["d_xn0"][n * 128:(n + 1) * 128, :], in_=xn0[pr][:])
                    p.wait("gpsimd", sV, tG0)
                    p.op("gpsimd", "dma_start", sem=sWR, step=16,
                         out=dbg_t["d_g0"][n * 128:(n + 1) * 128, :], in_=g0t[pr][:])
                    p.wait("gpsimd", sV, tS0)
                    p.op("gpsimd", "dma_start", sem=sWR, step=16,
                         out=dbg_t["d_s0"][n * 128:(n + 1) * 128, :], in_=sc0[pr][:])
                    p.wait("gpsimd", sV, tX1)
                    p.op("gpsimd", "dma_start", sem=sWR, step=16,
                         out=dbg_t["d_x1"][n * 128:(n + 1) * 128, :], in_=x1t[pr][:])

                if dbg:
                    for eng in ("vector", "scalar", "tensor", "sync", "gpsimd"):
                        p.wait(eng, sWR, sWR.count)
                tokPuFree = tChain
                last_v_guard[pr] = tYs
                last_xn0_write_guard[pr] = tS0
                last_qtl_guard[pr] = tTq
                last_xn1_guard[pr] = tTxn1

            if dbg:
                p.wait("gpsimd", sV, tokZaEv_prev)
                p.op("gpsimd", "dma_start", sem=sWR, step=16,
                     out=dbg_t["d_za"], in_=zaall[:])
                p.op("gpsimd", "dma_start", sem=sWR, step=16,
                     out=dbg_t["d_ln1"], in_=ln1sv[:])
            p.wait("gpsimd", sWR, sWR.count)
            p.replay(blk)

    if "B" in run_phases:
        with nc.Block() as blk:
            p = Prog(esems={"vector": sV, "scalar": sS, "gpsimd": sG})
            la = zaall
            for n in range(NCH):
                p.op("vector", "tensor_add", la[:, n * NH:(n + 1) * NH],
                     la[:, n * NH:(n + 1) * NH], bab_t[:])
            tv = p.op("vector", "memset", cols[0:1, 31:32], 0.0, sem=sV)
            p.wait("scalar", sV, tv)
            W = NCH * NH
            ts1 = p.op("scalar", "activation", sqdump[:, 0:W], la[:, 0:W], AF.Abs,
                       sem=sS)
            p.wait("scalar", sS, ts1)
            ts1 = p.op("scalar", "activation", sqdump[:, 0:W], sqdump[:, 0:W], AF.Exp,
                       scale=-1.0, sem=sS)
            p.wait("vector", sS, ts1)
            tv = p.opw("vector", sV, "tensor_scalar_add", sqdump[:, 0:W],
                       sqdump[:, 0:W], 1.0)
            p.wait("scalar", sV, tv)
            ts1 = p.op("scalar", "activation", sqdump[:, 0:W], sqdump[:, 0:W], AF.Ln,
                       sem=sS)
            p.wait("vector", sS, ts1)
            p.opw("vector", sV, "tensor_scalar_max", la[:, 0:W], la[:, 0:W], 0.0)
            p.opw("vector", sV, "tensor_add", la[:, 0:W], la[:, 0:W], sqdump[:, 0:W])
            tv = p.opw("vector", sV, "tensor_scalar_mul", la[:, 0:W], la[:, 0:W],
                       -1.0)
            p.wait("tensor", sV, tv)
            for half in range((W + 511) // 512):
                w0, w1 = half * 512, min(W, (half + 1) * 512)
                p.op("tensor", "matmul", pbig[:, w0:w1], tri[:], la[:, w0:w1],
                     start=True, stop=True)
            tv = 0
            for half in range((W + 511) // 512):
                w0, w1 = half * 512, min(W, (half + 1) * 512)
                p.wait("tensor", sV, tv)
                tt = p.op("tensor", "matmul", ptiny[0:1, 0:w1 - w0], ones128[:],
                          la[:, w0:w1], start=True, stop=True, sem=sT)
                p.wait("vector", sT, tt)
                tv = p.op("vector", "tensor_copy", sqdump[0:1, w0:w1],
                          ptiny[0:1, 0:w1 - w0], sem=sV)
            totm = sbab("totm", (128, NH))
            offm = sbab("offm", (128, NH))
            p.op("vector", "memset", totm[:], 0.0)
            p.op("vector", "memset", offm[:], 0.0)
            p.wait("gpsimd", sV, sV.count)
            tw = p.op("gpsimd", "dma_start", sem=sWR, step=16,
                      out=rowscr.flatten(), in_=sqdump[0:1, 0:W])
            p.wait("gpsimd", sWR, tw)
            tw = p.op("gpsimd", "dma_start", sem=sWR, step=16,
                      out=totm[0:NCH, :], in_=rowscr)
            p.wait("tensor", sWR, tw)
            tt = p.op("tensor", "matmul", ph0[0:NCH, 0:NH], triuS[0:NCH, 0:NCH],
                      totm[0:NCH, :], start=True, stop=True, sem=sT)
            p.wait("vector", sT, tt)
            tv = p.op("vector", "tensor_copy", offm[0:NCH, :], ph0[0:NCH, 0:NH],
                      sem=sV)
            p.wait("gpsimd", sV, tv)
            tw = p.op("gpsimd", "dma_start", sem=sWR, step=16,
                      out=rowscr, in_=offm[0:NCH, :])
            p.wait("gpsimd", sWR, tw)
            tw = p.op("gpsimd", "dma_start", sem=sWR, step=16,
                      out=sqdump[0:1, 0:W], in_=rowscr.flatten())
            p.wait("tensor", sWR, tw)
            tv = 0
            for half in range((W + 511) // 512):
                w0, w1 = half * 512, min(W, (half + 1) * 512)
                p.wait("tensor", sV, tv)
                tt = p.op("tensor", "matmul", ph1[:, 0:w1 - w0], onescol[:],
                          sqdump[0:1, w0:w1], start=True, stop=True, sem=sT)
                p.wait("vector", sT, tt)
                tv = p.op("vector", "tensor_copy", cumall[:, w0:w1],
                          pbig[:, w0:w1], sem=sV)
                tv = p.op("vector", "tensor_add", cumall[:, w0:w1],
                          cumall[:, w0:w1], ph1[:, 0:w1 - w0], sem=sV)
            cumTst = sbab("cumTst", (NH, 128))
            evt = 0
            evw = 0
            for n in range(NCH):
                p.wait("tensor", sV, max(tv, evt))
                tt = p.op("tensor", "transpose", ph0[0:NH, 0:128],
                          cumall[:, n * NH:(n + 1) * NH], ident[:], sem=sT)
                p.wait("vector", sT, tt)
                p.wait("vector", sWR, evw)
                evt = p.op("vector", "tensor_copy", cumTst[:], ph0[0:NH, 0:128],
                           sem=sV)
                p.wait("gpsimd", sV, evt)
                evw = p.op("gpsimd", "dma_start", sem=sWR, step=16,
                           out=cumt_hbm[n], in_=cumTst[:])
            if dbg:
                p.wait("gpsimd", sV, evt)
                p.op("gpsimd", "dma_start", sem=sWR, step=16, out=dbg_t["d_cum"],
                     in_=cumall[:])
                p.op("gpsimd", "dma_start", sem=sWR, step=16,
                     out=dbg_t["d_gst"][:, 0:NH], in_=totm[:])
                p.op("gpsimd", "dma_start", sem=sWR, step=16,
                     out=dbg_t["d_gst"][:, NH:2 * NH], in_=offm[:])

            p.wait("gpsimd", sWR, sWR.count)
            p.replay(blk)

    es_ab.close()

    es_cd = contextlib.ExitStack()

    def sbc(name, shape, dt=F32):
        return es_cd.enter_context(nc.sbuf_tensor(name, list(shape), dt))

    pdsC = es.enter_context(nc.psum_tensor("pdsC", [128, 512], F32))
    ptrC = es.enter_context(nc.psum_tensor("ptrC", [128, 1024], BF16))

    if "C" in run_phases:
        wq_w = sbc("wq_w", (128, KT * H), BF16)
        wk_w = sbc("wk_w", (128, KT * H), BF16)
        wv_w = sbc("wv_w", (128, KT * H), BF16)
        wstgC = sbc("wstgC", (128, H))
        xs = [sbc(f"xsC{i}", (128, H), BF16) for i in range(2)]
        crow = [sbc(f"crow{i}", (1, NH * 128)) for i in range(2)]
        qs = sbc("qs", (128, H), BF16)
        qsT = sbc("qsT", (128, H), BF16)
        ks = sbc("ks", (128, H), BF16)
        kT = sbc("kT", (128, H), BF16)
        kw = sbc("kw", (128, H), BF16)
        vt = sbc("vt", (128, H), BF16)
        expin = sbc("expin", (128, H))
        etile = sbc("etile", (128, H))
        e2til = sbc("e2til", (128, H))
        sct = sbc("sct", (128, H), BF16)
        obuf = [sbc(f"obuf{i}", (128, H), BF16) for i in range(2)]
        tscC = sbc("tscC", (128, H))
        tscC2 = sbc("tscC2", (128, H))
        sqC = sbc("sqC", (128, H))
        oasm = sbc("oasm", (128, H))
        sst = sbc("sst", (128, 512))
        sstb = sbc("sstb", (128, 512), BF16)
        a16 = sbc("a16", (128, NH))
        w16 = sbc("w16", (128, NH))
        peb = sbc("peb", (128, NH))
        grow16 = sbc("grow16", (1, NH))
        gsum = sbc("gsum", (128, NH))
        gss = sbc("gss", (128, NH))
        colsC = sbc("colsC", (128, 16))

        with nc.Block() as blk:
            p = Prog(esems={"vector": sV, "scalar": sS, "gpsimd": sG})

            def wcast(dst, src_dram):
                for kt in range(KT):
                    t = p.op("sync", "dma_start", sem=sLD, step=16,
                             out=wstgC[:], in_=src_dram[kt * 128:(kt + 1) * 128, :])
                    p.wait("scalar", sLD, t)
                    tc = p.op("scalar", "copy", dst[:, kt * H:(kt + 1) * H],
                              wstgC[:], sem=sS)
                    p.wait("sync", sS, tc)

            wcast(wq_w, Wq_in)
            wcast(wk_w, Wk_in)
            wcast(wv_w, Wv_in)
            p.op("vector", "memset", sst[:], 0.0)
            p.op("vector", "memset", sstb[:], 0.0)
            p.op("vector", "memset", crow[0][:], 0.0)
            p.op("vector", "memset", crow[1][:], 0.0)
            p.op("vector", "memset", gsum[:], 0.0)
            for t_ in (oasm, expin, etile, e2til, tscC, tscC2, sqC, a16, w16,
                       peb, colsC, grow16, xs[0], xs[1], qs, qsT, ks, kT, kw, vt,
                       sct, obuf[0], obuf[1]):
                p.op("vector", "memset", t_[:], 0.0)
            tv0 = p.op("vector", "memset", gss[:], 0.0, sem=sV)
            tokD = {"oin": 0, "oit": 0, "qk": 0, "ds": 0, "gb": 0, "big": tv0,
                    "k2": 0, "tr": 0, "sst": 0, "obuf": [0, 0], "xs": [0, 0],
                    "crow": [tv0, tv0], "ds": 0, "tE": 0, "tE2": 0, "sq": 0}
            tSS = 0
            p.wait("sync", sV, tv0)
            p.wait("tensor", sV, tv0)

            for n in range(NCH):
                pr = n % 2
                p.wait("sync", sT, tokD["xs"][pr])
                tX = p.op("sync", "dma_start", sem=sLD, step=16, out=xs[pr][:],
                          in_=xn1t_hbm[n].rearrange("(k q) t -> q k t", q=128))
                p.wait("sync", sV, tokD["crow"][pr])
                tC = p.op("sync", "dma_start", sem=sLD, step=16,
                          out=crow[pr][:],
                          in_=cumt_hbm[n].flatten())
                p.wait("tensor", sLD, tX)
                p.wait("tensor", sV, tokD["big"])
                for kt in range(KT):
                    p.op("tensor", "matmul", pbig[:, 0:512],
                         xs[pr][:, kt * 128:(kt + 1) * 128],
                         wq_w[:, kt * H:kt * H + 512],
                         start=(kt == 0), stop=(kt == KT - 1))
                tQm = None
                for kt in range(KT):
                    tQm = p.op("tensor", "matmul", pbig[:, 512:1024],
                               xs[pr][:, kt * 128:(kt + 1) * 128],
                               wq_w[:, kt * H + 512:(kt + 1) * H],
                               start=(kt == 0), stop=(kt == KT - 1),
                               sem=sT if kt == KT - 1 else None)
                p.wait("tensor", sV, tokD["oit"])
                for kt in range(KT):
                    p.op("tensor", "matmul", ph0[:],
                         xs[pr][:, kt * 128:(kt + 1) * 128],
                         wk_w[:, kt * H:kt * H + 512],
                         start=(kt == 0), stop=(kt == KT - 1))
                tKm = None
                for kt in range(KT):
                    tKm = p.op("tensor", "matmul", ph1[:],
                               xs[pr][:, kt * 128:(kt + 1) * 128],
                               wk_w[:, kt * H + 512:(kt + 1) * H],
                               start=(kt == 0), stop=(kt == KT - 1),
                               sem=sT if kt == KT - 1 else None)
                p.wait("scalar", sT, tQm)
                p.wait("scalar", sV, tokD["oin"])
                tS1 = p.op("scalar", "activation", tscC[:], pbig[:], AF.Tanh,
                           scale=0.5, sem=sS)
                p.wait("vector", sS, tS1)
                p.wait("vector", sT, tokD["ds"])
                p.op("vector", "scalar_tensor_tensor", tscC2[:], tscC[:], 1.0,
                     pbig[:], op0=ALU.add, op1=ALU.mult)
                tQs = p.op("vector", "tensor_scalar_mul", qs[:], tscC2[:], 0.5,
                           sem=sV)
                p.wait("scalar", sT, tKm)
                p.wait("scalar", sV, tQs)
                tS2 = p.op("scalar", "activation", tscC[:, 0:512], ph0[:], AF.Tanh,
                           scale=0.5, sem=sS)
                p.wait("scalar", sS, tS2)
                tS2 = p.op("scalar", "activation", tscC[:, 512:1024], ph1[:],
                           AF.Tanh, scale=0.5, sem=sS)
                p.wait("vector", sS, tS2)
                p.op("vector", "scalar_tensor_tensor", tscC2[:, 0:512],
                     tscC[:, 0:512], 1.0, ph0[:], op0=ALU.add, op1=ALU.mult)
                p.op("vector", "scalar_tensor_tensor", tscC2[:, 512:1024],
                     tscC[:, 512:1024], 1.0, ph1[:], op0=ALU.add, op1=ALU.mult)
                tKs = p.op("vector", "tensor_scalar_mul", ks[:], tscC2[:], 0.5,
                           sem=sV)
                p.wait("tensor", sV, tQs)
                for kt in range(KT):
                    p.op("tensor", "matmul", pbig[:, 0:512],
                         xs[pr][:, kt * 128:(kt + 1) * 128],
                         wv_w[:, kt * H:kt * H + 512],
                         start=(kt == 0), stop=(kt == KT - 1))
                tVm = None
                for kt in range(KT):
                    tVm = p.op("tensor", "matmul", pbig[:, 512:1024],
                               xs[pr][:, kt * 128:(kt + 1) * 128],
                               wv_w[:, kt * H + 512:(kt + 1) * H],
                               start=(kt == 0), stop=(kt == KT - 1),
                               sem=sT if kt == KT - 1 else None)
                p.wait("vector", sT, tVm)
                tVs = p.op("vector", "tensor_copy", vt[:], pbig[:], sem=sV)

                p.wait("sync", sLD, tC)
                p.wait("tensor", sLD, tC)
                p.wait("tensor", sV, tSS)
                prv = crow[1 - pr]
                tt = p.op("tensor", "matmul", pdsC[:, 0:NH], onescol[:],
                          prv[0:1, 127::128], start=True, stop=True)
                tt = p.op("tensor", "matmul", pdsC[:, NH:2 * NH], onescol[:],
                          crow[pr][0:1, 127::128], start=True, stop=True, sem=sT)
                p.wait("vector", sT, tt)
                p.op("vector", "tensor_sub", peb[:], cumall[:, n * NH:(n + 1) * NH],
                     pdsC[:, 0:NH])
                tv = p.op("vector", "tensor_sub", w16[:], pdsC[:, NH:2 * NH],
                          cumall[:, n * NH:(n + 1) * NH], sem=sV)
                p.wait("scalar", sV, tv)
                p.op("scalar", "activation", a16[:], peb[:], AF.Exp)
                tA = p.op("scalar", "activation", w16[:], w16[:], AF.Exp, sem=sS)
                p.wait("vector", sS, tA)
                p.op("vector", "tensor_sub", grow16[:], crow[pr][0:1, 127::128],
                     prv[0:1, 127::128])
                tv = p.op("vector", "memset", colsC[0:1, 15:16], 0.0, sem=sV)
                p.wait("scalar", sV, tv)
                tGr = p.op("scalar", "activation", grow16[:], grow16[:], AF.Exp,
                           sem=sS)
                p.wait("vector", sS, tA)
                tKw = p.op("vector", "tensor_tensor", kw[:], ks[:],
                           w16[:].rearrange("p (h o) -> p h o", o=1)
                           .broadcast_to((128, NH, DH)), op=ALU.mult, sem=sV)

                p.wait("tensor", sV, max(tokD["tr"], tQs))
                tt = None
                for i in range(8):
                    tt = p.op("tensor", "transpose", ptrC[:, i * 128:(i + 1) * 128],
                              qs[:, i * 128:(i + 1) * 128], identb[:],
                              sem=sT if i == 7 else None)
                p.wait("vector", sT, tt)
                tTq = p.op("vector", "tensor_copy", qsT[:], ptrC[:], sem=sV)
                p.wait("tensor", sV, max(tTq, tKs))
                for i in range(8):
                    tt = p.op("tensor", "transpose", ptrC[:, i * 128:(i + 1) * 128],
                              ks[:, i * 128:(i + 1) * 128], identb[:],
                              sem=sT if i == 7 else None)
                p.wait("vector", sT, tt)
                tTk = p.op("vector", "tensor_copy", kT[:], ptrC[:], sem=sV)
                tokD["tr"] = tTk

                tOe = [None, None]
                for g in range(2):
                    p.wait("tensor", sV, max(tVs, tokD["qk"]))
                    tt = p.op("tensor", "matmul", pbig[:, 0:512], onescol[:],
                              crow[pr][0:1, g * 1024:g * 1024 + 512],
                              start=True, stop=True)
                    tt = p.op("tensor", "matmul", pbig[:, 512:1024], onescol[:],
                              crow[pr][0:1, g * 1024 + 512:(g + 1) * 1024],
                              start=True, stop=True, sem=sT)
                    p.wait("vector", sT, tt)
                    p.wait("vector", sS, tokD["tE"])
                    h0 = g * 8
                    p.op("vector", "tensor_sub", expin[:], pbig[:],
                         cumall[:, n * NH + h0:n * NH + h0 + 8]
                         .rearrange("p (h o) -> p h o", o=1).broadcast_to((128, 8, 128)))
                    tv = p.op("vector", "tensor_scalar_min", expin[:], expin[:],
                              0.0, sem=sV)
                    p.wait("scalar", sV, tv)
                    p.wait("scalar", sG, tokD["tE2"])
                    tE = p.op("scalar", "activation", etile[:], expin[:], AF.Exp,
                              sem=sS)
                    tokD["tE"] = tE
                    p.wait("gpsimd", sS, tE)
                    p.wait("gpsimd", sV, tokD["qk"])
                    tE2 = p.op("gpsimd", "tensor_tensor", e2til[:], etile[:],
                               trilx8[:], op=ALU.mult, sem=sG)
                    tokD["tE2"] = tE2
                    p.wait("tensor", sV, tv)
                    tt = None
                    for hh in range(8):
                        h = h0 + hh
                        base = (h % 2) * 64
                        c0 = (hh % 2) * 512 + (hh // 2) * 128
                        tt = p.op("tensor", "matmul",
                                  pbig[:, c0:c0 + 128],
                                  kT[base:base + 64,
                                     (h // 2) * 128:(h // 2) * 128 + 128],
                                  qsT[base:base + 64,
                                      (h // 2) * 128:(h // 2) * 128 + 128],
                                  start=True, stop=True,
                                  sem=sT if hh == 7 else None)
                    p.wait("vector", sT, tt)
                    p.wait("vector", sG, tE2)
                    tSc = p.op("vector", "tensor_mul",
                               sct[:].rearrange("p (h2 par i) -> p h2 par i",
                                                par=2, i=128),
                               e2til[:].rearrange("p (h2 par i) -> p h2 par i",
                                                  par=2, i=128),
                               pbig[:].rearrange("p (par h2 i) -> p h2 par i",
                                                 par=2, i=128),
                               sem=sV)
                    p.wait("tensor", sV, tSc)
                    if g == 0:
                        p.wait("tensor", sV, tokD["oin"])
                    for hh in range(8):
                        h = h0 + hh
                        phb = ph0 if h % 2 == 0 else ph1
                        c0 = (h // 2) * 64
                        p.op("tensor", "matmul", phb[:, c0:c0 + 64],
                             sct[:, hh * 128:(hh + 1) * 128],
                             vt[:, h * 64:(h + 1) * 64],
                             start=True, stop=True)
                    for hh in range(8):
                        h = h0 + hh
                        pib = pu if h % 2 == 0 else ptiny
                        c0 = (h // 2) * 64
                        base = (h % 2) * 64
                        tOe[g] = p.op("tensor", "matmul", pib[:, c0:c0 + 64],
                                      qsT[base:base + 64,
                                          (h // 2) * 128:(h // 2) * 128 + 128],
                                      sstb[base:base + 64,
                                           (h // 2) * 64:(h // 2) * 64 + 64],
                                      start=True, stop=True,
                                      sem=sT if hh == 7 else None)
                    p.wait("tensor", sV, max(tKw, tokD["sst"]))
                    for hh in range(8):
                        h = h0 + hh
                        base = (h % 2) * 64
                        tDs = p.op("tensor", "matmul",
                                   pdsC[base:base + 64,
                                        (h // 2) * 64:(h // 2) * 64 + 64],
                                   kw[:, h * 64:(h + 1) * 64],
                                   vt[:, h * 64:(h + 1) * 64],
                                   start=True, stop=True,
                                   sem=sT if hh == 7 else None)
                    tokD["qk"] = tSc
                tokD["ds"] = tDs
                p.wait("vector", sT, tOe[1])
                p.wait("vector", sS, tokD["sq"])
                p.wait("vector", sWR, tokD["obuf"][pr])
                p.op("vector", "tensor_mul", tscC[:, 0:512], pu[:],
                     a16[:, 0::2].rearrange("p (h o) -> p h o", o=1)
                     .broadcast_to((128, 8, DH)))
                tv = p.op("vector", "tensor_mul", tscC[:, 512:1024], ptiny[:],
                          a16[:, 1::2].rearrange("p (h o) -> p h o", o=1)
                          .broadcast_to((128, 8, DH)), sem=sV)
                oaV = oasm[:].rearrange("p (h2 x) -> p h2 x", x=128)
                p.op("vector", "tensor_add", oaV[:, :, 0:DH],
                     tscC[:, 0:512].rearrange("p (h2 d) -> p h2 d", d=DH),
                     ph0[:].rearrange("p (h2 d) -> p h2 d", d=DH))
                tOo = p.op("vector", "tensor_add", oaV[:, :, DH:128],
                           tscC[:, 512:1024].rearrange("p (h2 d) -> p h2 d",
                                                       d=DH),
                           ph1[:].rearrange("p (h2 d) -> p h2 d", d=DH),
                           sem=sV)
                tokD["oit"] = tOo
                tokD["oin"] = tOo
                tOb = p.op("vector", "tensor_copy", obuf[pr][:], oasm[:], sem=sV)
                p.wait("gpsimd", sV, tOb)
                tokD["obuf"][pr] = p.op("gpsimd", "dma_start", sem=sWR, step=16,
                                        out=o_hbm[n], in_=obuf[pr][:])
                if dbg:
                    tokD["obuf"][pr] = p.op("gpsimd", "dma_start", sem=sWR, step=16,
                                            out=dbg_t["d_o"][n * 128:(n + 1) * 128, :],
                                            in_=obuf[pr][:])
                if dbg and n == 0:
                    for src_t, off, w, bc in ((qs, 0, 512, True), (ks, 512, 512, True),
                                              (vt, 1024, 512, True),
                                              (sct, 1536, 512, True),
                                              (a16, 2048, 16, False),
                                              (w16, 2064, 16, False),
                                              (peb, 2080, 16, False),
                                              (etile, 2560, 1024, False)):
                        p.wait("gpsimd", sV, sV.count)
                        if bc:
                            p.op("gpsimd", "dma_start", sem=sWR, step=16,
                                 out=dbg_t["d_ffn0"][:, off:off + w].bitcast(BF16),
                                 in_=src_t[:])
                        else:
                            p.op("gpsimd", "dma_start", sem=sWR, step=16,
                                 out=dbg_t["d_ffn0"][:, off:off + w],
                                 in_=src_t[:, 0:w])
                    for e in ("vector", "scalar", "tensor", "sync"):
                        p.wait(e, sWR, sWR.count)
                p.op("vector", "tensor_reduce", colsC[:, 0:NH],
                     oasm[:].rearrange("p (h d) -> p h d", d=DH), axis=AX.X,
                     op=ALU.add)
                p.opw("vector", sV, "tensor_add", gsum[:], gsum[:], colsC[:, 0:NH])
                p.wait("scalar", sV, tOo)
                tSq = p.op("scalar", "activation", sqC[:], oasm[:], AF.Square,
                           sem=sS)
                tokD["sq"] = tSq
                p.wait("vector", sS, tSq)
                p.op("vector", "tensor_reduce", colsC[:, 0:NH],
                     sqC[:].rearrange("p (h d) -> p h d", d=DH), axis=AX.X,
                     op=ALU.add)
                tv = p.opw("vector", sV, "tensor_add", gss[:], gss[:],
                           colsC[:, 0:NH])
                p.wait("tensor", sV, max(tv, tokD["qk"]))
                p.wait("tensor", sS, tGr)
                tt = p.op("tensor", "matmul", pbig[0:64, 0:512], onescol[0:1, 0:64],
                          grow16[0:1, 0::2].rearrange("q (h o) -> q h o", o=1)
                          .broadcast_to((1, 8, 64)), start=True, stop=True)
                tt = p.op("tensor", "matmul", pbig[64:128, 0:512],
                          onescol[0:1, 0:64],
                          grow16[0:1, 1::2].rearrange("q (h o) -> q h o", o=1)
                          .broadcast_to((1, 8, 64)), start=True, stop=True, sem=sT)
                p.wait("vector", sT, max(tt, tDs))
                p.op("vector", "tensor_mul", tscC[:, 0:512], sst[:], pbig[:, 0:512])
                p.op("vector", "tensor_add", sst[:], tscC[:, 0:512], pdsC[:])
                tSS = p.op("vector", "tensor_copy", sstb[:], sst[:], sem=sV)
                tokD["sst"] = tSS
                tokD["big"] = tSS
                tokD["crow"][0] = tSS
                tokD["crow"][1] = tSS
                tokD["xs"][pr] = tVm

            p.wait("vector", sV, tSS)
            p.op("vector", "tensor_copy", gstat[:, 0:NH], gsum[:])
            tv = p.op("vector", "tensor_copy", gstat[:, NH:2 * NH], gss[:], sem=sV)
            if dbg:
                p.wait("gpsimd", sV, tv)
                p.op("gpsimd", "dma_start", sem=sWR, step=16,
                     out=dbg_t["d_gst"], in_=gstat[:])
            p.wait("gpsimd", sWR, sWR.count)
            p.wait("vector", sV, sV.count)
            p.replay(blk)

    es_cd.close()

    if "D" in run_phases:
        es_d = contextlib.ExitStack()
        es_d1 = contextlib.ExitStack()
        es_d2 = contextlib.ExitStack()

        def sbd(name, shape, dt=F32):
            return es_d.enter_context(nc.sbuf_tensor(name, list(shape), dt))

        def sbd1(name, shape, dt=F32):
            return es_d1.enter_context(nc.sbuf_tensor(name, list(shape), dt))

        SCLb = sbd("SCLb", (128, H))
        BIAb = sbd("BIAb", (128, H))
        colD = sbd("colD", (1, 160))
        dqcol = sbd("dqcol", (128, 3 * NOWN))
        xa = sbd("xaD", (128, H))
        xsel = sbd("xselD", (128, H))
        xnb = sbd("xnbD", (128, H), BF16)
        tscD = sbd("tscD", (128, H))
        tscD2 = sbd("tscD2", (128, H))
        xnT = sbd("xnTD", (128, H), BF16)
        gub = sbd("gubD", (128, H), BF16)
        gub2 = sbd("gubD2", (128, H), BF16)
        gu_l = [sbd(f"gu_l{i}", (128, INNER), BF16) for i in range(2)]
        guqT = sbd("guqT", (128, INNER), BF16)
        g1t = sbd("g1tD", (128, H))
        ogb = sbd("ogbD", (128, H), BF16)
        ogT = sbd("ogTD", (128, H), BF16)
        outt = [sbd(f"outtD{i}", (128, H)) for i in range(2)]
        rowD = sbd1("rowD", (1, 2 * H))
        gnr = sbd1("gnrD", (1, 2 * H))
        qb = sbd1("qbD", (128, H), BF16)
        qbT = sbd1("qbTD", (128, H), BF16)
        wg_sl = sbd1("wg_sl", (128, KT * H), BF16)
        wu_sl = sbd1("wu_sl", (128, KT * H), BF16)
        xq_l = [sbd1(f"xq_l{i}", (128, H), BF16) for i in range(2)]

        DQG, DQU, AMG = 0, NOWN, 2 * NOWN

        with nc.Block() as blk:
            p = Prog(esems={"vector": sV, "scalar": sS, "gpsimd": sG})
            tZM = p.op("sync", "dma_start", sem=sLD, step=16, out=zmc[:],
                       in_=zmcols)
            p.op("sync", "dma_start", sem=sLD, step=16, out=gnr[:],
                 in_=gn_row)
            p.wait("vector", sLD, sLD.count)
            p.wait("tensor", sV, sV.count)
            tt = p.op("tensor", "matmul", pdsC[0:1, 0:2 * NH], ones128[:],
                      gstat[:], start=True, stop=True, sem=sT)
            p.wait("vector", sT, tt)
            CNT = 1.0 / (float(L) * DH)
            p.opw("vector", sV, "tensor_scalar_mul", colD[0:1, 0:16],
                  pdsC[0:1, 0:NH], CNT)
            p.opw("vector", sV, "tensor_scalar_mul", colD[0:1, 32:48],
                  pdsC[0:1, NH:2 * NH], CNT)
            p.opw("vector", sV, "tensor_mul", colD[0:1, 48:64], colD[0:1, 0:16],
                  colD[0:1, 0:16])
            p.opw("vector", sV, "tensor_sub", colD[0:1, 32:48], colD[0:1, 32:48],
                  colD[0:1, 48:64])
            tv = p.opw("vector", sV, "tensor_scalar_add", colD[0:1, 32:48],
                       colD[0:1, 32:48], 1e-5)
            p.opw("vector", sV, "tensor_single_scalar",
                  colD[0:1, 64:80].bitcast(I32), colD[0:1, 32:48].bitcast(I32), 1,
                  op=ALU.logical_shift_right)
            p.wait("gpsimd", sV, sV.count)
            tg = p.op("gpsimd", "tensor_tensor", colD[0:1, 80:96].bitcast(I32),
                      cmagic[0:1, 0:1].broadcast_to((1, NH)).bitcast(I32),
                      colD[0:1, 64:80].bitcast(I32), op=ALU.subtract, sem=sG)
            p.wait("vector", sG, tg)
            p.opw("vector", sV, "tensor_scalar_mul", colD[0:1, 96:112],
                  colD[0:1, 32:48], -0.5)
            for _ in range(3):
                p.opw("vector", sV, "tensor_mul", colD[0:1, 112:128],
                      colD[0:1, 80:96], colD[0:1, 80:96])
                p.opw("vector", sV, "tensor_mul", colD[0:1, 112:128],
                      colD[0:1, 112:128], colD[0:1, 96:112])
                p.opw("vector", sV, "tensor_scalar_add", colD[0:1, 112:128],
                      colD[0:1, 112:128], 1.5)
                p.opw("vector", sV, "tensor_mul", colD[0:1, 80:96],
                      colD[0:1, 80:96], colD[0:1, 112:128])
            p.opw("vector", sV, "tensor_mul",
                  rowD[0:1, 0:H].rearrange("q (h d) -> q h d", d=DH),
                  gnr[0:1, 0:H].rearrange("q (h d) -> q h d", d=DH),
                  colD[0:1, 80:96].rearrange("q (h o) -> q h o", o=1)
                  .broadcast_to((1, NH, DH)))
            p.opw("vector", sV, "tensor_mul", colD[0:1, 128:144], colD[0:1, 0:16],
                  colD[0:1, 80:96])
            p.opw("vector", sV, "tensor_scalar_mul", colD[0:1, 128:144],
                  colD[0:1, 128:144], -1.0)
            p.opw("vector", sV, "tensor_mul",
                  rowD[0:1, H:2 * H].rearrange("q (h d) -> q h d", d=DH),
                  gnr[0:1, 0:H].rearrange("q (h d) -> q h d", d=DH),
                  colD[0:1, 128:144].rearrange("q (h o) -> q h o", o=1)
                  .broadcast_to((1, NH, DH)))
            tv = p.opw("vector", sV, "tensor_add", rowD[0:1, H:2 * H],
                       rowD[0:1, H:2 * H], gnr[0:1, H:2 * H])
            p.wait("tensor", sV, tv)
            for hh in range(2):
                tt = p.op("tensor", "matmul", pbig[:, 0:512], onescol[:],
                          rowD[0:1, hh * 512:(hh + 1) * 512], start=True,
                          stop=True)
                tt = p.op("tensor", "matmul", pbig[:, 512:1024], onescol[:],
                          rowD[0:1, H + hh * 512:H + (hh + 1) * 512],
                          start=True, stop=True, sem=sT)
                p.wait("vector", sT, tt)
                p.op("vector", "tensor_copy", SCLb[:, hh * 512:(hh + 1) * 512],
                     pbig[:, 0:512])
                tv = p.op("vector", "tensor_copy",
                          BIAb[:, hh * 512:(hh + 1) * 512], pbig[:, 512:1024],
                          sem=sV)
                p.wait("tensor", sV, tv)
            p.op("vector", "memset", dqcol[:], 0.0)
            p.wait("vector", sLD, tZM)

            tokTRd = 0
            tokXselW = 0
            tokXaRd = 0
            tokQT = {0: 0, 1: 0}
            for j in range(NOWN):
                p.wait("sync", sV, tokXaRd)
                t = p.op("sync", "dma_start", sem=sLD, step=16, out=xa[:],
                         in_=x1_hbm[j])
                p.wait("vector", sLD, t)
                p.wait("vector", sWR, tokXselW)
                tok1 = p.op("vector", "tensor_single_scalar", xsel[:], xa[:],
                            zmc[:, 0:1], op=ALU.mult, sem=sV)
                p.wait("sync", sV, tok1)
                t = p.op("sync", "dma_start", sem=sLD, step=16, out=xa[:],
                         in_=x1_hbm[j + NOWN])
                p.wait("vector", sLD, t)
                tXs = p.op("vector", "scalar_tensor_tensor", xsel[:], xa[:],
                           zmc[:, 1:2], xsel[:], op0=ALU.mult, op1=ALU.add,
                           sem=sV)
                tokXaRd = tXs
                p.wait("gpsimd", sV, tXs)
                tokXselW = p.op("gpsimd", "dma_start", sem=sWR, step=16,
                                out=x1p_hbm[j], in_=xsel[:])
                p.opw("vector", sV, "reduce_sum", cols[:, 0:1], xsel[:],
                      axis=AX.X)
                p.wait("scalar", sV, tXs)
                tSq = p.op("scalar", "activation", tscD[:], xsel[:], AF.Square,
                           accum_out=cols[:, 1:2], sem=sS)
                p.wait("vector", sS, tSq)
                emit_ln_cols(p, sV, cols, 0, 1, 2, 3, H, 1e-5, sG=sG,
                             cmagic=cmagic)
                tXn = p.op("vector", "scalar_tensor_tensor", xnb[:], xsel[:],
                           cols[:, 2:3], cols[:, 3:4].broadcast_to((128, H)),
                           op0=ALU.mult, op1=ALU.add, sem=sV)
                p.opw("vector", sV, "reduce_max", cols[:, 8:9], xnb[:],
                      axis=AX.X, apply_absolute_value=True)
                p.opw("vector", sV, "tensor_scalar_max", cols[:, 8:9],
                      cols[:, 8:9], 1e-5)
                p.opw("vector", sV, "reciprocal", cols[:, 9:10], cols[:, 8:9])
                p.opw("vector", sV, "tensor_scalar_mul", cols[:, 10:11],
                      cols[:, 9:10], 127.0)
                p.opw("vector", sV, "tensor_single_scalar",
                      dqcol[:, DQG + j:DQG + j + 1], cols[:, 8:9],
                      scolD[:, 0:1], op=ALU.mult)
                p.opw("vector", sV, "tensor_scalar_mul",
                      dqcol[:, DQG + j:DQG + j + 1],
                      dqcol[:, DQG + j:DQG + j + 1], 1.0 / 254.0)
                p.opw("vector", sV, "tensor_single_scalar",
                      dqcol[:, DQU + j:DQU + j + 1], cols[:, 8:9],
                      scolD[:, 2:3], op=ALU.mult)
                p.opw("vector", sV, "tensor_scalar_mul",
                      dqcol[:, DQU + j:DQU + j + 1],
                      dqcol[:, DQU + j:DQU + j + 1], 1.0 / 127.0)
                p.op("vector", "scalar_tensor_tensor", tscD2[:], xnb[:],
                     cols[:, 10:11], cC[:, 0:1].broadcast_to((128, H)),
                     op0=ALU.mult, op1=ALU.add)
                p.op("vector", "tensor_scalar", tscD2[:], tscD2[:], RND_C,
                     127.0, op0=ALU.subtract, op1=ALU.min)
                tQ = p.op("vector", "tensor_scalar_max", qb[:], tscD2[:],
                          -128.0, sem=sV)
                for (srcb, dstb, tok) in ((xnb, xnT, tXn), (qb, qbT, tQ)):
                    p.wait("tensor", sS, tokTRd)
                    p.wait("tensor", sV, tok)
                    lt = None
                    for i in range(8):
                        lt = p.op("tensor", "transpose",
                                  ptrC[:, i * 128:(i + 1) * 128],
                                  srcb[:, i * 128:(i + 1) * 128], identb[:],
                                  sem=sT if i == 7 else None)
                    p.wait("scalar", sT, lt)
                    tokTRd = p.op("scalar", "copy", dstb[:], ptrC[:], sem=sS)
                p.wait("gpsimd", sS, tokTRd)
                tokQT[0] = p.op(
                    "gpsimd", "dma_start", sem=sWR, step=16,
                    out=xn1tp_hbm[j].rearrange("(k q) t -> q k t", q=128),
                    in_=xnT[:])
                tokQT[0] = p.op(
                    "gpsimd", "dma_start", sem=sWR, step=16,
                    out=xqt_hbm[j].rearrange("(k q) t -> q k t", q=128),
                    in_=qbT[:])
                p.wait("scalar", sWR, tokQT[0])

            p.wait("sync", sWR, sWR.count)
            tokGuW = {0: 0, 1: 0}
            tokGuRd = 0
            tokXq = {0: 0, 1: 0}
            tokMM = 0
            for s in range(4):
                tWS = 0
                p.wait("sync", sT, tokMM)
                for kt in range(KT):
                    tWS = p.op("sync", "dma_start", sem=sLD, step=16,
                               out=wg_sl[:, kt * H:(kt + 1) * H],
                               in_=wgq_hbm[kt][:, s * H:(s + 1) * H])
                    tWS = p.op("sync", "dma_start", sem=sLD, step=16,
                               out=wu_sl[:, kt * H:(kt + 1) * H],
                               in_=wuq_hbm[kt][:, s * H:(s + 1) * H])
                for j in range(NOWN):
                    pr = j % 2
                    p.wait("sync", sT, tokXq[pr])
                    tX = p.op("sync", "dma_start", sem=sLD, step=16,
                              out=xq_l[pr][:],
                              in_=xqt_hbm[j].rearrange("(k q) t -> q k t",
                                                       q=128))
                    p.wait("tensor", sLD, max(tX, tWS))
                    p.wait("tensor", sV, tokGuRd)
                    for kt in range(KT):
                        p.op("tensor", "matmul", pbig[:, 0:512],
                             xq_l[pr][:, kt * 128:(kt + 1) * 128],
                             wg_sl[:, kt * H:kt * H + 512],
                             start=(kt == 0), stop=(kt == KT - 1))
                    for kt in range(KT):
                        p.op("tensor", "matmul", pbig[:, 512:1024],
                             xq_l[pr][:, kt * 128:(kt + 1) * 128],
                             wg_sl[:, kt * H + 512:(kt + 1) * H],
                             start=(kt == 0), stop=(kt == KT - 1))
                    for kt in range(KT):
                        p.op("tensor", "matmul", pu[:],
                             xq_l[pr][:, kt * 128:(kt + 1) * 128],
                             wu_sl[:, kt * H:kt * H + 512],
                             start=(kt == 0), stop=(kt == KT - 1))
                    for kt in range(KT):
                        tokMM = p.op("tensor", "matmul", ptiny[:],
                                     xq_l[pr][:, kt * 128:(kt + 1) * 128],
                                     wu_sl[:, kt * H + 512:(kt + 1) * H],
                                     start=(kt == 0), stop=(kt == KT - 1),
                                     sem=sT if kt == KT - 1 else None)
                    tokXq[pr] = tokMM
                    gubj = gub if pr == 0 else gub2
                    p.wait("scalar", sT, tokMM)
                    p.wait("vector", sWR, tokGuW[pr])
                    tS1 = p.op("scalar", "activation", tscD[:], pbig[:],
                               AF.Tanh, scale=dqcol[:, DQG + j:DQG + j + 1],
                               sem=sS)
                    p.wait("vector", sS, tS1)
                    p.op("vector", "tensor_scalar_add", tscD[:], tscD[:], 1.0)
                    p.op("vector", "scalar_tensor_tensor", tscD2[:], pbig[:],
                         dqcol[:, DQG + j:DQG + j + 1], tscD[:], op0=ALU.mult,
                         op1=ALU.mult)
                    p.op("vector", "scalar_tensor_tensor",
                         gubj[:, 0:512], pu[:], dqcol[:, DQU + j:DQU + j + 1],
                         tscD2[:, 0:512], op0=ALU.mult, op1=ALU.mult)
                    tGu = p.op("vector", "scalar_tensor_tensor",
                               gubj[:, 512:1024], ptiny[:],
                               dqcol[:, DQU + j:DQU + j + 1],
                               tscD2[:, 512:1024], op0=ALU.mult, op1=ALU.mult,
                               sem=sV)
                    tokGuRd = tGu
                    p.opw("vector", sV, "reduce_max", cols[:, 12:13], gubj[:],
                          axis=AX.X, apply_absolute_value=True)
                    p.opw("vector", sV, "tensor_max",
                          dqcol[:, AMG + j:AMG + j + 1],
                          dqcol[:, AMG + j:AMG + j + 1], cols[:, 12:13])
                    p.wait("gpsimd", sV, tGu)
                    tokGuW[pr] = p.op("gpsimd", "dma_start", sem=sWR, step=16,
                                      out=gu_hbm[j][:, s * H:(s + 1) * H],
                                      in_=gubj[:])

            for j in range(NOWN):
                p.opw("vector", sV, "tensor_scalar_max",
                      dqcol[:, AMG + j:AMG + j + 1],
                      dqcol[:, AMG + j:AMG + j + 1], 1e-5)
                p.opw("vector", sV, "tensor_single_scalar",
                      dqcol[:, AMG + j:AMG + j + 1],
                      dqcol[:, AMG + j:AMG + j + 1], scolD[:, 4:5],
                      op=ALU.mult)
                p.opw("vector", sV, "tensor_scalar_mul",
                      dqcol[:, AMG + j:AMG + j + 1],
                      dqcol[:, AMG + j:AMG + j + 1], 1.0 / 254.0)

            es_d1.close()
            wd_w = es_d2.enter_context(
                nc.sbuf_tensor("wd_w", [128, 32 * H], BF16))
            wg_w = es_d2.enter_context(
                nc.sbuf_tensor("wg_wD", [128, KT * H], BF16))
            wo_w = es_d2.enter_context(
                nc.sbuf_tensor("wo_wD", [128, KT * H], BF16))
            p.wait("sync", sV, sV.count)
            p.wait("sync", sWR, sWR.count)
            tWD = 0
            for kt in range(32):
                tWD = p.op("sync", "dma_start", sem=sLD, step=16,
                           out=wd_w[:, kt * H:(kt + 1) * H], in_=wdq_hbm[kt])
            for kt in range(KT):
                t = p.op("sync", "dma_start", sem=sLD, step=16,
                         out=tscD[:], in_=Wg_in[kt * 128:(kt + 1) * 128, :])
                p.wait("vector", sLD, t)
                tv = p.op("vector", "tensor_copy", wg_w[:, kt * H:(kt + 1) * H],
                          tscD[:], sem=sV)
                p.wait("sync", sV, tv)
                t = p.op("sync", "dma_start", sem=sLD, step=16,
                         out=tscD[:], in_=Wo_in[kt * 128:(kt + 1) * 128, :])
                p.wait("vector", sLD, t)
                tv = p.op("vector", "tensor_copy", wo_w[:, kt * H:(kt + 1) * H],
                          tscD[:], sem=sV)
                p.wait("sync", sV, tv)

            tokOut = {0: 0, 1: 0}
            tokGuL = {0: 0, 1: 0}
            tokTRd = 0
            tokQrd = 0
            tokG1 = 0
            tokOg = 0
            tokGp = 0
            tokSg = 0
            tokOselRd = 0
            p.wait("tensor", sLD, sLD.count)
            p.wait("tensor", sS, sS.count)
            p.wait("vector", sWR, sWR.count)
            for j in range(NOWN):
                pr = j % 2
                p.wait("sync", sV, tokGuL[pr])
                tGu = p.op("sync", "dma_start", sem=sLD, step=16,
                           out=gu_l[pr][:], in_=gu_hbm[j])
                p.opw("vector", sV, "tensor_single_scalar", cols[:, 15:16],
                      dqcol[:, AMG + j:AMG + j + 1], scolD[:, 5:6],
                      op=ALU.mult)
                p.opw("vector", sV, "reciprocal", cols[:, 15:16],
                      cols[:, 15:16])
                p.opw("vector", sV, "tensor_scalar_mul", cols[:, 15:16],
                      cols[:, 15:16], 0.5)
                p.wait("vector", sLD, tGu)
                for q4 in range(4):
                    sl = slice(q4 * H, (q4 + 1) * H)
                    p.wait("vector", sT, tokQrd if q4 == 0 else 0)
                    p.op("vector", "scalar_tensor_tensor", tscD2[:],
                         gu_l[pr][:, sl], cols[:, 15:16],
                         cC[:, 0:1].broadcast_to((128, H)), op0=ALU.mult,
                         op1=ALU.add)
                    p.op("vector", "tensor_scalar", tscD2[:], tscD2[:], RND_C,
                         127.0, op0=ALU.subtract, op1=ALU.min)
                    tQ = p.op("vector", "tensor_scalar_max", gub[:], tscD2[:],
                              -128.0, sem=sV)
                    p.wait("tensor", sV, max(tokTRd, tQ))
                    lt = None
                    for i in range(8):
                        lt = p.op("tensor", "transpose",
                                  ptrC[:, i * 128:(i + 1) * 128],
                                  gub[:, i * 128:(i + 1) * 128], identb[:],
                                  sem=sT if i == 7 else None)
                    p.wait("vector", sT, lt)
                    tokTRd = p.op("vector", "tensor_copy", guqT[:, sl],
                                  ptrC[:], sem=sV)
                tokGuL[pr] = tokTRd
                p.wait("tensor", sV, max(tokTRd, tokG1))
                for kt in range(32):
                    p.op("tensor", "matmul", pbig[:, 0:512],
                         guqT[:, kt * 128:(kt + 1) * 128],
                         wd_w[:, kt * H:kt * H + 512],
                         start=(kt == 0), stop=(kt == 31))
                tDn = None
                for kt in range(32):
                    tDn = p.op("tensor", "matmul", pbig[:, 512:1024],
                               guqT[:, kt * 128:(kt + 1) * 128],
                               wd_w[:, kt * H + 512:(kt + 1) * H],
                               start=(kt == 0), stop=(kt == 31),
                               sem=sT if kt == 31 else None)
                tokQrd = tDn
                p.wait("scalar", sT, tDn)
                tS1 = p.op("scalar", "activation", tscD[:], pbig[:], AF.Tanh,
                           scale=dqcol[:, AMG + j:AMG + j + 1], sem=sS)
                p.wait("vector", sS, tS1)
                p.op("vector", "tensor_scalar_add", tscD[:], tscD[:], 1.0)
                tokG1 = p.op("vector", "scalar_tensor_tensor", g1t[:], pbig[:],
                             dqcol[:, AMG + j:AMG + j + 1], tscD[:],
                             op0=ALU.mult, op1=ALU.mult, sem=sV)
                p.wait("sync", sV, tokOselRd)
                t = p.op("sync", "dma_start", sem=sLD, step=16, out=xnb[:],
                         in_=o_hbm[j])
                p.wait("vector", sLD, t)
                tok1 = p.op("vector", "tensor_single_scalar", xsel[:], xnb[:],
                            zmc[:, 0:1], op=ALU.mult, sem=sV)
                p.wait("sync", sV, tok1)
                t = p.op("sync", "dma_start", sem=sLD, step=16, out=xnb[:],
                         in_=o_hbm[j + NOWN])
                p.wait("vector", sLD, t)
                tokOselRd = p.op("vector", "scalar_tensor_tensor", xsel[:],
                                 xnb[:], zmc[:, 1:2], xsel[:], op0=ALU.mult,
                                 op1=ALU.add, sem=sV)
                p.op("vector", "tensor_mul", xsel[:], xsel[:], SCLb[:])
                tOn = p.op("vector", "tensor_add", xsel[:], xsel[:], BIAb[:],
                           sem=sV)
                p.wait("sync", sT, tokGp)
                tX = p.op("sync", "dma_start", sem=sLD, step=16, out=xnT[:],
                          in_=xn1tp_hbm[j].rearrange("(k q) t -> q k t",
                                                     q=128))
                p.wait("tensor", sLD, tX)
                p.wait("tensor", sS, tokSg)
                for kt in range(KT):
                    p.op("tensor", "matmul", pu[:],
                         xnT[:, kt * 128:(kt + 1) * 128],
                         wg_w[:, kt * H:kt * H + 512],
                         start=(kt == 0), stop=(kt == KT - 1))
                tGp = None
                for kt in range(KT):
                    tGp = p.op("tensor", "matmul", ptiny[:],
                               xnT[:, kt * 128:(kt + 1) * 128],
                               wg_w[:, kt * H + 512:(kt + 1) * H],
                               start=(kt == 0), stop=(kt == KT - 1),
                               sem=sT if kt == KT - 1 else None)
                tokGp = tGp
                p.wait("scalar", sT, tGp)
                tS2 = p.op("scalar", "activation", tscD[:, 0:512], pu[:],
                           AF.Tanh, scale=0.5, sem=sS)
                tS2 = p.op("scalar", "activation", tscD[:, 512:1024],
                           ptiny[:], AF.Tanh, scale=0.5, sem=sS)
                tokSg = tS2
                p.wait("vector", sS, tS2)
                p.op("vector", "tensor_scalar", tscD[:], tscD[:], 0.5, 0.5,
                     op0=ALU.mult, op1=ALU.add)
                tOgb = p.op("vector", "tensor_mul", ogb[:], xsel[:], tscD[:],
                            sem=sV)
                p.wait("tensor", sV, max(tokTRd, tOgb))
                lt = None
                for i in range(8):
                    lt = p.op("tensor", "transpose",
                              ptrC[:, i * 128:(i + 1) * 128],
                              ogb[:, i * 128:(i + 1) * 128], identb[:],
                              sem=sT if i == 7 else None)
                p.wait("vector", sT, lt)
                tokTRd = p.op("vector", "tensor_copy", ogT[:], ptrC[:], sem=sV)
                tokOg = tokTRd
                p.wait("tensor", sV, max(tokTRd, tokG1))
                for kt in range(KT):
                    p.op("tensor", "matmul", pbig[:, 0:512],
                         ogT[:, kt * 128:(kt + 1) * 128],
                         wo_w[:, kt * H:kt * H + 512],
                         start=(kt == 0), stop=(kt == KT - 1))
                tWo = None
                for kt in range(KT):
                    tWo = p.op("tensor", "matmul", pbig[:, 512:1024],
                               ogT[:, kt * 128:(kt + 1) * 128],
                               wo_w[:, kt * H + 512:(kt + 1) * H],
                               start=(kt == 0), stop=(kt == KT - 1),
                               sem=sT if kt == KT - 1 else None)
                tokQrd = tWo
                p.wait("sync", sV, tokXaRd)
                tX1 = p.op("sync", "dma_start", sem=sLD, step=16, out=xa[:],
                           in_=x1p_hbm[j])
                p.wait("vector", sT, tWo)
                p.wait("vector", sLD, tX1)
                p.wait("vector", sWR, tokOut[pr])
                p.op("vector", "tensor_mul", outt[pr][:], pbig[:], g1t[:])
                tOu = p.op("vector", "tensor_add", outt[pr][:], outt[pr][:],
                           xa[:], sem=sV)
                tokXaRd = tOu
                p.wait("gpsimd", sV, tOu)
                tokOut[pr] = p.op("gpsimd", "dma_start", sem=sWR, step=16,
                                  out=out[j * 128:(j + 1) * 128, :],
                                  in_=outt[pr][:])
            p.wait("gpsimd", sWR, sWR.count)
            p.wait("vector", sV, sV.count)
            p.wait("sync", sWR, sWR.count)
            p.replay(blk)
        es_d2.close()
        es_d.close()

    es.close()
    return nc, dbg_t, inp



def make_in_map(inputs, b, z=0, L=None):
    if L is None:
        L = L_TOKENS
    f32 = np.float32
    x = np.ascontiguousarray(inputs["L0_attn_input_sequence"][b, :L]).astype(f32, copy=False)
    sp = np.stack([inputs["L0_attn_steps_raw"].reshape(2, 128).T,
                   inputs["L0_attn_G_raw"].reshape(2, 128).T,
                   inputs["L0_attn_A_raw"].reshape(2, 128).T], axis=1)
    ssm_par = np.ascontiguousarray(sp.reshape(128, 6).astype(f32))
    Bp = np.ascontiguousarray(
        inputs["L0_attn_B_mat"].transpose(1, 0, 2).reshape(H, 2 * P).astype(f32))
    Cp = np.ascontiguousarray(
        inputs["L0_attn_C_mat"].transpose(1, 2, 0).reshape(2 * P, H).astype(f32))
    Db = np.ascontiguousarray(
        np.broadcast_to(inputs["L0_attn_D_vec"][None, :], (128, H)).astype(f32))
    rhs = np.ascontiguousarray(inputs["L0_ffn_rhs"].astype(f32))
    rhsT = np.ascontiguousarray(rhs.T)
    gn_row = np.ascontiguousarray(
        np.concatenate([inputs["L1_attn_gn_w"], inputs["L1_attn_gn_b"]])[None, :]
        .astype(f32))
    bab = np.ascontiguousarray(
        np.broadcast_to(inputs["L1_attn_ba"][None, :], (128, NH)).astype(f32))
    triu_np = np.triu(np.ones((128, 128), np.float32))
    tril_np = np.tril(np.ones((128, 128), np.float32))
    ident_np = np.eye(128, dtype=np.float32)
    pm1_np = np.ones((128, 128), np.float32)
    pm1_np[1::2, :] = -1.0
    triuS_np = np.triu(np.ones((128, 128), np.float32), 1)
    zpad = np.zeros((128, 128), np.float32)
    consts_in = np.ascontiguousarray(np.concatenate(
        [triu_np, tril_np, ident_np, pm1_np, triuS_np, zpad], axis=1))
    return {
        "zmcols": np.ascontiguousarray(
            np.broadcast_to(np.array([1.0 - z, float(z)], np.float32)[None, :],
                            (128, 2))),
        "consts_in": consts_in,
        "x0": x,
        "ssm_par": ssm_par,
        "Bp_in": Bp,
        "Cp_in": Cp,
        "Db": Db,
        "rhs_in": rhs,
        "rhsT_in": rhsT,
        "Wq_in": np.ascontiguousarray(inputs["L1_attn_Wq"].astype(f32)),
        "Wk_in": np.ascontiguousarray(inputs["L1_attn_Wk"].astype(f32)),
        "Wv_in": np.ascontiguousarray(inputs["L1_attn_Wv"].astype(f32)),
        "Wg_in": np.ascontiguousarray(inputs["L1_attn_Wg"].astype(f32)),
        "Wa_in": np.ascontiguousarray(inputs["L1_attn_Wa"].astype(f32)),
        "bab": bab,
        "Wo_in": np.ascontiguousarray(inputs["L1_attn_Wo"].astype(f32)),
        "gn_row": gn_row,
        "wgT_in": np.ascontiguousarray(inputs["L1_ffn_w_gate"].T.astype(f32)),
        "wuT_in": np.ascontiguousarray(inputs["L1_ffn_w_up"].T.astype(f32)),
        "wdT_in": np.ascontiguousarray(inputs["L1_ffn_w_down"].T.astype(f32)),
    }


def kernel(**inputs):
    nc, _, _ = build_program()
    in_maps = []
    for c in range(8):
        b = c // 2
        m = make_in_map(inputs, b, z=c % 2)
        in_maps.append(m)
    res = run_bass_kernel_spmd(nc, in_maps, core_ids=list(range(8)))
    B, L = 4, L_TOKENS
    outp = np.empty((B, L, H), np.float32)
    for c in range(8):
        b, zz = c // 2, c % 2
        outp[b, zz * (L // 2):(zz + 1) * (L // 2)] = res.results[c]["out"]
    return outp


# version 03 (final)
# speedup vs baseline: 1.0279x; 1.0059x over previous
import contextlib

import numpy as np
import concourse.bass as bass
import concourse.mybir as mybir
from concourse.bass_utils import run_bass_kernel_spmd

F32 = mybir.dt.float32
BF16 = mybir.dt.bfloat16
I32 = mybir.dt.int32
AF = mybir.ActivationFunctionType
ALU = mybir.AluOpType
AX = mybir.AxisListType

H = 1024
P = 256
NH = 16
DH = 64
INNER = 4096
CT = 128
L_TOKENS = 8192
RND_C = 1.5 * 2.0 ** 23
MAGIC_RSQRT = 0x5F3759DF

ENGINES = ("sync", "gpsimd", "vector", "scalar", "tensor")


class Sem:
    def __init__(self, handle):
        self.h = handle
        self.count = 0


class Prog:

    def __init__(self, esems=None):
        self.streams = {e: [] for e in ENGINES}
        self.esems = esems or {}

    def op(self, eng, fname, *args, sem=None, step=1, **kw):
        ser = self.esems.get(eng)
        if ser is not None and fname != "dma_start":
            if sem is None:
                sem = ser
            if sem is ser:
                self.wait(eng, ser, ser.count)
        tok = None
        if sem is not None:
            sem.count += step
            tok = sem.count

        def fn(E, fname=fname, args=args, kw=kw, sem=sem, step=step):
            rargs = [a() if callable(a) else a for a in args]
            rkw = {k: (v() if callable(v) else v) for k, v in kw.items()}
            ins = getattr(E, fname)(*rargs, **rkw)
            if sem is not None:
                ins.then_inc(sem.h, step)

        self.streams[eng].append(fn)
        if fname == "dma_start" and sem is not None:
            self.wait(eng, sem, tok)
        return tok

    def opw(self, eng, sem, fname, *args, **kw):
        tok = self.op(eng, fname, *args, sem=sem, **kw)
        self.wait(eng, sem, tok)
        return tok

    def wait(self, eng, sem, tok):
        if tok is None or tok <= 0:
            return
        self.streams[eng].append(lambda E, s=sem, t=tok: E.wait_ge(s.h, t))

    def raw(self, eng, fn):
        self.streams[eng].append(fn)

    def replay(self, block):
        for name in ENGINES:
            fns = self.streams[name]
            if not fns:
                continue

            def make(fl):
                def body(E):
                    for f in fl:
                        f(E)
                return body

            getattr(block, name)(make(fns))


def emit_ln_cols(p, sV, cols, i_sum, i_sq, i_out_is, i_out_negmis, n, eps,
                 sG=None, cmagic=None):
    c = cols
    p.opw("vector", sV, "tensor_scalar_mul", c[:, 24:25], c[:, i_sum:i_sum + 1],
          1.0 / n)
    p.opw("vector", sV, "tensor_scalar_mul", c[:, 25:26], c[:, i_sq:i_sq + 1], 1.0 / n)
    p.opw("vector", sV, "tensor_mul", c[:, 26:27], c[:, 24:25], c[:, 24:25])
    p.opw("vector", sV, "tensor_sub", c[:, 25:26], c[:, 25:26], c[:, 26:27])
    p.opw("vector", sV, "tensor_scalar_add", c[:, 25:26], c[:, 25:26], eps)
    tv = p.opw("vector", sV, "tensor_single_scalar", c[:, 26:27].bitcast(I32),
               c[:, 25:26].bitcast(I32), 1, op=ALU.logical_shift_right)
    p.wait("gpsimd", sV, tv)
    tg = p.op("gpsimd", "tensor_tensor", c[:, 26:27].bitcast(I32),
              cmagic[:], c[:, 26:27].bitcast(I32), op=ALU.subtract, sem=sG)
    p.wait("vector", sG, tg)
    p.opw("vector", sV, "tensor_scalar_mul", c[:, 27:28], c[:, 25:26], -0.5)
    for _ in range(3):
        p.opw("vector", sV, "tensor_mul", c[:, 28:29], c[:, 26:27], c[:, 26:27])
        p.opw("vector", sV, "tensor_mul", c[:, 28:29], c[:, 28:29], c[:, 27:28])
        p.opw("vector", sV, "tensor_scalar_add", c[:, 28:29], c[:, 28:29], 1.5)
        p.opw("vector", sV, "tensor_mul", c[:, 26:27], c[:, 26:27], c[:, 28:29])
    p.opw("vector", sV, "tensor_copy", c[:, i_out_is:i_out_is + 1], c[:, 26:27])
    p.opw("vector", sV, "tensor_mul", c[:, 29:30], c[:, 24:25], c[:, 26:27])
    return p.opw("vector", sV, "tensor_scalar_mul",
                 c[:, i_out_negmis:i_out_negmis + 1], c[:, 29:30], -1.0)


def build_program(L=None, z=0, run_phases=("A", "B", "C", "D"), dbg=False):
    if L is None:
        L = L_TOKENS
    NCH = L // CT
    KT = H // 128
    NOWN = (L // 2) // CT
    OWN0 = NOWN * z

    nc = bass.Bass("TRN2", target_bir_lowering=False, debug=False)

    inp = {}

    def I(name, shape):
        inp[name] = nc.dram_tensor(name, list(shape), F32, kind="ExternalInput").ap()
        return inp[name]

    x0 = I("x0", (L, H))
    ssm_par = I("ssm_par", (128, 6))
    Bp_in = I("Bp_in", (H, 2 * P))
    Cp_in = I("Cp_in", (2 * P, H))
    Db = I("Db", (128, H))
    rhs_in = I("rhs_in", (H, H))
    rhsT_in = I("rhsT_in", (H, H))
    Wq_in = I("Wq_in", (H, H))
    Wk_in = I("Wk_in", (H, H))
    Wv_in = I("Wv_in", (H, H))
    Wg_in = I("Wg_in", (H, H))
    Wa_in = I("Wa_in", (H, NH))
    bab = I("bab", (128, NH))
    Wo_in = I("Wo_in", (H, H))
    gn_row = I("gn_row", (1, 2 * H))
    zmcols = I("zmcols", (128, 2))
    wgT_in = I("wgT_in", (H, INNER))
    wuT_in = I("wuT_in", (H, INNER))
    wdT_in = I("wdT_in", (INNER, H))

    out = nc.dram_tensor("out", [L // 2, H], F32, kind="ExternalOutput").ap()

    dbg_t = {}
    if dbg:
        for nm, shape, dt in (
            ("d_xn0", (L, H), BF16), ("d_g0", (L, H), F32), ("d_s0", (L, H), F32),
            ("d_x1", (L, H), F32), ("d_za", (128, NCH * NH), F32),
            ("d_cum", (128, NCH * NH), F32), ("d_ln1", (128, NCH * 4), F32),
            ("d_setup", (128, 4864), F32), ("d_ffn0", (128, 4 * H), F32),
            ("d_gst", (128, 2 * NH), F32), ("d_o", (L, H), BF16),
        ):
            dbg_t[nm] = nc.dram_tensor(nm, list(shape), dt, kind="ExternalOutput").ap()

    xn1t_hbm = nc.dram_tensor("xn1t_hbm", [NCH, KT * 128, 128], BF16, kind="Internal").ap()
    x1_hbm = nc.dram_tensor("x1_hbm", [NCH, 128, H], F32, kind="Internal").ap()
    o_hbm = nc.dram_tensor("o_hbm", [NCH, 128, H], BF16, kind="Internal").ap()
    wgq_hbm = nc.dram_tensor("wgq_hbm", [KT, 128, INNER], BF16, kind="Internal").ap()
    wuq_hbm = nc.dram_tensor("wuq_hbm", [KT, 128, INNER], BF16, kind="Internal").ap()
    wdq_hbm = nc.dram_tensor("wdq_hbm", [INNER // 128, 128, H], BF16, kind="Internal").ap()
    rowscr = nc.dram_tensor("rowscr", [NCH, NH], F32, kind="Internal").ap()
    cumt_hbm = nc.dram_tensor("cumt_hbm", [NCH, NH, 128], F32, kind="Internal").ap()
    gu_hbm = nc.dram_tensor("gu_hbm", [L // 256, 128, INNER], BF16, kind="Internal").ap()
    xqt_hbm = nc.dram_tensor("xqt_hbm", [L // 256, KT * 128, 128], BF16, kind="Internal").ap()
    xn1tp_hbm = nc.dram_tensor("xn1tp_hbm", [L // 256, KT * 128, 128], BF16, kind="Internal").ap()
    x1p_hbm = nc.dram_tensor("x1p_hbm", [L // 256, 128, H], F32, kind="Internal").ap()

    consts = nc.dram_tensor("consts_in", [128, 768], F32,
                            kind="ExternalInput").ap()

    es = contextlib.ExitStack()
    es_s0 = contextlib.ExitStack()
    es_ab = contextlib.ExitStack()

    def _sb(stack, name, shape, dt=F32):
        return stack.enter_context(nc.sbuf_tensor(name, list(shape), dt))

    def sb(name, shape, dt=F32):
        return _sb(es, name, shape, dt)

    def sb0(name, shape, dt=F32):
        return es_s0.enter_context(nc.sbuf_tensor(name, list(shape), dt, side="right"))

    def sbab(name, shape, dt=F32):
        return _sb(es_ab, name, shape, dt)

    def psb(name, shape, dt=F32):
        return es.enter_context(nc.psum_tensor(name, list(shape), dt))

    def mksem(name):
        return Sem(es.enter_context(nc.semaphore(name)))

    tri = sb("tri", (128, 128))
    ident = sb("identt", (128, 128))
    identb = sb("identb", (128, 128), BF16)
    pm1 = sb("pm1", (128, 1))
    triuS = sb("triuS", (128, 128))
    trilx8 = sb("trilx8", (128, 8 * 128))
    bab_t = sb("bab_t", (128, NH))
    ln1sv = sb("ln1sv", (128, NCH * 4))
    cumall = sb("cumall", (128, NCH * NH))
    gstat = sb("gstat", (128, 2 * NH))
    cC = sb("cC", (128, 1))
    cmagic = sb("cmagic", (128, 1), I32)
    c05 = sb("c05", (128, 1))
    onescol = sb("onescol", (1, 128))
    ones128 = sb("ones128", (128, 1))
    cols = sb("colsA", (128, 32))
    zmc = sb("zmc", (128, 2))
    scolD = sb("scolD", (128, 6))

    q0e = sbab("q0e", (128, 512))
    q1e = sbab("q1e", (128, 512))
    r21e = sbab("r21e", (128, 512))
    r22e = sbab("r22e", (128, 512))
    trow = [sbab(f"trow{i}", (1, 512)) for i in range(4)]
    rscr = [sbab(f"rscr{i}", (1, 512)) for i in range(2)]
    grow = [sbab(f"grow{i}", (1, 512)) for i in range(2)]
    dexp = sbab("dexp", (128, H), BF16)
    rsexp = sbab("rsexp", (128, H))
    bpw = sbab("bpw", (128, KT * 512), BF16)
    cpw = sbab("cpw", (128, 4 * H), BF16)
    rhq = sbab("rhq", (128, KT * H), BF16)
    wab = sbab("wab", (128, KT * NH), BF16)
    zaall = sbab("zaall", (128, NCH * NH))
    sqdump = sbab("sqdump", (128, H))
    x0t = [sbab(f"x0t{i}", (128, H)) for i in range(2)]
    xn0 = [sbab(f"xn0_{i}", (128, H), BF16) for i in range(2)]
    xn0T = [sbab(f"xn0T{i}", (128, H), BF16) for i in range(2)]
    qtl = [sbab(f"qtl{i}", (128, H), BF16) for i in range(2)]
    qtlT = [sbab(f"qtlT{i}", (128, H), BF16) for i in range(2)]
    g0t = [sbab(f"g0t{i}", (128, H)) for i in range(2)]
    sc0 = [sbab(f"sc0_{i}", (128, H)) for i in range(2)]
    x1t = [sbab(f"x1t{i}", (128, H)) for i in range(2)]
    xn1 = [sbab(f"xn1_{i}", (128, H), BF16) for i in range(2)]
    xn1T = [sbab(f"xn1T{i}", (128, H), BF16) for i in range(2)]
    v0t = [sbab(f"v0t{i}", (128, 512)) for i in range(2)]
    v1t = [sbab(f"v1t{i}", (128, 512)) for i in range(2)]
    yst = [sbab(f"yst{i}", (128, 512), BF16) for i in range(2)]
    ysT = [sbab(f"ysT{i}", (128, 512), BF16) for i in range(2)]
    tsc = [sbab(f"tsc{i}", (128, H)) for i in range(2)]
    tsc2 = [sbab(f"tsc2_{i}", (128, H)) for i in range(2)]

    pbig = psb("pbig", (128, 1024))
    pu = psb("pu", (128, 512))
    ph0 = psb("ph0", (128, 512))
    ph1 = psb("ph1", (128, 512))
    ptiny = psb("ptiny", (128, 512))
    ptr = es_ab.enter_context(nc.psum_tensor("ptr", [128, 1024], BF16))
    ptr2 = es_ab.enter_context(nc.psum_tensor("ptr2", [128, 1024], BF16))

    sLD = mksem("sLD")
    sWR = mksem("sWR")
    sT = mksem("sT")
    sV = mksem("sV")
    sS = mksem("sS")
    sG = mksem("sG")

    WSTG_C = 2048
    wstg = sb0("wstg", (128, WSTG_C))
    wbf_stage = sb0("wbf_stage", (128, WSTG_C), BF16)

    with nc.Block() as blk:
        p = Prog(esems={"vector": sV, "scalar": sS, "gpsimd": sG})

        def ld(dst, src):
            return p.op("sync", "dma_start", sem=sLD, step=16, out=dst, in_=src)

        t_consts = ld(tri[:], consts[:, 0:128])
        t_ident = ld(ident[:], consts[:, 256:384])
        p.op("sync", "dma_start", sem=sLD, step=16, out=pm1[:],
             in_=consts[:, 384:385], allow_slow_non_contiguous=True)
        parms = sb0("parms", (128, 6))
        t_par = ld(parms[:], ssm_par)
        t_db = ld(sqdump[:, 0:H], Db)
        ld(bab_t[:], bab)
        ld(triuS[:], consts[:, 512:640])
        for i8 in range(8):
            ld(trilx8[:, i8 * 128:(i8 + 1) * 128], consts[:, 0:128])

        p.wait("vector", sLD, t_consts)
        p.op("vector", "memset", cC[:], RND_C)
        p.op("vector", "memset", c05[:], 0.5)
        p.op("vector", "memset", onescol[:], 1.0)
        p.op("vector", "memset", ones128[:], 1.0)
        p.op("gpsimd", "memset", cmagic[:], MAGIC_RSQRT)
        p.op("vector", "memset", grow[0][:], 0.0)
        p.op("vector", "memset", grow[1][:], 0.0)
        p.op("vector", "memset", gstat[:], 0.0)
        p.op("vector", "memset", ln1sv[:], 0.0)
        p.wait("vector", sLD, t_ident)
        p.op("vector", "tensor_copy", identb[:], ident[:])
        p.wait("vector", sLD, t_db)
        p.op("vector", "tensor_copy", dexp[:], sqdump[:, 0:H])

        d = {nm: sb0(f"dv_{nm}", (128, 2)) for nm in
             ("steps", "S", "alpha", "c1", "c2", "m00", "m01", "m10", "m11",
              "i00", "i01", "i10", "i11", "t1", "rS")}
        stepsraw = parms[:, 0:2]
        graw = parms[:, 2:4]
        araw = parms[:, 4:6]
        p.wait("scalar", sLD, t_par)
        tok = p.op("scalar", "activation", d["t1"][:], stepsraw, AF.Tanh, scale=0.5,
                   sem=sS)
        p.wait("vector", sS, tok)
        tok = p.opw("vector", sV, "tensor_scalar", d["steps"][:], d["t1"][:], 0.5, 0.5,
                   op0=ALU.mult, op1=ALU.add)
        p.wait("vector", sV, tok)
        p.opw("vector", sV, "tensor_scalar_max", d["t1"][:], graw, 0.0)
        p.opw("vector", sV, "tensor_mul", d["S"][:], d["t1"][:], d["steps"][:])
        p.opw("vector", sV, "tensor_scalar_add", d["S"][:], d["S"][:], 1.0)
        p.opw("vector", sV, "reciprocal", d["rS"][:], d["S"][:])
        p.opw("vector", sV, "tensor_scalar_max", d["t1"][:], araw, 0.0)
        p.opw("vector", sV, "tensor_mul", d["t1"][:], d["t1"][:], d["steps"][:])
        p.opw("vector", sV, "tensor_mul", d["t1"][:], d["t1"][:], d["steps"][:])
        tok = p.opw("vector", sV, "tensor_mul", d["t1"][:], d["t1"][:], d["rS"][:])
        p.wait("scalar", sV, tok)
        tok = p.op("scalar", "activation", d["alpha"][:], d["t1"][:], AF.Tanh,
                   scale=1.0 / 1.99, sem=sS)
        p.wait("vector", sS, tok)
        p.opw("vector", sV, "tensor_scalar_mul", d["alpha"][:], d["alpha"][:], 1.99)
        p.opw("vector", sV, "tensor_copy", d["m00"][:], d["rS"][:])
        p.opw("vector", sV, "reciprocal", d["t1"][:], d["steps"][:])
        p.opw("vector", sV, "tensor_mul", d["m01"][:], d["alpha"][:], d["t1"][:])
        p.opw("vector", sV, "tensor_scalar_mul", d["m01"][:], d["m01"][:], -1.0)
        p.opw("vector", sV, "tensor_mul", d["m10"][:], d["steps"][:], d["rS"][:])
        p.opw("vector", sV, "tensor_scalar", d["m11"][:], d["alpha"][:], -1.0, 1.0,
             op0=ALU.mult, op1=ALU.add)
        p.opw("vector", sV, "tensor_copy", d["c1"][:], d["m10"][:])
        p.opw("vector", sV, "tensor_mul", d["c2"][:], d["m10"][:], d["steps"][:])
        p.opw("vector", sV, "tensor_mul", d["i00"][:], d["m11"][:], d["S"][:])
        p.opw("vector", sV, "tensor_mul", d["i01"][:], d["m01"][:], d["S"][:])
        p.opw("vector", sV, "tensor_scalar_mul", d["i01"][:], d["i01"][:], -1.0)
        p.opw("vector", sV, "tensor_mul", d["i10"][:], d["m10"][:], d["S"][:])
        p.opw("vector", sV, "tensor_scalar_mul", d["i10"][:], d["i10"][:], -1.0)
        p.opw("vector", sV, "tensor_mul", d["i11"][:], d["m00"][:], d["S"][:])

        pos = {e: sb0(f"pow{e}", (128, 258)) for e in ("00", "01", "10", "11")}
        neg = {e: sb0(f"neg{e}", (128, 258)) for e in ("00", "01", "10", "11")}
        powtmp = sb0("powtmp", (128, 129))
        for tab in (pos, neg):
            for e in tab:
                p.opw("vector", sV, "memset", tab[e][:], 0.0)
        for tl in range(2):
            b = tl * 129
            for e in ("00", "11"):
                p.opw("vector", sV, "memset", pos[e][:, b:b + 1], 1.0)
                p.opw("vector", sV, "memset", neg[e][:, b:b + 1], 1.0)
            for e, srct in (("00", "m00"), ("01", "m01"), ("10", "m10"), ("11", "m11")):
                p.opw("vector", sV, "tensor_copy", pos[e][:, b + 1:b + 2],
                     d[srct][:, tl:tl + 1])
            for e, srct in (("00", "i00"), ("01", "i01"), ("10", "i10"), ("11", "i11")):
                p.opw("vector", sV, "tensor_copy", neg[e][:, b + 1:b + 2],
                     d[srct][:, tl:tl + 1])

        def extend(tab, n, tl):
            cnt = min(n, 128 - n)
            if cnt <= 0:
                return
            b = tl * 129
            for eo in ("00", "01", "10", "11"):
                r, c = eo[0], eo[1]
                sA = tab[r + "0"][:, b + n:b + n + 1]
                sB = tab[r + "1"][:, b + n:b + n + 1]
                p.opw("vector", sV, "tensor_single_scalar", powtmp[:, 0:cnt],
                     tab["0" + c][:, b + 1:b + 1 + cnt], sA, op=ALU.mult)
                p.opw("vector", sV, "scalar_tensor_tensor",
                     tab[eo][:, b + n + 1:b + n + 1 + cnt],
                     tab["1" + c][:, b + 1:b + 1 + cnt], sB, powtmp[:, 0:cnt],
                     op0=ALU.mult, op1=ALU.add)

        for tl in range(2):
            n = 1
            while n < 128:
                extend(pos, n, tl)
                extend(neg, n, tl)
                n *= 2

        qpk = {0: sb0("qpk0", (128, 258)), 1: sb0("qpk1", (128, 258))}
        for tl in range(2):
            b = tl * 129
            for comp, (ea, eb) in ((0, ("00", "01")), (1, ("10", "11"))):
                p.opw("vector", sV, "tensor_single_scalar", powtmp[:, 0:129],
                     neg[ea][:, b:b + 129], d["c1"][:, tl:tl + 1], op=ALU.mult)
                p.opw("vector", sV, "scalar_tensor_tensor", qpk[comp][:, b:b + 129],
                     neg[eb][:, b:b + 129], d["c2"][:, tl:tl + 1], powtmp[:, 0:129],
                     op0=ALU.mult, op1=ALU.add)
        tokVq = p.opw("vector", sV, "memset", powtmp[0:1, 0:1], 0.0)

        p.wait("tensor", sV, tokVq)
        p.wait("tensor", sLD, t_consts)
        evtok = {0: 0, 1: 0}

        def table_exp(srctab, base, dest, ptile, drow=None):
            nonlocal bk
            bk = 0
            pb = ptiny[:, 0:128]
            p.wait("tensor", sV, evtok[bk])
            t = p.op("tensor", "transpose", pb[0:128, 0:128] if drow is None
                     else pb[0:1, 0:128],
                     srctab[:, base:base + 128] if drow is None
                     else srctab[:, base:base + 1],
                     ident[:], sem=sT)
            p.wait("vector", sT, t)
            if drow is None:
                p.opw("vector", sV, "tensor_copy",
                     dest[:, ptile * 256 + 0:ptile * 256 + 256:2], pb[:, 0:128])
                evtok[bk] = p.opw("vector", sV, "tensor_copy",
                                 dest[:, ptile * 256 + 1:ptile * 256 + 256:2],
                                 pb[:, 0:128])
            else:
                p.opw("vector", sV, "tensor_copy",
                     dest[0:1, ptile * 256 + 0:ptile * 256 + 256:2],
                     pb[0:1, 0:128])
                evtok[bk] = p.opw("vector", sV, "tensor_copy",
                                 dest[0:1, ptile * 256 + 1:ptile * 256 + 256:2],
                                 pb[0:1, 0:128])
            bk ^= 1

        bk = 0
        for ptile in range(2):
            b = ptile * 129
            table_exp(qpk[0], b, q0e, ptile)
            table_exp(qpk[1], b, q1e, ptile)
            table_exp(pos["10"], b, r21e, ptile)
            table_exp(pos["11"], b, r22e, ptile)
        for i, e in enumerate(("00", "01", "10", "11")):
            for tl in range(2):
                table_exp(pos[e], tl * 129 + 128, trow[i], tl, drow=0)

        def cast_seq(src_dram, dst, ncols, scale_ap=None):
            t = ld(wstg[:, 0:ncols], src_dram)
            p.wait("scalar", sLD, t)
            if scale_ap is None:
                tc = p.op("scalar", "copy", dst, wstg[:, 0:ncols], sem=sS)
            else:
                tc = p.op("scalar", "activation", dst, wstg[:, 0:ncols], AF.Copy,
                          scale=scale_ap, sem=sS)
            p.wait("sync", sS, tc)
            return tc

        for kt in range(KT):
            cast_seq(Bp_in[kt * 128:(kt + 1) * 128, :], bpw[:, kt * 512:(kt + 1) * 512],
                     512)
        for pt in range(4):
            cast_seq(Cp_in[pt * 128:(pt + 1) * 128, :], cpw[:, pt * H:(pt + 1) * H],
                     H, scale_ap=pm1[:, 0:1])
        for kt in range(KT):
            cast_seq(Wa_in[kt * 128:(kt + 1) * 128, :], wab[:, kt * NH:(kt + 1) * NH],
                     NH)

        rsc_row = sb0("rsc_row", (1, H))
        lsc_row = sb0("lsc_row", (1, H))
        amx = sb0("amx", (128, 2))
        for ot in range(KT):
            t = ld(wstg[:, 0:H], rhsT_in[ot * 128:(ot + 1) * 128, :])
            p.wait("vector", sLD, t)
            p.opw("vector", sV, "reduce_max", amx[:, 0:1], wstg[:, 0:H], axis=AX.X,
                 apply_absolute_value=True)
            p.opw("vector", sV, "tensor_scalar_max", amx[:, 0:1], amx[:, 0:1], 1e-12)
            tokv = p.opw("vector", sV, "reciprocal", amx[:, 1:2], amx[:, 0:1])
            p.wait("tensor", sV, tokv)
            p.op("tensor", "transpose", ptiny[0:1, 0:128], amx[:, 0:1], ident[:])
            tokt = p.op("tensor", "transpose", ptiny[0:1, 128:256], amx[:, 1:2],
                        ident[:], sem=sT)
            p.wait("vector", sT, tokt)
            p.opw("vector", sV, "tensor_scalar_mul", lsc_row[:, ot * 128:(ot + 1) * 128],
                 ptiny[0:1, 0:128], 1.0 / 127.0)
            tokv = p.opw("vector", sV, "tensor_scalar_mul",
                        rsc_row[:, ot * 128:(ot + 1) * 128],
                        ptiny[0:1, 128:256], 127.0)
            p.wait("sync", sV, tokv)
        rscexp = wbf_stage[:].bitcast(F32)
        p.wait("tensor", sV, sV.count)
        tokv = 0
        for hh in range(2):
            p.op("tensor", "matmul", pbig[:, 0:512], onescol[:],
                 lsc_row[:, hh * 512:(hh + 1) * 512], start=True, stop=True)
            t2 = p.op("tensor", "matmul", pbig[:, 512:1024], onescol[:],
                      rsc_row[:, hh * 512:(hh + 1) * 512], start=True, stop=True,
                      sem=sT)
            p.wait("vector", sT, t2)
            p.opw("vector", sV, "tensor_copy", rsexp[:, hh * 512:(hh + 1) * 512],
                 pbig[:, 0:512])
            tokv = p.opw("vector", sV, "tensor_copy", rscexp[:, hh * 512:(hh + 1) * 512],
                        pbig[:, 512:1024])
            p.wait("tensor", sV, tokv)
        for kt in range(KT):
            t = ld(wstg[:, 0:H], rhs_in[kt * 128:(kt + 1) * 128, :])
            p.wait("vector", sLD, t)
            p.opw("vector", sV, "tensor_mul", wstg[:, 0:H], wstg[:, 0:H], rscexp)
            p.opw("vector", sV, "tensor_scalar", wstg[:, 0:H], wstg[:, 0:H], RND_C, RND_C,
                 op0=ALU.add, op1=ALU.subtract)
            p.opw("vector", sV, "tensor_scalar", wstg[:, 0:H], wstg[:, 0:H], 127.0, -127.0,
                 op0=ALU.min, op1=ALU.max)
            tokv = p.opw("vector", sV, "tensor_copy", rhq[:, kt * H:(kt + 1) * H],
                        wstg[:, 0:H])
            p.wait("sync", sV, tokv)

        def quant_ffn1(w_in, whbm, ntile, ncols, name):
            nsub = (ncols + WSTG_C - 1) // WSTG_C
            cs = min(ncols, WSTG_C)
            racc = sb0(f"racc_{name}", (128, ntile * nsub))
            for it in range(ntile):
                for hh in range(nsub):
                    t = ld(wstg[:, 0:cs],
                           w_in[it * 128:(it + 1) * 128, hh * cs:(hh + 1) * cs])
                    p.wait("vector", sLD, t)
                    tokv = p.opw("vector", sV, "reduce_sum",
                                racc[:, it * nsub + hh:it * nsub + hh + 1],
                                wstg[:, 0:cs], axis=AX.X, apply_absolute_value=True,
                                )
                    p.wait("sync", sV, tokv)
            p.wait("tensor", sV, sV.count)
            t1 = p.op("tensor", "matmul", ptiny[0:1, 0:ntile * nsub], ones128[:],
                      racc[:], start=True, stop=True, sem=sT)
            p.wait("vector", sT, t1)
            p.opw("vector", sV, "reduce_sum", amx[0:1, 0:1], ptiny[0:1, 0:ntile * nsub],
                 axis=AX.X)
            p.opw("vector", sV, "tensor_scalar_mul", amx[0:1, 0:1], amx[0:1, 0:1],
                 1.0 / (float(ntile) * 128.0 * ncols))
            p.opw("vector", sV, "tensor_scalar_max", amx[0:1, 0:1], amx[0:1, 0:1], 1e-5)
            tokv = p.opw("vector", sV, "reciprocal", amx[0:1, 1:2], amx[0:1, 0:1])
            p.wait("tensor", sV, tokv)
            t2 = p.op("tensor", "matmul", ptiny[:, 256:258], onescol[:],
                      amx[0:1, 0:2], start=True, stop=True, sem=sT)
            p.wait("vector", sT, t2)
            scol = scolD[:, 2 * _sidx[name]:2 * _sidx[name] + 2]
            tokv = p.opw("vector", sV, "tensor_copy", scol, ptiny[:, 256:258])
            for it in range(ntile):
                for hh in range(nsub):
                    t = ld(wstg[:, 0:cs],
                           w_in[it * 128:(it + 1) * 128, hh * cs:(hh + 1) * cs])
                    p.wait("vector", sLD, t)
                    p.opw("vector", sV, "tensor_single_scalar", wstg[:, 0:cs],
                         wstg[:, 0:cs], scol[:, 1:2], op=ALU.mult)
                    p.opw("vector", sV, "tensor_scalar", wstg[:, 0:cs], wstg[:, 0:cs],
                         RND_C, RND_C, op0=ALU.add, op1=ALU.subtract)
                    p.opw("vector", sV, "tensor_scalar", wstg[:, 0:cs], wstg[:, 0:cs],
                         1.0, -1.0, op0=ALU.min, op1=ALU.max)
                    tokv = p.opw("vector", sV, "tensor_copy", wbf_stage[:, 0:cs],
                                wstg[:, 0:cs])
                    p.wait("gpsimd", sV, tokv)
                    tokw = p.op("gpsimd", "dma_start", sem=sWR, step=16,
                                out=whbm[it][:, hh * cs:(hh + 1) * cs],
                                in_=wbf_stage[:, 0:cs])
                    p.wait("vector", sWR, tokw)
                    p.wait("sync", sV, tokv)
            return scol

        _sidx = {"wg": 0, "wu": 1, "wd": 2}
        scol_g = quant_ffn1(wgT_in, wgq_hbm, KT, INNER, "wg")
        scol_u = quant_ffn1(wuT_in, wuq_hbm, KT, INNER, "wu")
        scol_d = quant_ffn1(wdT_in, wdq_hbm, INNER // 128, H, "wd")

        p.replay(blk)
    es_s0.close()

    if dbg:
        with nc.Block() as blk:
            p = Prog(esems={"vector": sV, "scalar": sS, "gpsimd": sG})
            dsu = dbg_t["d_setup"]
            cvt = sb0("dsu_cvt", (128, 1024))
            specs = [(q0e[:], 0, 512, None), (r21e[:], 512, 512, None),
                     (r22e[:], 1024, 512, None), (rsexp[:], 1536, 1024, None),
                     (rhq[:, 0:1024], 2560, 1024, BF16),
                     (bpw[:, 0:512], 3584, 512, BF16),
                     (cpw[:, 0:512], 4096, 512, BF16),
                     (triuS[:, 0:128], 4608, 128, None),
                     (trilx8[:, 0:128], 4736, 128, None)]
            for ap, off, w, cast in specs:
                if cast is not None:
                    tv = p.op("vector", "tensor_copy", cvt[:, 0:w], ap, sem=sV)
                    p.wait("gpsimd", sV, tv)
                    tw = p.op("gpsimd", "dma_start", sem=sWR, step=16,
                              out=dsu[:, off:off + w], in_=cvt[:, 0:w])
                    p.wait("vector", sWR, tw)
                else:
                    p.op("gpsimd", "dma_start", sem=sWR, step=16,
                         out=dsu[:, off:off + w], in_=ap)
            p.wait("gpsimd", sWR, sWR.count)
            p.replay(blk)

    if "A" in run_phases:
        with nc.Block() as blk:
            p = Prog(esems={"vector": sV, "scalar": sS, "gpsimd": sG})
            tokEvac = {0: 0, 1: 0, "next": 0}
            tokPuFree = 0
            tChain = 0
            tokS0_prev = 0
            tokV0V1_prev = 0
            tokZaEv_prev = 0
            last_x0_use = [0, 0]
            last_v_guard = [0, 0]
            last_xn0_write_guard = [0, 0]
            last_qtl_guard = [0, 0]
            last_xn1_guard = [0, 0]
            last_xn1_dma = [0, 0]
            last_x1_dma = [0, 0]

            def transpose_group(src_bf16, dst_bf16, guard_sem, guard_tok):
                bank = tokEvac["next"]
                tokEvac["next"] ^= 1
                pb = (ptr if bank == 0 else ptr2)[:]
                p.wait("tensor", sS, tokEvac[bank])
                p.wait("tensor", guard_sem, guard_tok)
                last_t = None
                for i in range(8):
                    last_t = p.op("tensor", "transpose",
                                  pb[:, i * 128:(i + 1) * 128],
                                  src_bf16[:, i * 128:(i + 1) * 128], identb[:],
                                  sem=sT if i == 7 else None)
                p.wait("scalar", sT, last_t)
                tokEvac[bank] = p.op("scalar", "copy", dst_bf16[:], pb[:],
                                     sem=sS)
                return tokEvac[bank], last_t

            for n in range(NCH):
                pr = n % 2
                own = OWN0 <= n < OWN0 + NOWN

                p.wait("sync", sV, last_x0_use[pr])
                tLD = p.op("sync", "dma_start", sem=sLD, step=16,
                           out=x0t[pr][:], in_=x0[n * 128:(n + 1) * 128, :])

                p.wait("vector", sLD, tLD)
                p.opw("vector", sV, "reduce_sum", cols[:, 0:1], x0t[pr][:], axis=AX.X)
                p.wait("scalar", sLD, tLD)
                tSq = p.op("scalar", "activation", sqdump[:], x0t[pr][:], AF.Square,
                           accum_out=cols[:, 1:2], sem=sS)
                p.wait("vector", sS, tSq)
                tokN = emit_ln_cols(p, sV, cols, 0, 1, 2, 3, H, 1e-5, sG=sG, cmagic=cmagic)
                p.wait("vector", sV, max(tokN, last_xn0_write_guard[pr]))
                tXn0 = p.op("vector", "scalar_tensor_tensor", xn0[pr][:], x0t[pr][:],
                            cols[:, 2:3], cols[:, 3:4].broadcast_to((128, H)),
                            op0=ALU.mult, op1=ALU.add, sem=sV)

                p.opw("vector", sV, "reduce_max", cols[:, 8:9], xn0[pr][:],
                      axis=AX.X, apply_absolute_value=True)
                p.opw("vector", sV, "tensor_scalar_max", cols[:, 8:9], cols[:, 8:9],
                      1e-12)
                p.opw("vector", sV, "reciprocal", cols[:, 9:10], cols[:, 8:9])
                p.opw("vector", sV, "tensor_scalar_mul", cols[:, 10:11],
                      cols[:, 9:10], 127.0)
                tQs = p.opw("vector", sV, "tensor_scalar_mul", cols[:, 11:12],
                            cols[:, 8:9], 0.5 / 127.0)
                p.wait("vector", sV, tQs)
                tQ1 = p.op("vector", "scalar_tensor_tensor", tsc2[pr][:], xn0[pr][:],
                           cols[:, 10:11], cC[:, 0:1].broadcast_to((128, H)),
                           op0=ALU.mult, op1=ALU.add, sem=sV)
                p.wait("vector", sT, last_qtl_guard[pr])
                p.op("vector", "tensor_scalar", tsc2[pr][:], tsc2[pr][:], RND_C, 127.0,
                     op0=ALU.subtract, op1=ALU.min)
                tQ2 = p.op("vector", "tensor_scalar_max", qtl[pr][:], tsc2[pr][:],
                           -127.0, sem=sV)

                tEx0, tTx0 = transpose_group(xn0[pr], xn0T[pr], sV, tXn0)
                last_xn0_write_guard[pr] = 0
                tEq, tTq = transpose_group(qtl[pr], qtlT[pr], sV, tQ2)

                p.wait("tensor", sV, tokS0_prev)
                p.wait("tensor", sS, tEq)
                for kt in range(KT):
                    p.op("tensor", "matmul", pbig[:, 0:512],
                         qtlT[pr][:, kt * 128:(kt + 1) * 128],
                         rhq[:, kt * H:kt * H + 512],
                         start=(kt == 0), stop=(kt == KT - 1))
                tFfn = None
                for kt in range(KT):
                    last = kt == KT - 1
                    tFfn = p.op("tensor", "matmul", pbig[:, 512:1024],
                                qtlT[pr][:, kt * 128:(kt + 1) * 128],
                                rhq[:, kt * H + 512:(kt + 1) * H],
                                start=(kt == 0), stop=last, sem=sT if last else None)
                p.wait("vector", sT, tFfn)
                tDeq = p.op("vector", "tensor_mul", tsc[pr][:], pbig[:], rsexp[:],
                            sem=sV)
                p.wait("scalar", sV, tDeq)
                tTh = p.op("scalar", "activation", tsc2[pr][:], tsc[pr][:], AF.Tanh,
                           scale=cols[:, 11:12], sem=sS)
                p.wait("vector", sS, tTh)
                p.op("vector", "scalar_tensor_tensor", tsc[pr][:], tsc2[pr][:], 1.0,
                     tsc[pr][:], op0=ALU.add, op1=ALU.mult)
                tG0 = p.op("vector", "tensor_single_scalar", g0t[pr][:], tsc[pr][:],
                           cols[:, 11:12], op=ALU.mult, sem=sV)

                p.wait("tensor", sV, tokPuFree)
                p.wait("tensor", sS, tEx0)
                tBu = None
                for kt in range(KT):
                    last = kt == KT - 1
                    tBu = p.op("tensor", "matmul", pu[:],
                               xn0T[pr][:, kt * 128:(kt + 1) * 128],
                               bpw[:, kt * 512:(kt + 1) * 512],
                               start=(kt == 0), stop=last, sem=sT if last else None)
                p.wait("vector", sT, tBu)
                p.wait("vector", sG, last_v_guard[pr])
                p.op("vector", "tensor_mul", v0t[pr][:], pu[:], q0e[:])
                tokV0V1_prev = p.op("vector", "tensor_mul", v1t[pr][:], pu[:], q1e[:],
                                    sem=sV)
                p.op("vector", "tensor_add", v0t[pr][0:1, :], v0t[pr][0:1, :],
                     grow[0][:])
                tVc = p.op("vector", "tensor_add", v1t[pr][0:1, :], v1t[pr][0:1, :],
                           grow[1][:], sem=sV)
                p.wait("tensor", sV, tVc)
                p.op("tensor", "matmul", ph0[:], tri[:], v0t[pr][:], start=True,
                     stop=True)
                p.op("tensor", "matmul", ph1[:], tri[:], v1t[pr][:], start=True,
                     stop=True)
                p.wait("tensor", sV, tokZaEv_prev)
                p.op("tensor", "matmul", pu[0:1, 0:512], ones128[:], v0t[pr][:],
                     start=True, stop=True)
                tCum = p.op("tensor", "matmul", ptiny[0:1, 0:512], ones128[:],
                            v1t[pr][:], start=True, stop=True, sem=sT)
                p.wait("vector", sT, tCum)
                p.op("vector", "tensor_mul", v0t[pr][:], ph0[:], r21e[:])
                tYab = p.op("vector", "tensor_mul", v1t[pr][:], ph1[:],
                            r22e[:], sem=sV)
                p.wait("gpsimd", sV, tYab)
                tYs = p.op("gpsimd", "tensor_add", yst[pr][:], v0t[pr][:],
                           v1t[pr][:], sem=sG)
                p.op("vector", "tensor_mul", rscr[0][:], pu[0:1, 0:512], trow[0][:])
                tcv = p.op("vector", "tensor_mul", rscr[1][:], ptiny[0:1, 0:512],
                           trow[1][:], sem=sV)
                p.wait("vector", sV, tcv)
                p.op("vector", "tensor_add", grow[0][:], rscr[0][:], rscr[1][:])
                p.op("vector", "tensor_mul", rscr[0][:], pu[0:1, 0:512], trow[2][:])
                tcv = p.op("vector", "tensor_mul", rscr[1][:], ptiny[0:1, 0:512],
                           trow[3][:], sem=sV)
                p.wait("vector", sV, tcv)
                tChain = p.op("vector", "tensor_add", grow[1][:], rscr[0][:],
                              rscr[1][:], sem=sV)

                p.wait("tensor", sS, tokEvac[0])
                p.wait("tensor", sG, tYs)
                tT = None
                pbb = ptr[:, 0:512]
                for j in range(4):
                    tT = p.op("tensor", "transpose", pbb[:, j * 128:(j + 1) * 128],
                              yst[pr][:, j * 128:(j + 1) * 128], identb[:],
                              sem=sT if j == 3 else None)
                p.wait("scalar", sT, tT)
                tokEvac[0] = p.op("scalar", "copy", ysT[pr][:], pbb[:], sem=sS)
                tEys = tokEvac[0]
                p.wait("tensor", sS, tEys)
                p.wait("tensor", sV, tDeq)
                for pt in range(4):
                    p.op("tensor", "matmul", pbig[:, 0:512],
                         ysT[pr][:, pt * 128:(pt + 1) * 128],
                         cpw[:, pt * H:pt * H + 512],
                         start=(pt == 0), stop=(pt == 3))
                tCy = None
                for pt in range(4):
                    last = pt == 3
                    tCy = p.op("tensor", "matmul", pbig[:, 512:1024],
                               ysT[pr][:, pt * 128:(pt + 1) * 128],
                               cpw[:, pt * H + 512:(pt + 1) * H],
                               start=(pt == 0), stop=last, sem=sT if last else None)
                p.wait("vector", sT, tCy)
                p.op("vector", "tensor_mul", tsc2[pr][:], xn0[pr][:], dexp[:])
                tS0 = p.op("vector", "tensor_add", sc0[pr][:], tsc2[pr][:], pbig[:],
                           sem=sV)
                tokS0_prev = tS0

                p.wait("vector", sWR, last_x1_dma[pr])
                p.op("vector", "tensor_mul", tsc[pr][:], sc0[pr][:], g0t[pr][:])
                tX1 = p.op("vector", "tensor_add", x1t[pr][:], tsc[pr][:], x0t[pr][:],
                           sem=sV)
                last_x0_use[pr] = tX1
                p.wait("gpsimd", sV, tX1)
                last_x1_dma[pr] = p.op("gpsimd", "dma_start", sem=sWR, step=16,
                                       out=x1_hbm[n], in_=x1t[pr][:])

                p.opw("vector", sV, "reduce_sum", cols[:, 16:17], x1t[pr][:],
                      axis=AX.X)
                p.wait("scalar", sV, tX1)
                tSq1 = p.op("scalar", "activation", sqdump[:], x1t[pr][:], AF.Square,
                            accum_out=cols[:, 17:18], sem=sS)
                p.wait("vector", sS, tSq1)
                tokN1 = emit_ln_cols(p, sV, cols, 16, 17, 18, 19, H, 1e-5, sG=sG, cmagic=cmagic)
                p.op("vector", "tensor_copy", ln1sv[:, n * 4 + 0:n * 4 + 1],
                     cols[:, 18:19])
                p.op("vector", "tensor_copy", ln1sv[:, n * 4 + 1:n * 4 + 2],
                     cols[:, 19:20])
                p.wait("vector", sT, last_xn1_guard[pr])
                p.wait("vector", sV, tokN1)
                tXn1 = p.op("vector", "scalar_tensor_tensor", xn1[pr][:], x1t[pr][:],
                            cols[:, 18:19], cols[:, 19:20].broadcast_to((128, H)),
                            op0=ALU.mult, op1=ALU.add, sem=sV)
                p.op("vector", "reduce_max", ln1sv[:, n * 4 + 2:n * 4 + 3],
                     xn1[pr][:], axis=AX.X, apply_absolute_value=True)

                p.wait("vector", sWR, last_xn1_dma[pr])
                tExn1, tTxn1 = transpose_group(xn1[pr], xn1T[pr], sV, tXn1)
                p.wait("gpsimd", sS, tExn1)
                last_xn1_dma[pr] = p.op(
                    "gpsimd", "dma_start", sem=sWR, step=16,
                    out=xn1t_hbm[n].rearrange("(k q) t -> q k t", q=128),
                    in_=xn1T[pr][:])

                p.wait("tensor", sV, max(tokZaEv_prev, tChain))
                p.wait("tensor", sS, tExn1)
                tZa = None
                for kt in range(KT):
                    last = kt == KT - 1
                    tZa = p.op("tensor", "matmul", ptiny[:, 0:NH],
                               xn1T[pr][:, kt * 128:(kt + 1) * 128],
                               wab[:, kt * NH:(kt + 1) * NH],
                               start=(kt == 0), stop=last, sem=sT if last else None)
                p.wait("vector", sT, tZa)
                tokZaEv_prev = p.op("vector", "tensor_copy",
                                    zaall[:, n * NH:(n + 1) * NH], ptiny[:, 0:NH],
                                    sem=sV)

                if dbg and n == 0:
                    p.wait("gpsimd", sV, tDeq)
                    p.op("gpsimd", "dma_start", sem=sWR, step=16,
                         out=dbg_t["d_ffn0"][:, 0:H], in_=tsc[pr][:])
                    p.wait("gpsimd", sS, tTh)
                    p.op("gpsimd", "dma_start", sem=sWR, step=16,
                         out=dbg_t["d_ffn0"][:, H:2 * H], in_=tsc2[pr][:])
                    cvt0 = tsc2[pr]
                    p.wait("gpsimd", sV, tQ2)
                    p.op("gpsimd", "dma_start", sem=sWR, step=16,
                         out=dbg_t["d_ffn0"][:, 2 * H:2 * H + 512].bitcast(BF16),
                         in_=qtl[pr][:])
                    p.wait("gpsimd", sV, tEq)
                    p.op("gpsimd", "dma_start", sem=sWR, step=16,
                         out=dbg_t["d_ffn0"][:, 2 * H + 512:3 * H].bitcast(BF16),
                         in_=qtlT[pr][:])
                if dbg:
                    p.wait("gpsimd", sV, tXn0)
                    p.op("gpsimd", "dma_start", sem=sWR, step=16,
                         out=dbg_t["d_xn0"][n * 128:(n + 1) * 128, :], in_=xn0[pr][:])
                    p.wait("gpsimd", sV, tG0)
                    p.op("gpsimd", "dma_start", sem=sWR, step=16,
                         out=dbg_t["d_g0"][n * 128:(n + 1) * 128, :], in_=g0t[pr][:])
                    p.wait("gpsimd", sV, tS0)
                    p.op("gpsimd", "dma_start", sem=sWR, step=16,
                         out=dbg_t["d_s0"][n * 128:(n + 1) * 128, :], in_=sc0[pr][:])
                    p.wait("gpsimd", sV, tX1)
                    p.op("gpsimd", "dma_start", sem=sWR, step=16,
                         out=dbg_t["d_x1"][n * 128:(n + 1) * 128, :], in_=x1t[pr][:])

                if dbg:
                    for eng in ("vector", "scalar", "tensor", "sync", "gpsimd"):
                        p.wait(eng, sWR, sWR.count)
                tokPuFree = tChain
                last_v_guard[pr] = tYs
                last_xn0_write_guard[pr] = tS0
                last_qtl_guard[pr] = tTq
                last_xn1_guard[pr] = tTxn1

            if dbg:
                p.wait("gpsimd", sV, tokZaEv_prev)
                p.op("gpsimd", "dma_start", sem=sWR, step=16,
                     out=dbg_t["d_za"], in_=zaall[:])
                p.op("gpsimd", "dma_start", sem=sWR, step=16,
                     out=dbg_t["d_ln1"], in_=ln1sv[:])
            p.wait("gpsimd", sWR, sWR.count)
            p.replay(blk)

    if "B" in run_phases:
        with nc.Block() as blk:
            p = Prog(esems={"vector": sV, "scalar": sS, "gpsimd": sG})
            la = zaall
            for n in range(NCH):
                p.op("vector", "tensor_add", la[:, n * NH:(n + 1) * NH],
                     la[:, n * NH:(n + 1) * NH], bab_t[:])
            tv = p.op("vector", "memset", cols[0:1, 31:32], 0.0, sem=sV)
            p.wait("scalar", sV, tv)
            W = NCH * NH
            ts1 = p.op("scalar", "activation", sqdump[:, 0:W], la[:, 0:W], AF.Abs,
                       sem=sS)
            p.wait("scalar", sS, ts1)
            ts1 = p.op("scalar", "activation", sqdump[:, 0:W], sqdump[:, 0:W], AF.Exp,
                       scale=-1.0, sem=sS)
            p.wait("vector", sS, ts1)
            tv = p.opw("vector", sV, "tensor_scalar_add", sqdump[:, 0:W],
                       sqdump[:, 0:W], 1.0)
            p.wait("scalar", sV, tv)
            ts1 = p.op("scalar", "activation", sqdump[:, 0:W], sqdump[:, 0:W], AF.Ln,
                       sem=sS)
            p.wait("vector", sS, ts1)
            p.opw("vector", sV, "tensor_scalar_max", la[:, 0:W], la[:, 0:W], 0.0)
            p.opw("vector", sV, "tensor_add", la[:, 0:W], la[:, 0:W], sqdump[:, 0:W])
            tv = p.opw("vector", sV, "tensor_scalar_mul", la[:, 0:W], la[:, 0:W],
                       -1.0)
            p.wait("tensor", sV, tv)
            for half in range((W + 511) // 512):
                w0, w1 = half * 512, min(W, (half + 1) * 512)
                p.op("tensor", "matmul", pbig[:, w0:w1], tri[:], la[:, w0:w1],
                     start=True, stop=True)
            tv = 0
            for half in range((W + 511) // 512):
                w0, w1 = half * 512, min(W, (half + 1) * 512)
                p.wait("tensor", sV, tv)
                tt = p.op("tensor", "matmul", ptiny[0:1, 0:w1 - w0], ones128[:],
                          la[:, w0:w1], start=True, stop=True, sem=sT)
                p.wait("vector", sT, tt)
                tv = p.op("vector", "tensor_copy", sqdump[0:1, w0:w1],
                          ptiny[0:1, 0:w1 - w0], sem=sV)
            totm = sbab("totm", (128, NH))
            offm = sbab("offm", (128, NH))
            p.op("vector", "memset", totm[:], 0.0)
            p.op("vector", "memset", offm[:], 0.0)
            p.wait("gpsimd", sV, sV.count)
            tw = p.op("gpsimd", "dma_start", sem=sWR, step=16,
                      out=rowscr.flatten(), in_=sqdump[0:1, 0:W])
            p.wait("gpsimd", sWR, tw)
            tw = p.op("gpsimd", "dma_start", sem=sWR, step=16,
                      out=totm[0:NCH, :], in_=rowscr)
            p.wait("tensor", sWR, tw)
            tt = p.op("tensor", "matmul", ph0[0:NCH, 0:NH], triuS[0:NCH, 0:NCH],
                      totm[0:NCH, :], start=True, stop=True, sem=sT)
            p.wait("vector", sT, tt)
            tv = p.op("vector", "tensor_copy", offm[0:NCH, :], ph0[0:NCH, 0:NH],
                      sem=sV)
            p.wait("gpsimd", sV, tv)
            tw = p.op("gpsimd", "dma_start", sem=sWR, step=16,
                      out=rowscr, in_=offm[0:NCH, :])
            p.wait("gpsimd", sWR, tw)
            tw = p.op("gpsimd", "dma_start", sem=sWR, step=16,
                      out=sqdump[0:1, 0:W], in_=rowscr.flatten())
            p.wait("tensor", sWR, tw)
            tv = 0
            for half in range((W + 511) // 512):
                w0, w1 = half * 512, min(W, (half + 1) * 512)
                p.wait("tensor", sV, tv)
                tt = p.op("tensor", "matmul", ph1[:, 0:w1 - w0], onescol[:],
                          sqdump[0:1, w0:w1], start=True, stop=True, sem=sT)
                p.wait("vector", sT, tt)
                tv = p.op("vector", "tensor_copy", cumall[:, w0:w1],
                          pbig[:, w0:w1], sem=sV)
                tv = p.op("vector", "tensor_add", cumall[:, w0:w1],
                          cumall[:, w0:w1], ph1[:, 0:w1 - w0], sem=sV)
            cumTst = sbab("cumTst", (NH, 128))
            evt = 0
            evw = 0
            for n in range(NCH):
                p.wait("tensor", sV, max(tv, evt))
                tt = p.op("tensor", "transpose", ph0[0:NH, 0:128],
                          cumall[:, n * NH:(n + 1) * NH], ident[:], sem=sT)
                p.wait("vector", sT, tt)
                p.wait("vector", sWR, evw)
                evt = p.op("vector", "tensor_copy", cumTst[:], ph0[0:NH, 0:128],
                           sem=sV)
                p.wait("gpsimd", sV, evt)
                evw = p.op("gpsimd", "dma_start", sem=sWR, step=16,
                           out=cumt_hbm[n], in_=cumTst[:])
            if dbg:
                p.wait("gpsimd", sV, evt)
                p.op("gpsimd", "dma_start", sem=sWR, step=16, out=dbg_t["d_cum"],
                     in_=cumall[:])
                p.op("gpsimd", "dma_start", sem=sWR, step=16,
                     out=dbg_t["d_gst"][:, 0:NH], in_=totm[:])
                p.op("gpsimd", "dma_start", sem=sWR, step=16,
                     out=dbg_t["d_gst"][:, NH:2 * NH], in_=offm[:])

            p.wait("gpsimd", sWR, sWR.count)
            p.replay(blk)

    es_ab.close()

    es_cd = contextlib.ExitStack()

    def sbc(name, shape, dt=F32):
        return es_cd.enter_context(nc.sbuf_tensor(name, list(shape), dt))

    pdsC = es.enter_context(nc.psum_tensor("pdsC", [128, 512], F32))
    ptrC = es.enter_context(nc.psum_tensor("ptrC", [128, 1024], BF16))

    if "C" in run_phases:
        wq_w = sbc("wq_w", (128, KT * H), BF16)
        wk_w = sbc("wk_w", (128, KT * H), BF16)
        wv_w = sbc("wv_w", (128, KT * H), BF16)
        wstgC = sbc("wstgC", (128, H))
        xs = [sbc(f"xsC{i}", (128, H), BF16) for i in range(2)]
        crow = [sbc(f"crow{i}", (1, NH * 128)) for i in range(2)]
        qs = sbc("qs", (128, H), BF16)
        qsT = sbc("qsT", (128, H), BF16)
        ks = sbc("ks", (128, H), BF16)
        kT = sbc("kT", (128, H), BF16)
        kw = sbc("kw", (128, H), BF16)
        vt = sbc("vt", (128, H), BF16)
        expin = sbc("expin", (128, H))
        etile = sbc("etile", (128, H))
        e2til = sbc("e2til", (128, H))
        sct = sbc("sct", (128, H), BF16)
        obuf = [sbc(f"obuf{i}", (128, H), BF16) for i in range(2)]
        tscC = sbc("tscC", (128, H))
        tscC2 = sbc("tscC2", (128, H))
        sqC = sbc("sqC", (128, H))
        oasm = sbc("oasm", (128, H))
        sst = sbc("sst", (128, 512))
        sstb = sbc("sstb", (128, 512), BF16)
        a16 = sbc("a16", (128, NH))
        w16 = sbc("w16", (128, NH))
        peb = sbc("peb", (128, NH))
        grow16 = sbc("grow16", (1, NH))
        gsum = sbc("gsum", (128, NH))
        gss = sbc("gss", (128, NH))
        colsC = sbc("colsC", (128, 16))

        with nc.Block() as blk:
            p = Prog(esems={"vector": sV, "scalar": sS, "gpsimd": sG})

            def wcast(dst, src_dram):
                for kt in range(KT):
                    t = p.op("sync", "dma_start", sem=sLD, step=16,
                             out=wstgC[:], in_=src_dram[kt * 128:(kt + 1) * 128, :])
                    p.wait("scalar", sLD, t)
                    tc = p.op("scalar", "copy", dst[:, kt * H:(kt + 1) * H],
                              wstgC[:], sem=sS)
                    p.wait("sync", sS, tc)

            wcast(wq_w, Wq_in)
            wcast(wk_w, Wk_in)
            wcast(wv_w, Wv_in)
            p.op("vector", "memset", sst[:], 0.0)
            p.op("vector", "memset", sstb[:], 0.0)
            p.op("vector", "memset", crow[0][:], 0.0)
            p.op("vector", "memset", crow[1][:], 0.0)
            p.op("vector", "memset", gsum[:], 0.0)
            for t_ in (oasm, expin, etile, e2til, tscC, tscC2, sqC, a16, w16,
                       peb, colsC, grow16, xs[0], xs[1], qs, qsT, ks, kT, kw, vt,
                       sct, obuf[0], obuf[1]):
                p.op("vector", "memset", t_[:], 0.0)
            tv0 = p.op("vector", "memset", gss[:], 0.0, sem=sV)
            tokD = {"oin": 0, "oit": 0, "qk": 0, "ds": 0, "gb": 0, "big": tv0,
                    "k2": 0, "tr": 0, "sst": 0, "obuf": [0, 0], "xs": [0, 0],
                    "crow": [tv0, tv0], "ds": 0, "tE": 0, "tE2": 0, "sq": 0}
            tSS = 0
            p.wait("sync", sV, tv0)
            p.wait("tensor", sV, tv0)

            for n in range(NCH):
                pr = n % 2
                p.wait("sync", sT, tokD["xs"][pr])
                tX = p.op("sync", "dma_start", sem=sLD, step=16, out=xs[pr][:],
                          in_=xn1t_hbm[n].rearrange("(k q) t -> q k t", q=128))
                p.wait("sync", sV, tokD["crow"][pr])
                tC = p.op("sync", "dma_start", sem=sLD, step=16,
                          out=crow[pr][:],
                          in_=cumt_hbm[n].flatten())
                p.wait("tensor", sLD, tX)
                p.wait("tensor", sV, tokD["big"])
                for kt in range(KT):
                    p.op("tensor", "matmul", pbig[:, 0:512],
                         xs[pr][:, kt * 128:(kt + 1) * 128],
                         wq_w[:, kt * H:kt * H + 512],
                         start=(kt == 0), stop=(kt == KT - 1))
                tQm = None
                for kt in range(KT):
                    tQm = p.op("tensor", "matmul", pbig[:, 512:1024],
                               xs[pr][:, kt * 128:(kt + 1) * 128],
                               wq_w[:, kt * H + 512:(kt + 1) * H],
                               start=(kt == 0), stop=(kt == KT - 1),
                               sem=sT if kt == KT - 1 else None)
                p.wait("tensor", sV, tokD["oit"])
                for kt in range(KT):
                    p.op("tensor", "matmul", ph0[:],
                         xs[pr][:, kt * 128:(kt + 1) * 128],
                         wk_w[:, kt * H:kt * H + 512],
                         start=(kt == 0), stop=(kt == KT - 1))
                tKm = None
                for kt in range(KT):
                    tKm = p.op("tensor", "matmul", ph1[:],
                               xs[pr][:, kt * 128:(kt + 1) * 128],
                               wk_w[:, kt * H + 512:(kt + 1) * H],
                               start=(kt == 0), stop=(kt == KT - 1),
                               sem=sT if kt == KT - 1 else None)
                p.wait("scalar", sT, tQm)
                p.wait("scalar", sV, tokD["oin"])
                tS1 = p.op("scalar", "activation", tscC[:], pbig[:], AF.Tanh,
                           scale=0.5, sem=sS)
                p.wait("vector", sS, tS1)
                p.wait("vector", sT, tokD["ds"])
                p.op("vector", "scalar_tensor_tensor", tscC2[:], tscC[:], 1.0,
                     pbig[:], op0=ALU.add, op1=ALU.mult)
                tQs = p.op("vector", "tensor_scalar_mul", qs[:], tscC2[:], 0.5,
                           sem=sV)
                p.wait("scalar", sT, tKm)
                p.wait("scalar", sV, tQs)
                tS2 = p.op("scalar", "activation", tscC[:, 0:512], ph0[:], AF.Tanh,
                           scale=0.5, sem=sS)
                p.wait("scalar", sS, tS2)
                tS2 = p.op("scalar", "activation", tscC[:, 512:1024], ph1[:],
                           AF.Tanh, scale=0.5, sem=sS)
                p.wait("vector", sS, tS2)
                p.op("vector", "scalar_tensor_tensor", tscC2[:, 0:512],
                     tscC[:, 0:512], 1.0, ph0[:], op0=ALU.add, op1=ALU.mult)
                p.op("vector", "scalar_tensor_tensor", tscC2[:, 512:1024],
                     tscC[:, 512:1024], 1.0, ph1[:], op0=ALU.add, op1=ALU.mult)
                tKs = p.op("vector", "tensor_scalar_mul", ks[:], tscC2[:], 0.5,
                           sem=sV)
                p.wait("tensor", sV, tQs)
                for kt in range(KT):
                    p.op("tensor", "matmul", pbig[:, 0:512],
                         xs[pr][:, kt * 128:(kt + 1) * 128],
                         wv_w[:, kt * H:kt * H + 512],
                         start=(kt == 0), stop=(kt == KT - 1))
                tVm = None
                for kt in range(KT):
                    tVm = p.op("tensor", "matmul", pbig[:, 512:1024],
                               xs[pr][:, kt * 128:(kt + 1) * 128],
                               wv_w[:, kt * H + 512:(kt + 1) * H],
                               start=(kt == 0), stop=(kt == KT - 1),
                               sem=sT if kt == KT - 1 else None)
                p.wait("vector", sT, tVm)
                tVs = p.op("vector", "tensor_copy", vt[:], pbig[:], sem=sV)

                p.wait("sync", sLD, tC)
                p.wait("tensor", sLD, tC)
                p.wait("tensor", sV, tSS)
                prv = crow[1 - pr]
                tt = p.op("tensor", "matmul", pdsC[:, 0:NH], onescol[:],
                          prv[0:1, 127::128], start=True, stop=True)
                tt = p.op("tensor", "matmul", pdsC[:, NH:2 * NH], onescol[:],
                          crow[pr][0:1, 127::128], start=True, stop=True, sem=sT)
                p.wait("vector", sT, tt)
                p.op("vector", "tensor_sub", peb[:], cumall[:, n * NH:(n + 1) * NH],
                     pdsC[:, 0:NH])
                tv = p.op("vector", "tensor_sub", w16[:], pdsC[:, NH:2 * NH],
                          cumall[:, n * NH:(n + 1) * NH], sem=sV)
                p.wait("scalar", sV, tv)
                p.op("scalar", "activation", a16[:], peb[:], AF.Exp)
                tA = p.op("scalar", "activation", w16[:], w16[:], AF.Exp, sem=sS)
                p.wait("vector", sS, tA)
                p.op("vector", "tensor_sub", grow16[:], crow[pr][0:1, 127::128],
                     prv[0:1, 127::128])
                tv = p.op("vector", "memset", colsC[0:1, 15:16], 0.0, sem=sV)
                p.wait("scalar", sV, tv)
                tGr = p.op("scalar", "activation", grow16[:], grow16[:], AF.Exp,
                           sem=sS)
                p.wait("vector", sS, tA)
                tKw = p.op("vector", "tensor_tensor", kw[:], ks[:],
                           w16[:].rearrange("p (h o) -> p h o", o=1)
                           .broadcast_to((128, NH, DH)), op=ALU.mult, sem=sV)

                p.wait("tensor", sV, max(tokD["tr"], tQs))
                tt = None
                for i in range(8):
                    tt = p.op("tensor", "transpose", ptrC[:, i * 128:(i + 1) * 128],
                              qs[:, i * 128:(i + 1) * 128], identb[:],
                              sem=sT if i == 7 else None)
                p.wait("vector", sT, tt)
                tTq = p.op("vector", "tensor_copy", qsT[:], ptrC[:], sem=sV)
                p.wait("tensor", sV, max(tTq, tKs))
                for i in range(8):
                    tt = p.op("tensor", "transpose", ptrC[:, i * 128:(i + 1) * 128],
                              ks[:, i * 128:(i + 1) * 128], identb[:],
                              sem=sT if i == 7 else None)
                p.wait("vector", sT, tt)
                tTk = p.op("vector", "tensor_copy", kT[:], ptrC[:], sem=sV)
                tokD["tr"] = tTk

                tOe = [None, None]
                for g in range(2):
                    p.wait("tensor", sV, max(tVs, tokD["qk"]))
                    tt = p.op("tensor", "matmul", pbig[:, 0:512], onescol[:],
                              crow[pr][0:1, g * 1024:g * 1024 + 512],
                              start=True, stop=True)
                    tt = p.op("tensor", "matmul", pbig[:, 512:1024], onescol[:],
                              crow[pr][0:1, g * 1024 + 512:(g + 1) * 1024],
                              start=True, stop=True, sem=sT)
                    p.wait("vector", sT, tt)
                    p.wait("vector", sS, tokD["tE"])
                    h0 = g * 8
                    p.op("vector", "tensor_sub", expin[:], pbig[:],
                         cumall[:, n * NH + h0:n * NH + h0 + 8]
                         .rearrange("p (h o) -> p h o", o=1).broadcast_to((128, 8, 128)))
                    tv = p.op("vector", "tensor_scalar_min", expin[:], expin[:],
                              0.0, sem=sV)
                    p.wait("scalar", sV, tv)
                    p.wait("scalar", sG, tokD["tE2"])
                    tE = p.op("scalar", "activation", etile[:], expin[:], AF.Exp,
                              sem=sS)
                    tokD["tE"] = tE
                    p.wait("gpsimd", sS, tE)
                    p.wait("gpsimd", sV, tokD["qk"])
                    tE2 = p.op("gpsimd", "tensor_tensor", e2til[:], etile[:],
                               trilx8[:], op=ALU.mult, sem=sG)
                    tokD["tE2"] = tE2
                    p.wait("tensor", sV, tv)
                    tt = None
                    for hh in range(8):
                        h = h0 + hh
                        base = (h % 2) * 64
                        c0 = (hh % 2) * 512 + (hh // 2) * 128
                        tt = p.op("tensor", "matmul",
                                  pbig[:, c0:c0 + 128],
                                  kT[base:base + 64,
                                     (h // 2) * 128:(h // 2) * 128 + 128],
                                  qsT[base:base + 64,
                                      (h // 2) * 128:(h // 2) * 128 + 128],
                                  start=True, stop=True,
                                  sem=sT if hh == 7 else None)
                    p.wait("vector", sT, tt)
                    p.wait("vector", sG, tE2)
                    tSc = p.op("vector", "tensor_mul",
                               sct[:].rearrange("p (h2 par i) -> p h2 par i",
                                                par=2, i=128),
                               e2til[:].rearrange("p (h2 par i) -> p h2 par i",
                                                  par=2, i=128),
                               pbig[:].rearrange("p (par h2 i) -> p h2 par i",
                                                 par=2, i=128),
                               sem=sV)
                    p.wait("tensor", sV, tSc)
                    if g == 0:
                        p.wait("tensor", sV, tokD["oin"])
                    for hh in range(8):
                        h = h0 + hh
                        phb = ph0 if h % 2 == 0 else ph1
                        c0 = (h // 2) * 64
                        p.op("tensor", "matmul", phb[:, c0:c0 + 64],
                             sct[:, hh * 128:(hh + 1) * 128],
                             vt[:, h * 64:(h + 1) * 64],
                             start=True, stop=True)
                    for hh in range(8):
                        h = h0 + hh
                        pib = pu if h % 2 == 0 else ptiny
                        c0 = (h // 2) * 64
                        base = (h % 2) * 64
                        tOe[g] = p.op("tensor", "matmul", pib[:, c0:c0 + 64],
                                      qsT[base:base + 64,
                                          (h // 2) * 128:(h // 2) * 128 + 128],
                                      sstb[base:base + 64,
                                           (h // 2) * 64:(h // 2) * 64 + 64],
                                      start=True, stop=True,
                                      sem=sT if hh == 7 else None)
                    p.wait("tensor", sV, max(tKw, tokD["sst"]))
                    for hh in range(8):
                        h = h0 + hh
                        base = (h % 2) * 64
                        tDs = p.op("tensor", "matmul",
                                   pdsC[base:base + 64,
                                        (h // 2) * 64:(h // 2) * 64 + 64],
                                   kw[:, h * 64:(h + 1) * 64],
                                   vt[:, h * 64:(h + 1) * 64],
                                   start=True, stop=True,
                                   sem=sT if hh == 7 else None)
                    tokD["qk"] = tSc
                tokD["ds"] = tDs
                p.wait("vector", sT, tOe[1])
                p.wait("vector", sS, tokD["sq"])
                p.wait("vector", sWR, tokD["obuf"][pr])
                p.op("vector", "tensor_mul", tscC[:, 0:512], pu[:],
                     a16[:, 0::2].rearrange("p (h o) -> p h o", o=1)
                     .broadcast_to((128, 8, DH)))
                tv = p.op("vector", "tensor_mul", tscC[:, 512:1024], ptiny[:],
                          a16[:, 1::2].rearrange("p (h o) -> p h o", o=1)
                          .broadcast_to((128, 8, DH)), sem=sV)
                oaV = oasm[:].rearrange("p (h2 x) -> p h2 x", x=128)
                p.op("vector", "tensor_add", oaV[:, :, 0:DH],
                     tscC[:, 0:512].rearrange("p (h2 d) -> p h2 d", d=DH),
                     ph0[:].rearrange("p (h2 d) -> p h2 d", d=DH))
                tOo = p.op("vector", "tensor_add", oaV[:, :, DH:128],
                           tscC[:, 512:1024].rearrange("p (h2 d) -> p h2 d",
                                                       d=DH),
                           ph1[:].rearrange("p (h2 d) -> p h2 d", d=DH),
                           sem=sV)
                tokD["oit"] = tOo
                tokD["oin"] = tOo
                tOb = p.op("vector", "tensor_copy", obuf[pr][:], oasm[:], sem=sV)
                p.wait("gpsimd", sV, tOb)
                tokD["obuf"][pr] = p.op("gpsimd", "dma_start", sem=sWR, step=16,
                                        out=o_hbm[n], in_=obuf[pr][:])
                if dbg:
                    tokD["obuf"][pr] = p.op("gpsimd", "dma_start", sem=sWR, step=16,
                                            out=dbg_t["d_o"][n * 128:(n + 1) * 128, :],
                                            in_=obuf[pr][:])
                if dbg and n == 0:
                    for src_t, off, w, bc in ((qs, 0, 512, True), (ks, 512, 512, True),
                                              (vt, 1024, 512, True),
                                              (sct, 1536, 512, True),
                                              (a16, 2048, 16, False),
                                              (w16, 2064, 16, False),
                                              (peb, 2080, 16, False),
                                              (etile, 2560, 1024, False)):
                        p.wait("gpsimd", sV, sV.count)
                        if bc:
                            p.op("gpsimd", "dma_start", sem=sWR, step=16,
                                 out=dbg_t["d_ffn0"][:, off:off + w].bitcast(BF16),
                                 in_=src_t[:])
                        else:
                            p.op("gpsimd", "dma_start", sem=sWR, step=16,
                                 out=dbg_t["d_ffn0"][:, off:off + w],
                                 in_=src_t[:, 0:w])
                    for e in ("vector", "scalar", "tensor", "sync"):
                        p.wait(e, sWR, sWR.count)
                p.op("vector", "tensor_reduce", colsC[:, 0:NH],
                     oasm[:].rearrange("p (h d) -> p h d", d=DH), axis=AX.X,
                     op=ALU.add)
                p.opw("vector", sV, "tensor_add", gsum[:], gsum[:], colsC[:, 0:NH])
                p.wait("scalar", sV, tOo)
                tSq = p.op("scalar", "activation", sqC[:], oasm[:], AF.Square,
                           sem=sS)
                tokD["sq"] = tSq
                p.wait("vector", sS, tSq)
                p.op("vector", "tensor_reduce", colsC[:, 0:NH],
                     sqC[:].rearrange("p (h d) -> p h d", d=DH), axis=AX.X,
                     op=ALU.add)
                tv = p.opw("vector", sV, "tensor_add", gss[:], gss[:],
                           colsC[:, 0:NH])
                p.wait("tensor", sV, max(tv, tokD["qk"]))
                p.wait("tensor", sS, tGr)
                tt = p.op("tensor", "matmul", pbig[0:64, 0:512], onescol[0:1, 0:64],
                          grow16[0:1, 0::2].rearrange("q (h o) -> q h o", o=1)
                          .broadcast_to((1, 8, 64)), start=True, stop=True)
                tt = p.op("tensor", "matmul", pbig[64:128, 0:512],
                          onescol[0:1, 0:64],
                          grow16[0:1, 1::2].rearrange("q (h o) -> q h o", o=1)
                          .broadcast_to((1, 8, 64)), start=True, stop=True, sem=sT)
                p.wait("vector", sT, max(tt, tDs))
                p.op("vector", "tensor_mul", tscC[:, 0:512], sst[:], pbig[:, 0:512])
                p.op("vector", "tensor_add", sst[:], tscC[:, 0:512], pdsC[:])
                tSS = p.op("vector", "tensor_copy", sstb[:], sst[:], sem=sV)
                tokD["sst"] = tSS
                tokD["big"] = tSS
                tokD["crow"][0] = tSS
                tokD["crow"][1] = tSS
                tokD["xs"][pr] = tVm

            p.wait("vector", sV, tSS)
            p.op("vector", "tensor_copy", gstat[:, 0:NH], gsum[:])
            tv = p.op("vector", "tensor_copy", gstat[:, NH:2 * NH], gss[:], sem=sV)
            if dbg:
                p.wait("gpsimd", sV, tv)
                p.op("gpsimd", "dma_start", sem=sWR, step=16,
                     out=dbg_t["d_gst"], in_=gstat[:])
            p.wait("gpsimd", sWR, sWR.count)
            p.wait("vector", sV, sV.count)
            p.replay(blk)

    es_cd.close()

    if "D" in run_phases:
        es_d = contextlib.ExitStack()
        es_d1 = contextlib.ExitStack()
        es_d2 = contextlib.ExitStack()

        def sbd(name, shape, dt=F32):
            return es_d.enter_context(nc.sbuf_tensor(name, list(shape), dt))

        def sbd1(name, shape, dt=F32):
            return es_d1.enter_context(nc.sbuf_tensor(name, list(shape), dt))

        SCLb = sbd("SCLb", (128, H))
        BIAb = sbd("BIAb", (128, H))
        colD = sbd("colD", (1, 160))
        dqcol = sbd("dqcol", (128, 3 * NOWN))
        xa = sbd("xaD", (128, H))
        xsel = sbd("xselD", (128, H))
        xnb = sbd("xnbD", (128, H), BF16)
        tscD = sbd("tscD", (128, H))
        tscD2 = sbd("tscD2", (128, H))
        xnT = sbd("xnTD", (128, H), BF16)
        gub = sbd("gubD", (128, H), BF16)
        gub2 = sbd("gubD2", (128, H), BF16)
        gu_l = [sbd(f"gu_l{i}", (128, INNER), BF16) for i in range(2)]
        guqT = sbd("guqT", (128, INNER), BF16)
        g1t = sbd("g1tD", (128, H))
        ogb = sbd("ogbD", (128, H), BF16)
        ogT = sbd("ogTD", (128, H), BF16)
        outt = [sbd(f"outtD{i}", (128, H)) for i in range(2)]
        rowD = sbd1("rowD", (1, 2 * H))
        gnr = sbd1("gnrD", (1, 2 * H))
        qb = sbd1("qbD", (128, H), BF16)
        qbT = sbd1("qbTD", (128, H), BF16)
        wg_sl_ = [sbd1(f"wg_sl{i}", (128, KT * H), BF16) for i in range(2)]
        wu_sl_ = [sbd1(f"wu_sl{i}", (128, KT * H), BF16) for i in range(2)]
        xq_l = [sbd1(f"xq_l{i}", (128, H), BF16) for i in range(2)]

        DQG, DQU, AMG = 0, NOWN, 2 * NOWN

        with nc.Block() as blk:
            p = Prog(esems={"vector": sV, "scalar": sS, "gpsimd": sG})
            tZM = p.op("sync", "dma_start", sem=sLD, step=16, out=zmc[:],
                       in_=zmcols)
            p.op("sync", "dma_start", sem=sLD, step=16, out=gnr[:],
                 in_=gn_row)
            p.wait("vector", sLD, sLD.count)
            p.wait("tensor", sV, sV.count)
            tt = p.op("tensor", "matmul", pdsC[0:1, 0:2 * NH], ones128[:],
                      gstat[:], start=True, stop=True, sem=sT)
            p.wait("vector", sT, tt)
            CNT = 1.0 / (float(L) * DH)
            p.opw("vector", sV, "tensor_scalar_mul", colD[0:1, 0:16],
                  pdsC[0:1, 0:NH], CNT)
            p.opw("vector", sV, "tensor_scalar_mul", colD[0:1, 32:48],
                  pdsC[0:1, NH:2 * NH], CNT)
            p.opw("vector", sV, "tensor_mul", colD[0:1, 48:64], colD[0:1, 0:16],
                  colD[0:1, 0:16])
            p.opw("vector", sV, "tensor_sub", colD[0:1, 32:48], colD[0:1, 32:48],
                  colD[0:1, 48:64])
            tv = p.opw("vector", sV, "tensor_scalar_add", colD[0:1, 32:48],
                       colD[0:1, 32:48], 1e-5)
            p.opw("vector", sV, "tensor_single_scalar",
                  colD[0:1, 64:80].bitcast(I32), colD[0:1, 32:48].bitcast(I32), 1,
                  op=ALU.logical_shift_right)
            p.wait("gpsimd", sV, sV.count)
            tg = p.op("gpsimd", "tensor_tensor", colD[0:1, 80:96].bitcast(I32),
                      cmagic[0:1, 0:1].broadcast_to((1, NH)).bitcast(I32),
                      colD[0:1, 64:80].bitcast(I32), op=ALU.subtract, sem=sG)
            p.wait("vector", sG, tg)
            p.opw("vector", sV, "tensor_scalar_mul", colD[0:1, 96:112],
                  colD[0:1, 32:48], -0.5)
            for _ in range(3):
                p.opw("vector", sV, "tensor_mul", colD[0:1, 112:128],
                      colD[0:1, 80:96], colD[0:1, 80:96])
                p.opw("vector", sV, "tensor_mul", colD[0:1, 112:128],
                      colD[0:1, 112:128], colD[0:1, 96:112])
                p.opw("vector", sV, "tensor_scalar_add", colD[0:1, 112:128],
                      colD[0:1, 112:128], 1.5)
                p.opw("vector", sV, "tensor_mul", colD[0:1, 80:96],
                      colD[0:1, 80:96], colD[0:1, 112:128])
            p.opw("vector", sV, "tensor_mul",
                  rowD[0:1, 0:H].rearrange("q (h d) -> q h d", d=DH),
                  gnr[0:1, 0:H].rearrange("q (h d) -> q h d", d=DH),
                  colD[0:1, 80:96].rearrange("q (h o) -> q h o", o=1)
                  .broadcast_to((1, NH, DH)))
            p.opw("vector", sV, "tensor_mul", colD[0:1, 128:144], colD[0:1, 0:16],
                  colD[0:1, 80:96])
            p.opw("vector", sV, "tensor_scalar_mul", colD[0:1, 128:144],
                  colD[0:1, 128:144], -1.0)
            p.opw("vector", sV, "tensor_mul",
                  rowD[0:1, H:2 * H].rearrange("q (h d) -> q h d", d=DH),
                  gnr[0:1, 0:H].rearrange("q (h d) -> q h d", d=DH),
                  colD[0:1, 128:144].rearrange("q (h o) -> q h o", o=1)
                  .broadcast_to((1, NH, DH)))
            tv = p.opw("vector", sV, "tensor_add", rowD[0:1, H:2 * H],
                       rowD[0:1, H:2 * H], gnr[0:1, H:2 * H])
            p.wait("tensor", sV, tv)
            for hh in range(2):
                tt = p.op("tensor", "matmul", pbig[:, 0:512], onescol[:],
                          rowD[0:1, hh * 512:(hh + 1) * 512], start=True,
                          stop=True)
                tt = p.op("tensor", "matmul", pbig[:, 512:1024], onescol[:],
                          rowD[0:1, H + hh * 512:H + (hh + 1) * 512],
                          start=True, stop=True, sem=sT)
                p.wait("vector", sT, tt)
                p.op("vector", "tensor_copy", SCLb[:, hh * 512:(hh + 1) * 512],
                     pbig[:, 0:512])
                tv = p.op("vector", "tensor_copy",
                          BIAb[:, hh * 512:(hh + 1) * 512], pbig[:, 512:1024],
                          sem=sV)
                p.wait("tensor", sV, tv)
            p.op("vector", "memset", dqcol[:], 0.0)
            p.wait("vector", sLD, tZM)

            tokTRd = 0
            tokXselW = 0
            tokXaRd = 0
            tokQT = {0: 0, 1: 0}
            for j in range(NOWN):
                p.wait("sync", sV, tokXaRd)
                t = p.op("sync", "dma_start", sem=sLD, step=16, out=xa[:],
                         in_=x1_hbm[j])
                p.wait("vector", sLD, t)
                p.wait("vector", sWR, tokXselW)
                tok1 = p.op("vector", "tensor_single_scalar", xsel[:], xa[:],
                            zmc[:, 0:1], op=ALU.mult, sem=sV)
                p.wait("sync", sV, tok1)
                t = p.op("sync", "dma_start", sem=sLD, step=16, out=xa[:],
                         in_=x1_hbm[j + NOWN])
                p.wait("vector", sLD, t)
                tXs = p.op("vector", "scalar_tensor_tensor", xsel[:], xa[:],
                           zmc[:, 1:2], xsel[:], op0=ALU.mult, op1=ALU.add,
                           sem=sV)
                tokXaRd = tXs
                p.wait("gpsimd", sV, tXs)
                tokXselW = p.op("gpsimd", "dma_start", sem=sWR, step=16,
                                out=x1p_hbm[j], in_=xsel[:])
                p.opw("vector", sV, "reduce_sum", cols[:, 0:1], xsel[:],
                      axis=AX.X)
                p.wait("scalar", sV, tXs)
                tSq = p.op("scalar", "activation", tscD[:], xsel[:], AF.Square,
                           accum_out=cols[:, 1:2], sem=sS)
                p.wait("vector", sS, tSq)
                emit_ln_cols(p, sV, cols, 0, 1, 2, 3, H, 1e-5, sG=sG,
                             cmagic=cmagic)
                tXn = p.op("vector", "scalar_tensor_tensor", xnb[:], xsel[:],
                           cols[:, 2:3], cols[:, 3:4].broadcast_to((128, H)),
                           op0=ALU.mult, op1=ALU.add, sem=sV)
                p.opw("vector", sV, "reduce_max", cols[:, 8:9], xnb[:],
                      axis=AX.X, apply_absolute_value=True)
                p.opw("vector", sV, "tensor_scalar_max", cols[:, 8:9],
                      cols[:, 8:9], 1e-5)
                p.opw("vector", sV, "reciprocal", cols[:, 9:10], cols[:, 8:9])
                p.opw("vector", sV, "tensor_scalar_mul", cols[:, 10:11],
                      cols[:, 9:10], 127.0)
                p.opw("vector", sV, "tensor_single_scalar",
                      dqcol[:, DQG + j:DQG + j + 1], cols[:, 8:9],
                      scolD[:, 0:1], op=ALU.mult)
                p.opw("vector", sV, "tensor_scalar_mul",
                      dqcol[:, DQG + j:DQG + j + 1],
                      dqcol[:, DQG + j:DQG + j + 1], 1.0 / 254.0)
                p.opw("vector", sV, "tensor_single_scalar",
                      dqcol[:, DQU + j:DQU + j + 1], cols[:, 8:9],
                      scolD[:, 2:3], op=ALU.mult)
                p.opw("vector", sV, "tensor_scalar_mul",
                      dqcol[:, DQU + j:DQU + j + 1],
                      dqcol[:, DQU + j:DQU + j + 1], 1.0 / 127.0)
                p.op("vector", "scalar_tensor_tensor", tscD2[:], xnb[:],
                     cols[:, 10:11], cC[:, 0:1].broadcast_to((128, H)),
                     op0=ALU.mult, op1=ALU.add)
                p.op("vector", "tensor_scalar", tscD2[:], tscD2[:], RND_C,
                     127.0, op0=ALU.subtract, op1=ALU.min)
                tQ = p.op("vector", "tensor_scalar_max", qb[:], tscD2[:],
                          -128.0, sem=sV)
                for (srcb, dstb, tok) in ((xnb, xnT, tXn), (qb, qbT, tQ)):
                    p.wait("tensor", sS, tokTRd)
                    p.wait("tensor", sV, tok)
                    lt = None
                    for i in range(8):
                        lt = p.op("tensor", "transpose",
                                  ptrC[:, i * 128:(i + 1) * 128],
                                  srcb[:, i * 128:(i + 1) * 128], identb[:],
                                  sem=sT if i == 7 else None)
                    p.wait("scalar", sT, lt)
                    tokTRd = p.op("scalar", "copy", dstb[:], ptrC[:], sem=sS)
                p.wait("gpsimd", sS, tokTRd)
                tokQT[0] = p.op(
                    "gpsimd", "dma_start", sem=sWR, step=16,
                    out=xn1tp_hbm[j].rearrange("(k q) t -> q k t", q=128),
                    in_=xnT[:])
                tokQT[0] = p.op(
                    "gpsimd", "dma_start", sem=sWR, step=16,
                    out=xqt_hbm[j].rearrange("(k q) t -> q k t", q=128),
                    in_=qbT[:])
                p.wait("scalar", sWR, tokQT[0])

            p.wait("sync", sWR, sWR.count)
            tokGuW = {0: 0, 1: 0}
            tokGuRd = 0
            tokXq = {0: 0, 1: 0}
            tokMM = 0
            tokSl = {0: 0, 1: 0}
            tWSl = {0: 0, 1: 0}
            for s in range(4):
                sb_ = s % 2
                wg_sl = wg_sl_[sb_]
                wu_sl = wu_sl_[sb_]
                tWS = 0
                p.wait("sync", sT, tokSl[sb_])
                for kt in range(KT):
                    tWS = p.op("sync", "dma_start", sem=sLD, step=16,
                               out=wg_sl[:, kt * H:(kt + 1) * H],
                               in_=wgq_hbm[kt][:, s * H:(s + 1) * H])
                    tWS = p.op("sync", "dma_start", sem=sLD, step=16,
                               out=wu_sl[:, kt * H:(kt + 1) * H],
                               in_=wuq_hbm[kt][:, s * H:(s + 1) * H])
                tWSl[sb_] = tWS
                for j in range(NOWN):
                    pr = j % 2
                    p.wait("sync", sT, tokXq[pr])
                    tX = p.op("sync", "dma_start", sem=sLD, step=16,
                              out=xq_l[pr][:],
                              in_=xqt_hbm[j].rearrange("(k q) t -> q k t",
                                                       q=128))
                    p.wait("tensor", sLD, max(tX, tWSl[sb_]))
                    p.wait("tensor", sV, tokGuRd)
                    for kt in range(KT):
                        p.op("tensor", "matmul", pbig[:, 0:512],
                             xq_l[pr][:, kt * 128:(kt + 1) * 128],
                             wg_sl[:, kt * H:kt * H + 512],
                             start=(kt == 0), stop=(kt == KT - 1))
                    for kt in range(KT):
                        p.op("tensor", "matmul", pbig[:, 512:1024],
                             xq_l[pr][:, kt * 128:(kt + 1) * 128],
                             wg_sl[:, kt * H + 512:(kt + 1) * H],
                             start=(kt == 0), stop=(kt == KT - 1))
                    for kt in range(KT):
                        p.op("tensor", "matmul", pu[:],
                             xq_l[pr][:, kt * 128:(kt + 1) * 128],
                             wu_sl[:, kt * H:kt * H + 512],
                             start=(kt == 0), stop=(kt == KT - 1))
                    for kt in range(KT):
                        tokMM = p.op("tensor", "matmul", ptiny[:],
                                     xq_l[pr][:, kt * 128:(kt + 1) * 128],
                                     wu_sl[:, kt * H + 512:(kt + 1) * H],
                                     start=(kt == 0), stop=(kt == KT - 1),
                                     sem=sT if kt == KT - 1 else None)
                    tokXq[pr] = tokMM
                    tokSl[sb_] = tokMM
                    gubj = gub if pr == 0 else gub2
                    p.wait("scalar", sT, tokMM)
                    p.wait("vector", sWR, tokGuW[pr])
                    tS1 = p.op("scalar", "activation", tscD[:], pbig[:],
                               AF.Tanh, scale=dqcol[:, DQG + j:DQG + j + 1],
                               sem=sS)
                    p.wait("vector", sS, tS1)
                    p.op("vector", "tensor_scalar_add", tscD[:], tscD[:], 1.0)
                    p.op("vector", "scalar_tensor_tensor", tscD2[:], pbig[:],
                         dqcol[:, DQG + j:DQG + j + 1], tscD[:], op0=ALU.mult,
                         op1=ALU.mult)
                    p.op("vector", "scalar_tensor_tensor",
                         gubj[:, 0:512], pu[:], dqcol[:, DQU + j:DQU + j + 1],
                         tscD2[:, 0:512], op0=ALU.mult, op1=ALU.mult)
                    tGu = p.op("vector", "scalar_tensor_tensor",
                               gubj[:, 512:1024], ptiny[:],
                               dqcol[:, DQU + j:DQU + j + 1],
                               tscD2[:, 512:1024], op0=ALU.mult, op1=ALU.mult,
                               sem=sV)
                    tokGuRd = tGu
                    p.opw("vector", sV, "reduce_max", cols[:, 12:13], gubj[:],
                          axis=AX.X, apply_absolute_value=True)
                    p.opw("vector", sV, "tensor_max",
                          dqcol[:, AMG + j:AMG + j + 1],
                          dqcol[:, AMG + j:AMG + j + 1], cols[:, 12:13])
                    p.wait("gpsimd", sV, tGu)
                    tokGuW[pr] = p.op("gpsimd", "dma_start", sem=sWR, step=16,
                                      out=gu_hbm[j][:, s * H:(s + 1) * H],
                                      in_=gubj[:])

            for j in range(NOWN):
                p.opw("vector", sV, "tensor_scalar_max",
                      dqcol[:, AMG + j:AMG + j + 1],
                      dqcol[:, AMG + j:AMG + j + 1], 1e-5)
                p.opw("vector", sV, "tensor_single_scalar",
                      dqcol[:, AMG + j:AMG + j + 1],
                      dqcol[:, AMG + j:AMG + j + 1], scolD[:, 4:5],
                      op=ALU.mult)
                p.opw("vector", sV, "tensor_scalar_mul",
                      dqcol[:, AMG + j:AMG + j + 1],
                      dqcol[:, AMG + j:AMG + j + 1], 1.0 / 254.0)

            es_d1.close()
            wd_w = es_d2.enter_context(
                nc.sbuf_tensor("wd_w", [128, 32 * H], BF16))
            wg_w = es_d2.enter_context(
                nc.sbuf_tensor("wg_wD", [128, KT * H], BF16))
            wo_w = es_d2.enter_context(
                nc.sbuf_tensor("wo_wD", [128, KT * H], BF16))
            p.wait("sync", sV, sV.count)
            p.wait("sync", sWR, sWR.count)
            tWD = 0
            for kt in range(32):
                tWD = p.op("sync", "dma_start", sem=sLD, step=16,
                           out=wd_w[:, kt * H:(kt + 1) * H], in_=wdq_hbm[kt])
            for kt in range(KT):
                t = p.op("sync", "dma_start", sem=sLD, step=16,
                         out=tscD[:], in_=Wg_in[kt * 128:(kt + 1) * 128, :])
                p.wait("vector", sLD, t)
                tv = p.op("vector", "tensor_copy", wg_w[:, kt * H:(kt + 1) * H],
                          tscD[:], sem=sV)
                p.wait("sync", sV, tv)
                t = p.op("sync", "dma_start", sem=sLD, step=16,
                         out=tscD[:], in_=Wo_in[kt * 128:(kt + 1) * 128, :])
                p.wait("vector", sLD, t)
                tv = p.op("vector", "tensor_copy", wo_w[:, kt * H:(kt + 1) * H],
                          tscD[:], sem=sV)
                p.wait("sync", sV, tv)

            tokOut = {0: 0, 1: 0}
            tokGuL = {0: 0, 1: 0}
            tokTRd = 0
            tokQrd = 0
            tokG1 = 0
            tokOg = 0
            tokGp = 0
            tokSg = 0
            tokOselRd = 0
            p.wait("tensor", sLD, sLD.count)
            p.wait("tensor", sS, sS.count)
            p.wait("vector", sWR, sWR.count)
            for j in range(NOWN):
                pr = j % 2
                p.wait("sync", sV, tokGuL[pr])
                tGu = p.op("sync", "dma_start", sem=sLD, step=16,
                           out=gu_l[pr][:], in_=gu_hbm[j])
                p.opw("vector", sV, "tensor_single_scalar", cols[:, 15:16],
                      dqcol[:, AMG + j:AMG + j + 1], scolD[:, 5:6],
                      op=ALU.mult)
                p.opw("vector", sV, "reciprocal", cols[:, 15:16],
                      cols[:, 15:16])
                p.opw("vector", sV, "tensor_scalar_mul", cols[:, 15:16],
                      cols[:, 15:16], 0.5)
                p.wait("vector", sLD, tGu)
                for q4 in range(4):
                    sl = slice(q4 * H, (q4 + 1) * H)
                    p.wait("vector", sT, tokQrd if q4 == 0 else 0)
                    p.op("vector", "scalar_tensor_tensor", tscD2[:],
                         gu_l[pr][:, sl], cols[:, 15:16],
                         cC[:, 0:1].broadcast_to((128, H)), op0=ALU.mult,
                         op1=ALU.add)
                    p.op("vector", "tensor_scalar", tscD2[:], tscD2[:], RND_C,
                         127.0, op0=ALU.subtract, op1=ALU.min)
                    tQ = p.op("vector", "tensor_scalar_max", gub[:], tscD2[:],
                              -128.0, sem=sV)
                    p.wait("tensor", sV, max(tokTRd, tQ))
                    lt = None
                    for i in range(8):
                        lt = p.op("tensor", "transpose",
                                  ptrC[:, i * 128:(i + 1) * 128],
                                  gub[:, i * 128:(i + 1) * 128], identb[:],
                                  sem=sT if i == 7 else None)
                    p.wait("vector", sT, lt)
                    tokTRd = p.op("vector", "tensor_copy", guqT[:, sl],
                                  ptrC[:], sem=sV)
                tokGuL[pr] = tokTRd
                p.wait("tensor", sV, max(tokTRd, tokG1))
                for kt in range(32):
                    p.op("tensor", "matmul", pbig[:, 0:512],
                         guqT[:, kt * 128:(kt + 1) * 128],
                         wd_w[:, kt * H:kt * H + 512],
                         start=(kt == 0), stop=(kt == 31))
                tDn = None
                for kt in range(32):
                    tDn = p.op("tensor", "matmul", pbig[:, 512:1024],
                               guqT[:, kt * 128:(kt + 1) * 128],
                               wd_w[:, kt * H + 512:(kt + 1) * H],
                               start=(kt == 0), stop=(kt == 31),
                               sem=sT if kt == 31 else None)
                tokQrd = tDn
                p.wait("scalar", sT, tDn)
                tS1 = p.op("scalar", "activation", tscD[:], pbig[:], AF.Tanh,
                           scale=dqcol[:, AMG + j:AMG + j + 1], sem=sS)
                p.wait("vector", sS, tS1)
                p.op("vector", "tensor_scalar_add", tscD[:], tscD[:], 1.0)
                tokG1 = p.op("vector", "scalar_tensor_tensor", g1t[:], pbig[:],
                             dqcol[:, AMG + j:AMG + j + 1], tscD[:],
                             op0=ALU.mult, op1=ALU.mult, sem=sV)
                p.wait("sync", sV, tokOselRd)
                t = p.op("sync", "dma_start", sem=sLD, step=16, out=xnb[:],
                         in_=o_hbm[j])
                p.wait("vector", sLD, t)
                tok1 = p.op("vector", "tensor_single_scalar", xsel[:], xnb[:],
                            zmc[:, 0:1], op=ALU.mult, sem=sV)
                p.wait("sync", sV, tok1)
                t = p.op("sync", "dma_start", sem=sLD, step=16, out=xnb[:],
                         in_=o_hbm[j + NOWN])
                p.wait("vector", sLD, t)
                tokOselRd = p.op("vector", "scalar_tensor_tensor", xsel[:],
                                 xnb[:], zmc[:, 1:2], xsel[:], op0=ALU.mult,
                                 op1=ALU.add, sem=sV)
                p.op("vector", "tensor_mul", xsel[:], xsel[:], SCLb[:])
                tOn = p.op("vector", "tensor_add", xsel[:], xsel[:], BIAb[:],
                           sem=sV)
                p.wait("sync", sT, tokGp)
                tX = p.op("sync", "dma_start", sem=sLD, step=16, out=xnT[:],
                          in_=xn1tp_hbm[j].rearrange("(k q) t -> q k t",
                                                     q=128))
                p.wait("tensor", sLD, tX)
                p.wait("tensor", sS, tokSg)
                for kt in range(KT):
                    p.op("tensor", "matmul", pu[:],
                         xnT[:, kt * 128:(kt + 1) * 128],
                         wg_w[:, kt * H:kt * H + 512],
                         start=(kt == 0), stop=(kt == KT - 1))
                tGp = None
                for kt in range(KT):
                    tGp = p.op("tensor", "matmul", ptiny[:],
                               xnT[:, kt * 128:(kt + 1) * 128],
                               wg_w[:, kt * H + 512:(kt + 1) * H],
                               start=(kt == 0), stop=(kt == KT - 1),
                               sem=sT if kt == KT - 1 else None)
                tokGp = tGp
                p.wait("scalar", sT, tGp)
                tS2 = p.op("scalar", "activation", tscD[:, 0:512], pu[:],
                           AF.Tanh, scale=0.5, sem=sS)
                tS2 = p.op("scalar", "activation", tscD[:, 512:1024],
                           ptiny[:], AF.Tanh, scale=0.5, sem=sS)
                tokSg = tS2
                p.wait("vector", sS, tS2)
                p.op("vector", "tensor_scalar", tscD[:], tscD[:], 0.5, 0.5,
                     op0=ALU.mult, op1=ALU.add)
                tOgb = p.op("vector", "tensor_mul", ogb[:], xsel[:], tscD[:],
                            sem=sV)
                p.wait("tensor", sV, max(tokTRd, tOgb))
                lt = None
                for i in range(8):
                    lt = p.op("tensor", "transpose",
                              ptrC[:, i * 128:(i + 1) * 128],
                              ogb[:, i * 128:(i + 1) * 128], identb[:],
                              sem=sT if i == 7 else None)
                p.wait("vector", sT, lt)
                tokTRd = p.op("vector", "tensor_copy", ogT[:], ptrC[:], sem=sV)
                tokOg = tokTRd
                p.wait("tensor", sV, max(tokTRd, tokG1))
                for kt in range(KT):
                    p.op("tensor", "matmul", pbig[:, 0:512],
                         ogT[:, kt * 128:(kt + 1) * 128],
                         wo_w[:, kt * H:kt * H + 512],
                         start=(kt == 0), stop=(kt == KT - 1))
                tWo = None
                for kt in range(KT):
                    tWo = p.op("tensor", "matmul", pbig[:, 512:1024],
                               ogT[:, kt * 128:(kt + 1) * 128],
                               wo_w[:, kt * H + 512:(kt + 1) * H],
                               start=(kt == 0), stop=(kt == KT - 1),
                               sem=sT if kt == KT - 1 else None)
                tokQrd = tWo
                p.wait("sync", sV, tokXaRd)
                tX1 = p.op("sync", "dma_start", sem=sLD, step=16, out=xa[:],
                           in_=x1p_hbm[j])
                p.wait("vector", sT, tWo)
                p.wait("vector", sLD, tX1)
                p.wait("vector", sWR, tokOut[pr])
                p.op("vector", "tensor_mul", outt[pr][:], pbig[:], g1t[:])
                tOu = p.op("vector", "tensor_add", outt[pr][:], outt[pr][:],
                           xa[:], sem=sV)
                tokXaRd = tOu
                p.wait("gpsimd", sV, tOu)
                tokOut[pr] = p.op("gpsimd", "dma_start", sem=sWR, step=16,
                                  out=out[j * 128:(j + 1) * 128, :],
                                  in_=outt[pr][:])
            p.wait("gpsimd", sWR, sWR.count)
            p.wait("vector", sV, sV.count)
            p.wait("sync", sWR, sWR.count)
            p.replay(blk)
        es_d2.close()
        es_d.close()

    es.close()
    return nc, dbg_t, inp



def make_in_map(inputs, b, z=0, L=None):
    if L is None:
        L = L_TOKENS
    f32 = np.float32
    x = np.ascontiguousarray(inputs["L0_attn_input_sequence"][b, :L]).astype(f32, copy=False)
    sp = np.stack([inputs["L0_attn_steps_raw"].reshape(2, 128).T,
                   inputs["L0_attn_G_raw"].reshape(2, 128).T,
                   inputs["L0_attn_A_raw"].reshape(2, 128).T], axis=1)
    ssm_par = np.ascontiguousarray(sp.reshape(128, 6).astype(f32))
    Bp = np.ascontiguousarray(
        inputs["L0_attn_B_mat"].transpose(1, 0, 2).reshape(H, 2 * P).astype(f32))
    Cp = np.ascontiguousarray(
        inputs["L0_attn_C_mat"].transpose(1, 2, 0).reshape(2 * P, H).astype(f32))
    Db = np.ascontiguousarray(
        np.broadcast_to(inputs["L0_attn_D_vec"][None, :], (128, H)).astype(f32))
    rhs = np.ascontiguousarray(inputs["L0_ffn_rhs"].astype(f32))
    rhsT = np.ascontiguousarray(rhs.T)
    gn_row = np.ascontiguousarray(
        np.concatenate([inputs["L1_attn_gn_w"], inputs["L1_attn_gn_b"]])[None, :]
        .astype(f32))
    bab = np.ascontiguousarray(
        np.broadcast_to(inputs["L1_attn_ba"][None, :], (128, NH)).astype(f32))
    triu_np = np.triu(np.ones((128, 128), np.float32))
    tril_np = np.tril(np.ones((128, 128), np.float32))
    ident_np = np.eye(128, dtype=np.float32)
    pm1_np = np.ones((128, 128), np.float32)
    pm1_np[1::2, :] = -1.0
    triuS_np = np.triu(np.ones((128, 128), np.float32), 1)
    zpad = np.zeros((128, 128), np.float32)
    consts_in = np.ascontiguousarray(np.concatenate(
        [triu_np, tril_np, ident_np, pm1_np, triuS_np, zpad], axis=1))
    return {
        "zmcols": np.ascontiguousarray(
            np.broadcast_to(np.array([1.0 - z, float(z)], np.float32)[None, :],
                            (128, 2))),
        "consts_in": consts_in,
        "x0": x,
        "ssm_par": ssm_par,
        "Bp_in": Bp,
        "Cp_in": Cp,
        "Db": Db,
        "rhs_in": rhs,
        "rhsT_in": rhsT,
        "Wq_in": np.ascontiguousarray(inputs["L1_attn_Wq"].astype(f32)),
        "Wk_in": np.ascontiguousarray(inputs["L1_attn_Wk"].astype(f32)),
        "Wv_in": np.ascontiguousarray(inputs["L1_attn_Wv"].astype(f32)),
        "Wg_in": np.ascontiguousarray(inputs["L1_attn_Wg"].astype(f32)),
        "Wa_in": np.ascontiguousarray(inputs["L1_attn_Wa"].astype(f32)),
        "bab": bab,
        "Wo_in": np.ascontiguousarray(inputs["L1_attn_Wo"].astype(f32)),
        "gn_row": gn_row,
        "wgT_in": np.ascontiguousarray(inputs["L1_ffn_w_gate"].T.astype(f32)),
        "wuT_in": np.ascontiguousarray(inputs["L1_ffn_w_up"].T.astype(f32)),
        "wdT_in": np.ascontiguousarray(inputs["L1_ffn_w_down"].T.astype(f32)),
    }


def kernel(**inputs):
    nc, _, _ = build_program()
    in_maps = []
    for c in range(8):
        b = c // 2
        m = make_in_map(inputs, b, z=c % 2)
        in_maps.append(m)
    res = run_bass_kernel_spmd(nc, in_maps, core_ids=list(range(8)))
    B, L = 4, L_TOKENS
    outp = np.empty((B, L, H), np.float32)
    for c in range(8):
        b, zz = c // 2, c % 2
        outp[b, zz * (L // 2):(zz + 1) * (L // 2)] = res.results[c]["out"]
    return outp
